# version 02
# baseline (speedup 1.0000x reference)
import numpy as np
from contextlib import ExitStack

import concourse.bass as bass
import concourse.mybir as mybir
from concourse.bass_utils import run_bass_kernel_spmd

F32 = mybir.dt.float32
AF = mybir.ActivationFunctionType
ALU = mybir.AluOpType
AX = mybir.AxisListType

D = 1024
H = 16
DH = 64
SOWN = 2048
SCTX = 4096
NDEPTH = 6
DI = 2730
WIN = 256
NEG = -1.0e9
NF = 8
NCC = (DI + 127) // 128
LASTW = DI - 128 * (NCC - 1)
ENGINES = ("sync", "tensor", "vector", "scalar", "gpsimd")


class Sem:
    __slots__ = ("h", "n", "name", "is_q")

    def __init__(self, h, name):
        self.h = h
        self.n = 0
        self.name = name
        self.is_q = False


class Buf:
    __slots__ = ("w", "r", "name")

    def __init__(self, name=""):
        self.w = {}
        self.r = {}
        self.name = name


def _merge(dst, src):
    for s, c in src.items():
        if dst.get(s, -1) < c:
            dst[s] = c


class KB:
    def __init__(self, nc, stack):
        self.nc = nc
        self.stack = stack
        self.ops = {e: [] for e in ENGINES}
        self.waited = {e: {} for e in ENGINES}
        self.sems = []
        self.dsem = {e: self.sem("zd" + e[:2]) for e in ENGINES}
        self.qsem = {e: [self.sem("zq" + e[:2] + str(i)) for i in range(8)]
                     for e in ("sync", "scalar", "gpsimd", "vector")}
        for pool in self.qsem.values():
            for s_ in pool:
                s_.is_q = True
        self.qrr = {e: 0 for e in self.qsem}

    def sem(self, name):
        h = self.stack.enter_context(self.nc.semaphore(name))
        s = Sem(h, name)
        self.sems.append(s)
        return s

    def op(self, eng, fn, reads=(), writes=(), dma=False, last=True, extra=()):
        if dma:
            pool = self.qsem[eng]
            s = pool[self.qrr[eng]]
            self.qrr[eng] = (self.qrr[eng] + 1) % len(pool)
            k = 16
        else:
            s = self.dsem[eng]
            k = 1
        waits = {}
        for b in reads:
            _merge(waits, b.w)
        for b in writes:
            _merge(waits, b.w)
            _merge(waits, b.r)
        for sm, c in extra:
            if waits.get(sm, -1) < c:
                waits[sm] = c
        wd = self.waited[eng]
        for sm in list(waits):
            if sm.is_q:
                waits[sm] = sm.n
        ws = [(sm, c) for sm, c in waits.items() if wd.get(sm, 0) < c]
        for sm, c in ws:
            wd[sm] = c
        if last:
            s.n += k
            cover = s.n
        else:
            cover = s.n + k
        self.ops[eng].append((ws, fn, s if last else None, k))
        for b in writes:
            if last:
                b.w = {s: cover}
                b.r = {}
        for b in reads:
            if b.r.get(s, -1) < cover:
                b.r[s] = cover
        return (s, cover)

    def barrier(self):
        bar = self.sem("zb%d" % len(self.sems))
        for e in ENGINES:
            ws = [(self.dsem[e], self.dsem[e].n)] if self.dsem[e].n else []
            for q in self.qsem.get(e, ()):
                if q.n:
                    ws.append((q, q.n))
            self.ops[e].append((ws, (lambda eng: eng.nop()), bar, 1))
            bar.n += 1
        tgt = bar.n
        snap = {s: s.n for s in self.sems if s.n > 0}
        for e in ENGINES:
            self.ops[e].append(([(bar, tgt)], None, None, 0))
            w = dict(snap)
            w[bar] = tgt
            self.waited[e] = w
        return bar

    def finalize(self, block):
        for e in ENGINES:
            deco = getattr(block, e)
            ops = self.ops[e]

            def make(ops=ops):
                def run(engine):
                    for ws, fn, s, k in ops:
                        for sm, c in ws:
                            engine.wait_ge(sm.h, c)
                        if fn is not None:
                            instr = fn(engine)
                            if s is not None:
                                instr.then_inc(s.h, k)
                return run
            deco(make())


def build_nc(debug=0, depth=NDEPTH, do_outer=True, do_inner=True, do_final=True,
             pairs_outer=8):
    nc = bass.Bass("TRN2", target_bir_lowering=False)
    stack = ExitStack()
    kb = KB(nc, stack)
    T = {}
    B = {}

    def dram_in(name, shape):
        T[name] = nc.dram_tensor(name, shape, F32, kind="ExternalInput").ap()
        B[name] = Buf(name)

    def dram_out(name, shape):
        T[name] = nc.dram_tensor(name, shape, F32, kind="ExternalOutput").ap()
        B[name] = Buf(name)

    def dram_tmp(name, shape):
        T[name] = nc.dram_tensor(name, shape, F32, kind="Internal").ap()
        B[name] = Buf(name)

    def sb(name, shape, dtype=F32):
        T[name] = nc.alloc_sbuf_tensor(name, shape, dtype).ap()
        B[name] = Buf(name)

    def psum(name, shape):
        T[name] = nc.alloc_psum_tensor(name, shape, F32).ap()
        B[name] = Buf(name)

    dram_in("x_ctx", [SCTX, D])
    dram_in("Wqkv", [6, D, 3 * D])
    dram_in("Wo", [6, D, D])
    dram_in("W1", [6, D, 2 * DI])
    dram_in("W2", [6, DI, D])
    dram_in("b1", [6, 2 * DI])
    dram_in("b2", [6, D])
    dram_in("lng", [6, D])
    dram_in("lnb", [6, D])
    dram_in("qsc", [6, DH])
    dram_in("ksc", [6, DH])
    dram_in("WK", [H, H])
    dram_in("WQ", [H, H])
    dram_in("WV", [H, H])
    dram_in("Wpred", [H, DH, DH])
    dram_in("tabA", [2304, DH])
    dram_in("tabB", [2304, DH])
    dram_in("tabAi", [2304, DH])
    dram_in("tabBi", [2304, DH])
    dram_in("pad_bias", [128, 1])
    dram_in("win0_bias", [128, 1])
    dram_in("halo_gate", [128, 1])
    dram_out("out_own", [SOWN, D])

    dram_tmp("z_scr", [D, SOWN])
    dram_tmp("probs_hbm", [3, H, H])
    dram_tmp("rows_hbm", [2, 512])
    dram_tmp("cc1_in", [NF, 128, 257])
    dram_tmp("cc1_out", [2, NF, 128, 257])
    dram_tmp("cc2_in", [NF, 128, 1])
    dram_tmp("cc2_out", [2, NF, 128, 1])

    for r in range(NF):
        sb("XN%d" % r, [128, SCTX])
    sb("ident", [128, 128])
    sb("diag2", [128, 128])
    sb("ones64", [64, 64])
    sb("ones65", [65, 64])
    sb("ones_row", [1, 128])
    sb("ones_colf", [128, 1])
    sb("zeros_col", [128, 1])
    sb("probs", [16, 48])
    sb("prob_s", [16, 8])
    sb("prob_r", [16, 8])
    sb("colK", [128, 128])
    sb("colQ", [128, 128])
    sb("colV", [128, 128])
    sb("pad_bias_t", [128, 1])
    sb("epsc", [128, 1])
    sb("win0_bias_t", [128, 1])
    sb("halo_gate_t", [128, 1])
    sb("wpred2", [128, 128])
    sb("stat", [128, 64])
    sb("denr", [65, 512])
    sb("vecs", [128, 72])
    sb("qsks", [128, 256])
    sb("rowsS", [1, 512])
    sb("tmpt", [128, 128])
    sb("tmpt2", [128, 128])
    sb("xpc", [128, 8])
    sb("ARENA", [128, 17616])
    for i in range(8):
        psum("pb%d" % i, [128, 512])

    AR = T["ARENA"]
    seg_tab = {}

    def seg(name, off, cols):
        if name not in seg_tab:
            seg_tab[name] = Buf("sg_" + name)
        return AR[:, off:off + cols], seg_tab[name]

    def op(eng, fn, reads=(), writes=(), **kw):
        return kb.op(eng, fn, reads=reads, writes=writes, **kw)

    def bb(*names):
        return [B[n] for n in names]

    XN = [T["XN%d" % r] for r in range(NF)]
    XNB = [B["XN%d" % r] for r in range(NF)]
    PB = [T["pb%d" % i] for i in range(8)]
    PBB = [B["pb%d" % i] for i in range(8)]
    ST = T["stat"]
    STB = B["stat"]

    def XA(r):
        return XN[r][:, 2048:4096]

    def XB(r):
        return XN[r][:, 0:2048]

    op("vector", lambda e: e.memset(T["ones64"][:], 1.0), writes=bb("ones64"))
    op("vector", lambda e: e.memset(T["ones65"][:], 1.0), writes=bb("ones65"))
    op("vector", lambda e: e.memset(T["ones_row"][:], 1.0), writes=bb("ones_row"))
    op("vector", lambda e: e.memset(T["ones_colf"][:], 1.0), writes=bb("ones_colf"))
    op("vector", lambda e: e.memset(T["zeros_col"][:], 0.0), writes=bb("zeros_col"))
    op("vector", lambda e: e.memset(T["epsc"][:], 1e-5), writes=bb("epsc"))
    op("vector", lambda e: e.memset(T["ident"][:], 0.0), writes=bb("ident"))
    op("vector", lambda e: e.memset(T["diag2"][:], 0.0), writes=bb("diag2"))
    for ph in (0, 64):
        op("gpsimd",
           lambda e, ph=ph: e.affine_select(
               T["ident"][ph:ph + 64, ph:ph + 64], T["ones64"][:],
               pattern=[[-1, 64]], compare_op=ALU.is_equal, fill=0.0,
               base=0, channel_multiplier=1),
           reads=bb("ones64"), writes=bb("ident"))
        for ch in (0, 64):
            op("gpsimd",
               lambda e, ph=ph, ch=ch: e.affine_select(
                   T["diag2"][ph:ph + 64, ch:ch + 64], T["ones64"][:],
                   pattern=[[-1, 64]], compare_op=ALU.is_equal, fill=0.0,
                   base=0, channel_multiplier=1),
               reads=bb("ones64"), writes=bb("diag2"))
    for nm in ("pad_bias", "win0_bias", "halo_gate"):
        op("sync", lambda e, nm=nm: e.dma_start(out=T[nm + "_t"][:], in_=T[nm]),
           reads=bb(nm), writes=bb(nm + "_t"), dma=True)
    for idx, wn in enumerate(("WK", "WQ", "WV")):
        op("sync",
           lambda e, wn=wn, c0=idx * 16: e.dma_start(
               out=T["probs"][0:16, c0:c0 + 16], in_=T[wn]),
           reads=bb(wn), writes=bb("probs"), dma=True)
    for idx in range(3):
        op("scalar",
           lambda e, c0=idx * 16, idx=idx: e.activation(
               T["stat"][0:16, c0:c0 + 16], T["probs"][0:16, c0:c0 + 16],
               AF.Exp, accum_out=T["prob_s"][0:16, idx:idx + 1]),
           reads=bb("probs"), writes=bb("stat", "prob_s"))
    op("vector", lambda e: e.reciprocal(T["prob_r"][0:16, 0:3], T["prob_s"][0:16, 0:3]),
       reads=bb("prob_s"), writes=bb("prob_r"))
    for idx in range(3):
        op("vector",
           lambda e, c0=idx * 16, idx=idx: e.tensor_scalar(
               T["probs"][0:16, c0:c0 + 16], T["stat"][0:16, c0:c0 + 16],
               T["prob_r"][0:16, idx:idx + 1], None, ALU.mult),
           reads=bb("stat", "prob_r"), writes=bb("probs"))
    for idx in range(3):
        op("scalar",
           lambda e, idx=idx, c0=idx * 16: e.dma_start(
               out=T["probs_hbm"][idx], in_=T["probs"][0:16, c0:c0 + 16]),
           reads=bb("probs"), writes=bb("probs_hbm"), dma=True)
    for idx, cn in enumerate(("colK", "colQ", "colV")):
        for half in (0, 1):
            op("sync",
               lambda e, idx=idx, cn=cn, half=half: e.dma_start(
                   out=T[cn][half * 64:half * 64 + 64, 0:128]
                       .rearrange("p (i r) -> p i r", i=16),
                   in_=T["probs_hbm"][idx][:, half::2]
                       .unsqueeze(0).broadcast_to([64, 16, 8]),
                   allow_slow_non_contiguous=True),
               reads=bb("probs_hbm"), writes=bb(cn), dma=True)

    xtm, xtm_b = seg("xtm", 0, 1024)
    xnm, xnm_b = seg("xnm", 1024, 1024)
    SQ1, SQ1_b = seg("SQ1", 2048, 1024)
    for tt in range(SCTX // 128):
        op("sync", lambda e, tt=tt: e.dma_start(
            out=xtm, in_=T["x_ctx"][tt * 128:(tt + 1) * 128, :]),
           reads=bb("x_ctx"), writes=[xtm_b], dma=True)
        op("scalar", lambda e: e.activation(SQ1, xtm, AF.Square,
                                            accum_out=ST[:, 1:2]),
           reads=[xtm_b], writes=[SQ1_b, STB])
        op("vector", lambda e: e.reduce_sum(ST[:, 0:1], xtm, axis=AX.X),
           reads=[xtm_b], writes=[STB])
        op("vector", lambda e: e.tensor_scalar(ST[:, 2:3], ST[:, 0:1],
                                               1.0 / D, None, ALU.mult),
           reads=[STB], writes=[STB])
        op("vector", lambda e: e.tensor_mul(ST[:, 3:4], ST[:, 2:3], ST[:, 2:3]),
           reads=[STB], writes=[STB])
        op("vector", lambda e: e.tensor_scalar(ST[:, 4:5], ST[:, 1:2],
                                               1.0 / D, None, ALU.mult),
           reads=[STB], writes=[STB])
        op("vector", lambda e: e.tensor_sub(ST[:, 5:6], ST[:, 4:5], ST[:, 3:4]),
           reads=[STB], writes=[STB])
        op("scalar", lambda e: e.activation(ST[:, 6:7], ST[:, 5:6], AF.Ln,
                                            bias=T["epsc"][:], scale=1.0),
           reads=[STB, B["epsc"]], writes=[STB])
        op("scalar", lambda e: e.activation(ST[:, 7:8], ST[:, 6:7], AF.Exp,
                                            scale=-0.5),
           reads=[STB], writes=[STB])
        op("vector", lambda e: e.tensor_mul(ST[:, 8:9], ST[:, 2:3], ST[:, 7:8]),
           reads=[STB], writes=[STB])
        op("vector", lambda e: e.tensor_scalar(ST[:, 9:10], ST[:, 8:9],
                                               -1.0, None, ALU.mult),
           reads=[STB], writes=[STB])
        op("vector", lambda e: e.tensor_scalar(xnm, xtm, ST[:, 7:8], ST[:, 9:10],
                                               ALU.mult, ALU.add),
           reads=[xtm_b, STB], writes=[xnm_b])
        for r in range(NF):
            bank = r % 2
            op("tensor", lambda e, r=r, bank=bank: e.transpose(
                PB[bank][:, 0:128], xnm[:, r * 128:(r + 1) * 128], T["ident"][:]),
               reads=[xnm_b, B["ident"]], writes=[PBB[bank]])
            op("vector", lambda e, r=r, tt=tt, bank=bank: e.tensor_copy(
                XN[r][:, tt * 128:(tt + 1) * 128], PB[bank][:, 0:128]),
               reads=[PBB[bank]], writes=[XNB[r]])

    if debug >= 1:
        dram_out("xn_dbg", [NF, 128, SCTX])
        for r in range(NF):
            op("gpsimd", lambda e, r=r: e.dma_start(out=T["xn_dbg"][r], in_=XN[r][:]),
               reads=[XNB[r]], writes=bb("xn_dbg"), dma=True)

    kb.barrier()

    if do_outer:
        KP, KP_b = seg("KP", 0, 4096)
        VA, VA_b = seg("VA", 4096, 2080)
        VB_, VB_b = seg("VB", 6176, 2080)
        QP, QP_b = seg("QP", 8256, 2048)
        P0, P0_b = seg("P0", 10304, 512)
        P1, P1_b = seg("P1", 10816, 512)
        KD, KD_b = seg("KD", 11328, 1024)
        QD, QD_b = seg("QD", 12352, 1024)
        VD, VD_b = seg("VD", 13376, 1024)
        ZT, ZT_b = seg("ZT", 14400, 1024)
        for hp in range(pairs_outer):
            iA = 2 * hp
            for r in range(NF):
                for iL in range(2):
                    cc = (iA + iL) * 8 + r
                    op("vector",
                       lambda e, r=r, iL=iL, cc=cc: e.tensor_scalar(
                           KD[:, r * 128 + iL * 64:r * 128 + iL * 64 + 64],
                           T["diag2"][:, iL * 64:iL * 64 + 64],
                           T["colK"][:, cc:cc + 1], None, ALU.mult),
                       reads=bb("diag2", "colK"), writes=[KD_b])
                    op("vector",
                       lambda e, r=r, iL=iL, cc=cc: e.tensor_scalar(
                           VD[:, r * 128 + iL * 64:r * 128 + iL * 64 + 64],
                           T["diag2"][:, iL * 64:iL * 64 + 64],
                           T["colV"][:, cc:cc + 1], None, ALU.mult),
                       reads=bb("diag2", "colV"), writes=[VD_b])
            for iL in range(2):
                for half in (0, 1):
                    op("sync",
                       lambda e, iL=iL, ih=iA + iL, half=half: e.dma_start(
                           out=T["wpred2"][half * 64:half * 64 + 64,
                                           iL * 64:iL * 64 + 64],
                           in_=T["Wpred"][ih]),
                       reads=bb("Wpred"), writes=bb("wpred2"), dma=True)
            for r in range(NF):
                for iL in range(2):
                    cc = (iA + iL) * 8 + r
                    op("vector",
                       lambda e, r=r, iL=iL, cc=cc: e.tensor_scalar(
                           QD[:, r * 128 + iL * 64:r * 128 + iL * 64 + 64],
                           T["wpred2"][:, iL * 64:iL * 64 + 64],
                           T["colQ"][:, cc:cc + 1], None, ALU.mult),
                       reads=bb("wpred2", "colQ"), writes=[QD_b])
            for mt in range(8):
                bank = 6 + (mt % 2)
                for r in range(NF):
                    op("tensor",
                       lambda e, r=r, mt=mt, bank=bank: e.matmul(
                           PB[bank][:], KD[:, r * 128:(r + 1) * 128],
                           XN[r][:, mt * 512:(mt + 1) * 512],
                           start=(r == 0), stop=(r == NF - 1)),
                       reads=[KD_b, XNB[r]], writes=[PBB[bank]],
                       last=(r == NF - 1))
                op("vector",
                   lambda e, mt=mt, bank=bank: e.tensor_copy(
                       KP[:, mt * 512:(mt + 1) * 512], PB[bank][:]),
                   reads=[PBB[bank]], writes=[KP_b])
            op("vector", lambda e: e.memset(
                VA.rearrange("p (t c) -> p t c", c=65)[:, :, 64:65], 1.0),
               writes=[VA_b])
            op("vector", lambda e: e.memset(
                VB_.rearrange("p (t c) -> p t c", c=65)[:, :, 64:65], 1.0),
               writes=[VB_b])
            for tt in range(32):
                bank = 6 + (tt % 2)
                for r in range(NF):
                    op("tensor",
                       lambda e, r=r, tt=tt, bank=bank: e.matmul(
                           PB[bank][:, 0:128], XN[r][:, tt * 128:(tt + 1) * 128],
                           VD[:, r * 128:(r + 1) * 128],
                           start=(r == 0), stop=(r == NF - 1)),
                       reads=[VD_b, XNB[r]], writes=[PBB[bank]],
                       last=(r == NF - 1))
                op("vector",
                   lambda e, tt=tt, bank=bank: e.tensor_copy(
                       VA[:, tt * 65:tt * 65 + 64], PB[bank][:, 0:64]),
                   reads=[PBB[bank]], writes=[VA_b])
                op("vector",
                   lambda e, tt=tt, bank=bank: e.tensor_copy(
                       VB_[:, tt * 65:tt * 65 + 64], PB[bank][:, 64:128]),
                   reads=[PBB[bank]], writes=[VB_b])
            for mt in range(4):
                bank = 6 + (mt % 2)
                for r in range(NF):
                    op("tensor",
                       lambda e, r=r, mt=mt, bank=bank: e.matmul(
                           PB[bank][:], QD[:, r * 128:(r + 1) * 128],
                           XN[r][:, 2048 + mt * 512:2048 + (mt + 1) * 512],
                           start=(r == 0), stop=(r == NF - 1)),
                       reads=[QD_b, XNB[r]], writes=[PBB[bank]],
                       last=(r == NF - 1))
                op("vector",
                   lambda e, mt=mt, bank=bank: e.tensor_copy(
                       QP[:, mt * 512:(mt + 1) * 512], PB[bank][:]),
                   reads=[PBB[bank]], writes=[QP_b])
            for j in range(4):
                nd = 20 + 4 * j
                for d in range(nd):
                    stair = d - (nd - 4)
                    for iL in range(2):
                        Pt, Pt_b = (P0, P0_b) if iL == 0 else (P1, P1_b)
                        sbank = iL
                        zb = 4 + iL
                        bias = T["pad_bias_t"][:] if d < 16 else T["zeros_col"][:]
                        bias_b = B["pad_bias_t"] if d < 16 else B["zeros_col"]
                        op("tensor",
                           lambda e, d=d, j=j, iL=iL, sbank=sbank: e.matmul(
                               PB[sbank][:],
                               KP[iL * 64:iL * 64 + 64, d * 128:(d + 1) * 128],
                               QP[iL * 64:iL * 64 + 64, j * 512:(j + 1) * 512],
                               start=True, stop=True,
                               tile_position=(iL * 64, 0)),
                           reads=[KP_b, QP_b], writes=[PBB[sbank]])
                        op("scalar",
                           lambda e, Pt=Pt, sbank=sbank, bias=bias: e.activation(
                               Pt, PB[sbank][:], AF.Exp, bias=bias, scale=0.125),
                           reads=[PBB[sbank], bias_b], writes=[Pt_b])
                        if stair >= 0:
                            op("gpsimd",
                               lambda e, Pt=Pt, stair=stair: e.affine_select(
                                   Pt, Pt, pattern=[[1, 512]],
                                   compare_op=ALU.is_ge, fill=0.0,
                                   base=-128 * stair, channel_multiplier=-1),
                               reads=[Pt_b], writes=[Pt_b])
                        VV, VVb = (VA, VA_b) if iL == 0 else (VB_, VB_b)
                        op("tensor",
                           lambda e, d=d, zb=zb, Pt=Pt, VV=VV, nd=nd: e.matmul(
                               PB[zb][0:65, :], VV[:, d * 65:d * 65 + 65], Pt,
                               start=(d == 0), stop=(d == nd - 1)),
                           reads=[VVb, Pt_b], writes=[PBB[zb]])
                for iL in range(2):
                    zb = 4 + iL
                    op("vector",
                       lambda e, zb=zb: e.reciprocal(
                           T["denr"][64:65, 0:512], PB[zb][64:65, 0:512]),
                       reads=[PBB[zb]], writes=bb("denr"))
                    op("tensor",
                       lambda e: e.matmul(
                           PB[6][0:64, :], T["ones65"][64:65, 0:64],
                           T["denr"][64:65, 0:512], start=True, stop=True),
                       reads=bb("denr", "ones65"), writes=[PBB[6]])
                    op("vector",
                       lambda e, zb=zb, iL=iL: e.tensor_copy(
                           ZT[0:64, iL * 512:iL * 512 + 512], PB[zb][0:64, :]),
                       reads=[PBB[zb]], writes=[ZT_b])
                    op("vector",
                       lambda e, iL=iL: e.tensor_mul(
                           ZT[64:128, iL * 512:iL * 512 + 512],
                           ZT[0:64, iL * 512:iL * 512 + 512], PB[6][0:64, :]),
                       reads=[ZT_b, PBB[6]], writes=[ZT_b])
                    op("gpsimd",
                       lambda e, hp=hp, iL=iL, j=j: e.dma_start(
                           out=T["z_scr"][hp * 128 + iL * 64:
                                          hp * 128 + iL * 64 + 64,
                                          j * 512:(j + 1) * 512],
                           in_=ZT[64:128, iL * 512:iL * 512 + 512]),
                       reads=[ZT_b], writes=bb("z_scr"), dma=True)

        if debug >= 2:
            dram_out("z_dbg", [D, SOWN])
            op("gpsimd", lambda e: e.dma_start(out=T["z_dbg"], in_=T["z_scr"]),
               reads=bb("z_scr"), writes=bb("z_dbg"), dma=True)

    kb.barrier()

    if do_inner:
        tA, tA_b = seg("tA", 0, 1024)
        tBt, tBt_b = seg("tBt", 1024, 1024)
        tAi, tAi_b = seg("tAi", 2048, 1152)
        tBi, tBi_b = seg("tBi", 3200, 1152)
        QF, QF_b = seg("QF", 4352, 2048)
        KF, KF_b = seg("KF", 6400, 2304)
        VAi, VAi_b = seg("VAi", 8704, 1170)
        VBi, VBi_b = seg("VBi", 9874, 1170)
        WQK, WQK_b = seg("WQK", 11044, 3072)
        QKVT, QKVT_b = seg("QKVT", 14116, 384)
        RT1, RT1_b = seg("RT1", 14500, 256)
        RT2, RT2_b = seg("RT2", 14756, 256)
        ROT1 = AR[:, 14116:14116 + 256]
        ROT1_b = QKVT_b
        P0o = AR[:, 14116:14116 + 256]
        P0o_b = QKVT_b
        P1o = AR[:, 14500:14500 + 256]
        P1o_b = RT1_b
        HF, HF_b = seg("HF", 15012, NF * 257)
        SHQ, SHQ_b = seg("SHQ", 17068, 512)
        WOB, WOB_b = seg("WOB", 0, 8192)
        STG, STG_b = seg("STG", 8192, 4096)
        GG, GG_b = seg("GG", 0, NCC * 512)
        W1A, W1A_b = seg("W1A", 11264, 1024)
        W1G, W1G_b = seg("W1G", 12288, 1024)
        W2C, W2C_b = seg("W2C", 13312, 1024)
        ABC, ABC_b = seg("ABC", 14336, 512)
        CBC, CBC_b = seg("CBC", 14848, 512)
        GEL, GEL_b = seg("GEL", 15360, 512)
        ATM, ATM_b = seg("ATM", 15872, 512)
        SHF, SHF_b = seg("SHF", 16384, 512)
        RWB, RWB_b = seg("RWB", 16896, 512)

        for l in range(depth):
            for tn, dst, dstb, ntl, skip in (
                    ("tabA", tA, tA_b, 16, 2), ("tabB", tBt, tBt_b, 16, 2),
                    ("tabAi", tAi, tAi_b, 18, 0), ("tabBi", tBi, tBi_b, 18, 0)):
                op("sync",
                   lambda e, tn=tn, dst=dst, ntl=ntl, skip=skip: e.dma_start(
                       out=dst.rearrange("p (t c) -> p t c", c=64),
                       in_=T[tn][skip * 128:, :]
                           .rearrange("(t p) c -> p t c", p=128)),
                   reads=bb(tn), writes=[dstb], dma=True)
            for src, c0 in (("qsc", 0), ("qsc", 64), ("ksc", 128), ("ksc", 192)):
                op("sync",
                   lambda e, src=src, c0=c0, l=l: e.dma_start(
                       out=T["qsks"][:, c0:c0 + 64],
                       in_=T[src][l].unsqueeze(0).broadcast_to([128, DH])),
                   reads=bb(src), writes=bb("qsks"), dma=True)
            op("sync", lambda e, l=l: e.dma_start(
                out=T["vecs"][:, 0:21],
                in_=T["b1"][l][0:21 * 128].rearrange("(c p) -> p c", p=128),
                allow_slow_non_contiguous=True),
               reads=bb("b1"), writes=bb("vecs"), dma=True)
            op("sync", lambda e, l=l: e.dma_start(
                out=T["vecs"][0:LASTW, 21:22],
                in_=T["b1"][l][21 * 128:DI].unsqueeze(1),
                allow_slow_non_contiguous=True),
               reads=bb("b1"), writes=bb("vecs"), dma=True)
            op("sync", lambda e, l=l: e.dma_start(
                out=T["vecs"][:, 22:43],
                in_=T["b1"][l][DI:DI + 21 * 128].rearrange("(c p) -> p c", p=128),
                allow_slow_non_contiguous=True),
               reads=bb("b1"), writes=bb("vecs"), dma=True)
            op("sync", lambda e, l=l: e.dma_start(
                out=T["vecs"][0:LASTW, 43:44],
                in_=T["b1"][l][DI + 21 * 128:2 * DI].unsqueeze(1),
                allow_slow_non_contiguous=True),
               reads=bb("b1"), writes=bb("vecs"), dma=True)
            op("sync", lambda e, l=l: e.dma_start(
                out=T["vecs"][:, 44:52],
                in_=T["b2"][l].rearrange("(c p) -> p c", p=128),
                allow_slow_non_contiguous=True),
               reads=bb("b2"), writes=bb("vecs"), dma=True)
            op("sync", lambda e, l=l: e.dma_start(
                out=T["vecs"][:, 52:60],
                in_=T["lng"][l].rearrange("(c p) -> p c", p=128),
                allow_slow_non_contiguous=True),
               reads=bb("lng"), writes=bb("vecs"), dma=True)
            op("sync", lambda e, l=l: e.dma_start(
                out=T["vecs"][:, 60:68],
                in_=T["lnb"][l].rearrange("(c p) -> p c", p=128),
                allow_slow_non_contiguous=True),
               reads=bb("lnb"), writes=bb("vecs"), dma=True)

            for r in range(NF):
                op("gpsimd", lambda e, r=r: e.dma_start(
                    out=T["cc1_in"][r], in_=XA(r)[:, 1791:2048]),
                   reads=[XNB[r]], writes=bb("cc1_in"), dma=True)
            op("gpsimd",
               lambda e: e.collective_compute(
                   "AllGather", ALU.bypass,
                   replica_groups=[[0, 1], [2, 3], [4, 5], [6, 7]],
                   ins=[T["cc1_in"]], outs=[T["cc1_out"]]),
               reads=bb("cc1_in"), writes=bb("cc1_out"))
            for r in range(NF):
                op("scalar", lambda e, r=r: e.dma_start(
                    out=SHQ[:, 0:257], in_=T["cc1_out"][0, r]),
                   reads=bb("cc1_out"), writes=[SHQ_b], dma=True)
                op("vector", lambda e, r=r: e.tensor_scalar(
                    HF[:, r * 257:(r + 1) * 257], SHQ[:, 0:257],
                    T["halo_gate_t"][:], None, ALU.mult),
                   reads=[SHQ_b, B["halo_gate_t"]], writes=[HF_b])

            for r in range(4, NF):
                c0 = (r - 4) * 128
                op("vector", lambda e, r=r, c0=c0: e.tensor_copy(
                    SHQ[:, c0:c0 + 1], HF[:, r * 257 + 256:r * 257 + 257]),
                   reads=[HF_b], writes=[SHQ_b])
                op("vector", lambda e, r=r, c0=c0: e.tensor_copy(
                    SHQ[:, c0 + 1:c0 + 128], XA(r)[:, 0:127]),
                   reads=[XNB[r]], writes=[SHQ_b])

            def shifted_attn(r, tile):
                if tile == 0:
                    o = r * 257 + (1 if r < 4 else 0)
                    return HF[:, o:o + 128], HF_b
                if tile == 1:
                    o = r * 257 + (129 if r < 4 else 128)
                    return HF[:, o:o + 128], HF_b
                t = tile - 2
                if r < 4:
                    return XA(r)[:, t * 128:(t + 1) * 128], XNB[r]
                if t == 0:
                    c0 = (r - 4) * 128
                    return SHQ[:, c0:c0 + 128], SHQ_b
                return XA(r)[:, t * 128 - 1:(t + 1) * 128 - 1], XNB[r]

            for hp in range(8):
                for sgi in range(3):
                    op("sync",
                       lambda e, l=l, hp=hp, sgi=sgi: e.dma_start(
                           out=WQK.rearrange("p (r c) -> p r c", r=8)
                               [:, :, sgi * 128:(sgi + 1) * 128],
                           in_=T["Wqkv"][l]
                               .rearrange("(r p) c -> p r c", p=128)
                               [:, :, sgi * D + hp * 128:sgi * D + (hp + 1) * 128]),
                       reads=bb("Wqkv"), writes=[WQK_b], dma=True)
                op("vector", lambda e: e.memset(
                    VAi.rearrange("p (t c) -> p t c", c=65)[:, :, 64:65], 1.0),
                   writes=[VAi_b])
                op("vector", lambda e: e.memset(
                    VBi.rearrange("p (t c) -> p t c", c=65)[:, :, 64:65], 1.0),
                   writes=[VBi_b])
                for tt in range(18):
                    bank = 6 + (tt % 2)
                    for r in range(NF):
                        lhs, lhs_b = shifted_attn(r, tt)
                        op("tensor",
                           lambda e, lhs=lhs, r=r, bank=bank: e.matmul(
                               PB[bank][:, 0:384], lhs,
                               WQK[:, r * 384:(r + 1) * 384],
                               start=(r == 0), stop=(r == NF - 1)),
                           reads=[lhs_b, WQK_b], writes=[PBB[bank]],
                           last=(r == NF - 1))
                    op("vector",
                       lambda e, bank=bank: e.tensor_copy(QKVT, PB[bank][:, 0:384]),
                       reads=[PBB[bank]], writes=[QKVT_b])
                    op("vector",
                       lambda e, tt=tt: e.tensor_copy(
                           VAi[:, tt * 65:tt * 65 + 64], QKVT[:, 256:320]),
                       reads=[QKVT_b], writes=[VAi_b])
                    op("vector",
                       lambda e, tt=tt: e.tensor_copy(
                           VBi[:, tt * 65:tt * 65 + 64], QKVT[:, 320:384]),
                       reads=[QKVT_b], writes=[VBi_b])
                    for q4 in range(4):
                        op("scalar",
                           lambda e, q4=q4: e.activation(
                               RT2[:, q4 * 64:q4 * 64 + 64],
                               QKVT[:, q4 * 64:q4 * 64 + 64], AF.Square,
                               accum_out=ST[:, q4:q4 + 1]),
                           reads=[QKVT_b], writes=[RT2_b, STB])
                    op("vector",
                       lambda e: e.tensor_scalar(ST[:, 4:8], ST[:, 0:4],
                                                 1e-24, None, ALU.max),
                       reads=[STB], writes=[STB])
                    op("scalar", lambda e: e.activation(ST[:, 8:12], ST[:, 4:8],
                                                        AF.Ln),
                       reads=[STB], writes=[STB])
                    op("scalar", lambda e: e.activation(ST[:, 12:16], ST[:, 8:12],
                                                        AF.Exp, scale=-0.5),
                       reads=[STB], writes=[STB])
                    for q4 in range(4):
                        op("vector",
                           lambda e, q4=q4: e.tensor_scalar(
                               RT1[:, q4 * 64:q4 * 64 + 64],
                               QKVT[:, q4 * 64:q4 * 64 + 64],
                               ST[:, 12 + q4:13 + q4], None, ALU.mult),
                           reads=[QKVT_b, STB], writes=[RT1_b])
                    op("vector",
                       lambda e: e.tensor_mul(RT2, RT1, T["qsks"][:]),
                       reads=[RT1_b, B["qsks"]], writes=[RT2_b])
                    sides = ([("q", 0, tA, tA_b, tBt, tBt_b, tt - 2)]
                             if tt >= 2 else [])
                    sides.append(("k", 128, tAi, tAi_b, tBi, tBi_b, tt))
                    for nmx, o0, tac, tac_b, tbc, tbc_b, ti in sides:
                        op("vector",
                           lambda e, o0=o0, tac=tac, ti=ti: e.tensor_mul(
                               RT1[:, o0:o0 + 128]
                                   .rearrange("p (h c) -> p h c", h=2),
                               RT2[:, o0:o0 + 128]
                                   .rearrange("p (h c) -> p h c", h=2),
                               tac[:, ti * 64:(ti + 1) * 64]
                                   .unsqueeze(1).broadcast_to([128, 2, 64])),
                           reads=[RT2_b, tac_b], writes=[RT1_b])
                        op("vector",
                           lambda e, tbc=tbc, ti=ti, o0=o0: e.tensor_mul(
                               ROT1[:, 0:128]
                                   .rearrange("p (h c) -> p h c", h=2),
                               RT2[:, o0:o0 + 128]
                                   .rearrange("p (h c) -> p h c", h=2),
                               tbc[:, ti * 64:(ti + 1) * 64]
                                   .unsqueeze(1).broadcast_to([128, 2, 64])),
                           reads=[RT2_b, tbc_b], writes=[ROT1_b])
                        op("vector",
                           lambda e, o0=o0: e.tensor_add(
                               RT2[:, o0:o0 + 128]
                                   .rearrange("p (h s c) -> p h s c", h=2, s=2),
                               RT1[:, o0:o0 + 128]
                                   .rearrange("p (h s c) -> p h s c", h=2, s=2),
                               _swap_ap(ROT1)),
                           reads=[RT1_b, ROT1_b], writes=[RT2_b])
                    if tt >= 2:
                        op("tensor", lambda e: e.transpose(
                            PB[2][:, 0:128], RT2[:, 0:128], T["ident"][:]),
                           reads=[RT2_b, B["ident"]], writes=[PBB[2]])
                        op("vector",
                           lambda e, tt=tt: e.tensor_copy(
                               QF[:, (tt - 2) * 128:(tt - 1) * 128],
                               PB[2][:, 0:128]),
                           reads=[PBB[2]], writes=[QF_b])
                    op("tensor", lambda e: e.transpose(
                        PB[3][:, 0:128], RT2[:, 128:256], T["ident"][:]),
                       reads=[RT2_b, B["ident"]], writes=[PBB[3]])
                    op("vector",
                       lambda e, tt=tt: e.tensor_copy(
                           KF[:, tt * 128:(tt + 1) * 128], PB[3][:, 0:128]),
                       reads=[PBB[3]], writes=[KF_b])
                for w in range(8):
                    for st in range(4):
                        kcol = w * 256 + st * 128
                        kt = 2 * w + st
                        for iL in range(2):
                            Pt, Pt_b = (P0o, P0o_b) if iL == 0 else (P1o, P1o_b)
                            sbank = iL
                            zb = 4 + iL
                            bias = (T["win0_bias_t"][:] if (w == 0 and st < 2)
                                    else T["zeros_col"][:])
                            bias_b = (B["win0_bias_t"] if (w == 0 and st < 2)
                                      else B["zeros_col"])
                            op("tensor",
                               lambda e, iL=iL, kcol=kcol, w=w, sbank=sbank:
                               e.matmul(
                                   PB[sbank][:, 0:256],
                                   KF[iL * 64:iL * 64 + 64, kcol:kcol + 128],
                                   QF[iL * 64:iL * 64 + 64,
                                      w * 256:(w + 1) * 256],
                                   start=True, stop=True,
                                   tile_position=(iL * 64, 0)),
                               reads=[KF_b, QF_b], writes=[PBB[sbank]])
                            op("scalar",
                               lambda e, Pt=Pt, sbank=sbank, bias=bias:
                               e.activation(Pt, PB[sbank][:, 0:256], AF.Exp,
                                            bias=bias, scale=8.0),
                               reads=[PBB[sbank], bias_b], writes=[Pt_b])
                            if st < 2:
                                args = dict(pattern=[[-1, 256]],
                                            base=128 * (st % 2),
                                            channel_multiplier=1)
                            else:
                                args = dict(pattern=[[1, 256]],
                                            base=-128 * (st % 2),
                                            channel_multiplier=-1)
                            op("gpsimd",
                               lambda e, Pt=Pt, args=args:
                               e.affine_select(
                                   Pt, Pt, compare_op=ALU.is_ge, fill=0.0,
                                   **args),
                               reads=[Pt_b], writes=[Pt_b])
                            VV, VVb = (VAi, VAi_b) if iL == 0 else (VBi, VBi_b)
                            op("tensor",
                               lambda e, kt=kt, zb=zb, Pt=Pt, VV=VV, st=st:
                               e.matmul(
                                   PB[zb][0:65, 0:256],
                                   VV[:, kt * 65:kt * 65 + 65], Pt,
                                   start=(st == 0), stop=(st == 3)),
                               reads=[VVb, Pt_b], writes=[PBB[zb]])
                    for iL in range(2):
                        zb = 4 + iL
                        op("vector",
                           lambda e, zb=zb: e.reciprocal(
                               T["denr"][64:65, 0:256], PB[zb][64:65, 0:256]),
                           reads=[PBB[zb]], writes=bb("denr"))
                        op("tensor",
                           lambda e: e.matmul(
                               PB[6][0:64, 0:256], T["ones65"][64:65, 0:64],
                               T["denr"][64:65, 0:256], start=True, stop=True),
                           reads=bb("denr", "ones65"), writes=[PBB[6]])
                        op("vector",
                           lambda e, zb=zb: e.tensor_copy(
                               RT2[0:64, 0:256], PB[zb][0:64, 0:256]),
                           reads=[PBB[zb]], writes=[RT2_b])
                        op("vector",
                           lambda e, hp=hp, iL=iL, w=w: e.tensor_mul(
                               XB(hp)[iL * 64:iL * 64 + 64,
                                      w * 256:(w + 1) * 256],
                               RT2[0:64, 0:256], PB[6][0:64, 0:256]),
                           reads=[RT2_b, PBB[6]], writes=[XNB[hp]])

            kb.barrier()

            for o in range(NF):
                op("sync",
                   lambda e, l=l, o=o: e.dma_start(
                       out=WOB[:, o * 1024:(o + 1) * 1024],
                       in_=T["Wo"][l]
                           .rearrange("(r p) c -> p r c", p=128)
                           [:, :, o * 128:(o + 1) * 128]),
                   reads=bb("Wo"), writes=[WOB_b], dma=True)
            for mt in range(4):
                for o in range(NF):
                    bank = o % 4
                    for r in range(NF):
                        op("tensor",
                           lambda e, o=o, r=r, mt=mt, bank=bank: e.matmul(
                               PB[bank][:],
                               WOB[:, o * 1024 + r * 128:o * 1024 + (r + 1) * 128],
                               XB(r)[:, mt * 512:(mt + 1) * 512],
                               start=(r == 0), stop=(r == NF - 1)),
                           reads=[WOB_b, XNB[r]], writes=[PBB[bank]],
                           last=(r == NF - 1))
                    op("vector",
                       lambda e, o=o, mt=mt, bank=bank: e.scalar_tensor_tensor(
                           STG[:, o * 512:(o + 1) * 512], PB[bank][:], 0.0,
                           XA(o)[:, mt * 512:(mt + 1) * 512],
                           ALU.add, ALU.add),
                       reads=[PBB[bank], XNB[o]], writes=[STG_b])
                for o in range(NF):
                    op("vector",
                       lambda e, o=o, mt=mt: e.tensor_copy(
                           XB(o)[:, mt * 512:(mt + 1) * 512],
                           STG[:, o * 512:(o + 1) * 512]),
                       reads=[STG_b], writes=[XNB[o]])

            for r in range(NF):
                op("gpsimd", lambda e, r=r: e.dma_start(
                    out=T["cc2_in"][r], in_=XB(r)[:, 2047:2048]),
                   reads=[XNB[r]], writes=bb("cc2_in"), dma=True)
            op("gpsimd",
               lambda e: e.collective_compute(
                   "AllGather", ALU.bypass,
                   replica_groups=[[0, 1], [2, 3], [4, 5], [6, 7]],
                   ins=[T["cc2_in"]], outs=[T["cc2_out"]]),
               reads=bb("cc2_in"), writes=bb("cc2_out"))
            for r in range(NF):
                op("scalar", lambda e, r=r: e.dma_start(
                    out=T["tmpt"][:, 0:1], in_=T["cc2_out"][0, r]),
                   reads=bb("cc2_out"), writes=bb("tmpt"), dma=True)
                op("vector", lambda e, r=r: e.tensor_scalar(
                    T["xpc"][:, r:r + 1], T["tmpt"][:, 0:1],
                    T["halo_gate_t"][:], None, ALU.mult),
                   reads=[B["tmpt"], B["halo_gate_t"]], writes=bb("xpc"))

            kb.barrier()

            for r in range(4, NF):
                c0 = (r - 4) * 128
                op("vector", lambda e, r=r, c0=c0: e.tensor_copy(
                    SHF[:, c0:c0 + 1], T["xpc"][:, r:r + 1]),
                   reads=bb("xpc"), writes=[SHF_b])
                op("vector", lambda e, r=r, c0=c0: e.tensor_copy(
                    SHF[:, c0 + 1:c0 + 128], XB(r)[:, 0:127]),
                   reads=[XNB[r]], writes=[SHF_b])

            def shifted_ffn(r, mt, lo, hi):
                if r < 4:
                    return XB(r)[:, mt * 512 + lo:mt * 512 + hi], XNB[r]
                base = mt * 512
                if base + lo == 0:
                    c0 = (r - 4) * 128
                    return SHF[:, c0 + lo:c0 + hi], SHF_b
                return XB(r)[:, base + lo - 1:base + hi - 1], XNB[r]

            for mt in range(4):
                for r in range(NF):
                    pieces = ([(0, 128), (128, 512)] if (mt == 0 and r >= 4)
                              else [(0, 512)])
                    for pi, (lo, hi) in enumerate(pieces):
                        first = (r == 0 and pi == 0)
                        last_ = (r == NF - 1 and pi == len(pieces) - 1)
                        src, src_b = shifted_ffn(r, mt, lo, hi)
                        op("tensor",
                           lambda e, src=src, lo=lo, hi=hi, first=first,
                           last_=last_: e.matmul(
                               PB[6][0:1, lo:hi], T["ones_colf"][:], src,
                               start=first, stop=last_),
                           reads=[src_b, B["ones_colf"]], writes=[PBB[6]])
                        sqo = (r % 2) * 512
                        op("scalar",
                           lambda e, src=src, sqo=sqo, lo=lo, hi=hi: e.activation(
                               GG[:, sqo + lo:sqo + hi], src, AF.Square),
                           reads=[src_b], writes=[GG_b])
                        op("tensor",
                           lambda e, sqo=sqo, lo=lo, hi=hi, first=first,
                           last_=last_: e.matmul(
                               PB[7][0:1, lo:hi], T["ones_colf"][:],
                               GG[:, sqo + lo:sqo + hi],
                               start=first, stop=last_),
                           reads=[GG_b, B["ones_colf"]], writes=[PBB[7]])
                op("vector", lambda e: e.tensor_scalar(
                    T["rowsS"][0:1, :], PB[6][0:1, :], 1.0 / D, None, ALU.mult),
                   reads=[PBB[6]], writes=bb("rowsS"))
                op("vector", lambda e: e.tensor_scalar(
                    RWB[0:1, 0:512], PB[7][0:1, :], 1.0 / D, None, ALU.mult),
                   reads=[PBB[7]], writes=[RWB_b])
                op("vector", lambda e: e.tensor_mul(
                    PB[4][0:1, :], T["rowsS"][0:1, :], T["rowsS"][0:1, :]),
                   reads=bb("rowsS"), writes=[PBB[4]])
                op("vector", lambda e: e.tensor_sub(
                    PB[5][0:1, :], RWB[0:1, 0:512], PB[4][0:1, :]),
                   reads=[RWB_b, PBB[4]], writes=[PBB[5]])
                op("scalar", lambda e: e.activation(
                    PB[4][0:1, :], PB[5][0:1, :], AF.Ln,
                    bias=T["epsc"][0:1, :], scale=1.0),
                   reads=[PBB[5], B["epsc"]], writes=[PBB[4]])
                op("scalar", lambda e: e.activation(
                    RWB[0:1, 0:512], PB[4][0:1, :], AF.Exp, scale=-0.5),
                   reads=[PBB[4]], writes=[RWB_b])
                op("vector", lambda e: e.tensor_mul(
                    PB[5][0:1, :], T["rowsS"][0:1, :], RWB[0:1, 0:512]),
                   reads=[B["rowsS"], RWB_b], writes=[PBB[5]])
                op("vector", lambda e: e.tensor_copy(
                    T["rowsS"][0:1, :], PB[5][0:1, :]),
                   reads=[PBB[5]], writes=bb("rowsS"))
                op("tensor", lambda e: e.matmul(
                    PB[6][:, 0:512], T["ones_row"][:], RWB[0:1, 0:512],
                    start=True, stop=True),
                   reads=[RWB_b, B["ones_row"]], writes=[PBB[6]])
                op("tensor", lambda e: e.matmul(
                    PB[7][:, 0:512], T["ones_row"][:], T["rowsS"][0:1, :],
                    start=True, stop=True),
                   reads=[B["rowsS"], B["ones_row"]], writes=[PBB[7]])
                op("vector", lambda e: e.tensor_copy(ABC, PB[6][:]),
                   reads=[PBB[6]], writes=[ABC_b])
                op("vector", lambda e: e.tensor_copy(CBC, PB[7][:]),
                   reads=[PBB[7]], writes=[CBC_b])
                for r in range(NF):
                    pieces = ([(0, 128), (128, 512)] if (mt == 0 and r >= 4)
                              else [(0, 512)])
                    for lo, hi in pieces:
                        src, src_b = shifted_ffn(r, mt, lo, hi)
                        op("vector",
                           lambda e, src=src, lo=lo, hi=hi: e.tensor_mul(
                               GG[:, 1024 + lo:1024 + hi], src, ABC[:, lo:hi]),
                           reads=[src_b, ABC_b], writes=[GG_b])
                    op("vector", lambda e: e.tensor_sub(
                        GG[:, 1536:2048], GG[:, 1024:1536], CBC),
                       reads=[GG_b, CBC_b], writes=[GG_b])
                    op("vector",
                       lambda e, r=r, mt=mt: e.tensor_scalar(
                           XA(r)[:, mt * 512:(mt + 1) * 512],
                           GG[:, 1536:2048],
                           T["vecs"][:, 52 + r:53 + r], T["vecs"][:, 60 + r:61 + r],
                           ALU.mult, ALU.add),
                       reads=[GG_b, B["vecs"]], writes=[XNB[r]])

            for mt in range(4):
                for c in range(NCC):
                    wlast = 128 if c < NCC - 1 else LASTW
                    op("sync", lambda e, l=l, c=c, wlast=wlast: e.dma_start(
                        out=W1A[:, 0:8 * wlast]
                            .rearrange("p (r c) -> p r c", c=wlast),
                        in_=T["W1"][l]
                            .rearrange("(r p) c -> p r c", p=128)
                            [:, :, c * 128:c * 128 + wlast]),
                       reads=bb("W1"), writes=[W1A_b], dma=True)
                    op("sync", lambda e, l=l, c=c, wlast=wlast: e.dma_start(
                        out=W1G[:, 0:8 * wlast]
                            .rearrange("p (r c) -> p r c", c=wlast),
                        in_=T["W1"][l]
                            .rearrange("(r p) c -> p r c", p=128)
                            [:, :, DI + c * 128:DI + c * 128 + wlast]),
                       reads=bb("W1"), writes=[W1G_b], dma=True)
                    op("sync", lambda e, l=l, c=c, wlast=wlast: e.dma_start(
                        out=W2C[0:wlast, :],
                        in_=T["W2"][l][c * 128:c * 128 + wlast, :]),
                       reads=bb("W2"), writes=[W2C_b], dma=True)
                    for r in range(NF):
                        op("tensor",
                           lambda e, r=r, mt=mt, wlast=wlast: e.matmul(
                               PB[6][0:wlast, :],
                               W1A[:, r * wlast:(r + 1) * wlast],
                               XA(r)[:, mt * 512:(mt + 1) * 512],
                               start=(r == 0), stop=(r == NF - 1)),
                           reads=[W1A_b, XNB[r]], writes=[PBB[6]],
                           last=(r == NF - 1))
                    for r in range(NF):
                        op("tensor",
                           lambda e, r=r, mt=mt, wlast=wlast: e.matmul(
                               PB[7][0:wlast, :],
                               W1G[:, r * wlast:(r + 1) * wlast],
                               XA(r)[:, mt * 512:(mt + 1) * 512],
                               start=(r == 0), stop=(r == NF - 1)),
                           reads=[W1G_b, XNB[r]], writes=[PBB[7]],
                           last=(r == NF - 1))
                    op("scalar",
                       lambda e, c=c, wlast=wlast: e.activation(
                           GEL[0:wlast, :], PB[7][0:wlast, :], AF.Gelu,
                           bias=T["vecs"][0:wlast, 22 + c:23 + c]),
                       reads=[PBB[7], B["vecs"]], writes=[GEL_b])
                    op("vector",
                       lambda e, c=c, wlast=wlast: e.tensor_scalar(
                           ATM[0:wlast, :], PB[6][0:wlast, :],
                           T["vecs"][0:wlast, c:c + 1], None, ALU.add),
                       reads=[PBB[6], B["vecs"]], writes=[ATM_b])
                    op("vector",
                       lambda e, c=c, wlast=wlast: e.tensor_mul(
                           GG[0:wlast, c * 512:(c + 1) * 512],
                           ATM[0:wlast, :], GEL[0:wlast, :]),
                       reads=[ATM_b, GEL_b], writes=[GG_b])
                    for o in range(4):
                        op("tensor",
                           lambda e, c=c, o=o, wlast=wlast: e.matmul(
                               PB[o][:],
                               W2C[0:wlast, o * 128:(o + 1) * 128],
                               GG[0:wlast, c * 512:(c + 1) * 512],
                               start=(c == 0), stop=(c == NCC - 1)),
                           reads=[W2C_b, GG_b], writes=[PBB[o]])
                for o in range(4):
                    op("vector",
                       lambda e, o=o, mt=mt: e.scalar_tensor_tensor(
                           XA(o)[:, mt * 512:(mt + 1) * 512], PB[o][:],
                           T["vecs"][:, 44 + o:45 + o],
                           XB(o)[:, mt * 512:(mt + 1) * 512],
                           ALU.add, ALU.add),
                       reads=[PBB[o], B["vecs"], XNB[o]], writes=[XNB[o]])
                for c in range(NCC):
                    wlast = 128 if c < NCC - 1 else LASTW
                    op("sync", lambda e, l=l, c=c, wlast=wlast: e.dma_start(
                        out=W2C[0:wlast, 0:512],
                        in_=T["W2"][l][c * 128:c * 128 + wlast, 512:1024]),
                       reads=bb("W2"), writes=[W2C_b], dma=True)
                    for o in range(4):
                        op("tensor",
                           lambda e, c=c, o=o, wlast=wlast: e.matmul(
                               PB[4 + o][:],
                               W2C[0:wlast, o * 128:(o + 1) * 128],
                               GG[0:wlast, c * 512:(c + 1) * 512],
                               start=(c == 0), stop=(c == NCC - 1)),
                           reads=[W2C_b, GG_b], writes=[PBB[4 + o]])
                for o in range(4):
                    op("vector",
                       lambda e, o=o, mt=mt: e.scalar_tensor_tensor(
                           XA(4 + o)[:, mt * 512:(mt + 1) * 512], PB[4 + o][:],
                           T["vecs"][:, 48 + o:49 + o],
                           XB(4 + o)[:, mt * 512:(mt + 1) * 512],
                           ALU.add, ALU.add),
                       reads=[PBB[4 + o], B["vecs"], XNB[4 + o]],
                       writes=[XNB[4 + o]])

            kb.barrier()

            if debug >= 3:
                dram_out("xl%d_dbg" % l, [NF, 128, SOWN])
                for r in range(NF):
                    op("gpsimd", lambda e, r=r, l=l: e.dma_start(
                        out=T["xl%d_dbg" % l][r], in_=XA(r)),
                       reads=[XNB[r]], writes=bb("xl%d_dbg" % l), dma=True)
                kb.barrier()

    if do_final:
        OTM, OTM_b = seg("OTM", 0, 1024)
        XOW, XOW_b = seg("XOW", 1024, 1024)
        for r in range(NF):
            op("sync", lambda e, r=r: e.dma_start(
                out=XB(r), in_=T["z_scr"][r * 128:(r + 1) * 128, :]),
               reads=bb("z_scr"), writes=[XNB[r]], dma=True)
        for tt in range(16):
            op("sync", lambda e, tt=tt: e.dma_start(
                out=XOW, in_=T["x_ctx"][2048 + tt * 128:2048 + (tt + 1) * 128, :]),
               reads=bb("x_ctx"), writes=[XOW_b], dma=True)
            for r in range(NF):
                tb = T["tmpt"] if r % 2 == 0 else T["tmpt2"]
                tbb = B["tmpt"] if r % 2 == 0 else B["tmpt2"]
                bank = 2 + (r % 2)
                op("vector", lambda e, r=r, tt=tt, tb=tb: e.tensor_add(
                    tb[:], XA(r)[:, tt * 128:(tt + 1) * 128],
                    XB(r)[:, tt * 128:(tt + 1) * 128]),
                   reads=[XNB[r]], writes=[tbb])
                op("tensor", lambda e, tb=tb, bank=bank: e.transpose(
                    PB[bank][:, 0:128], tb[:], T["ident"][:]),
                   reads=[tbb, B["ident"]], writes=[PBB[bank]])
                op("vector", lambda e, r=r, bank=bank: e.scalar_tensor_tensor(
                    OTM[:, r * 128:(r + 1) * 128], PB[bank][:, 0:128], 0.0,
                    XOW[:, r * 128:(r + 1) * 128], ALU.add, ALU.add),
                   reads=[PBB[bank], XOW_b], writes=[OTM_b])
            op("sync", lambda e, tt=tt: e.dma_start(
                out=T["out_own"][tt * 128:(tt + 1) * 128, :], in_=OTM),
               reads=[OTM_b], writes=bb("out_own"), dma=True)
    kb.barrier()

    block = stack.enter_context(nc.Block())
    kb.finalize(block)
    return nc, stack, T


def _swap_ap(rot1):
    ap = rot1[:, 0:128]
    return bass.AP(ap.tensor, ap.offset + 32,
                   [list(ap.ap[0]), [64, 2], [-32, 2], [1, 32]])


def make_tables(h):
    half = np.arange(0, DH, 2, dtype=np.float64)
    inv_freq = 1.0 / 10000.0 ** (half / DH)
    base_scale = (half + 0.4 * DH) / (1.4 * DH)
    start = h * 2048 - 256
    t = np.arange(start, start + 2304, dtype=np.float64)
    tc = np.clip(t, 0, None)
    freqs = tc[:, None] * inv_freq[None, :]
    freqs = np.concatenate([freqs, freqs], -1)
    power = (tc - SCTX // 2) / (WIN // 2)
    scale = base_scale[None, :] ** power[:, None]
    scale = np.concatenate([scale, scale], -1)
    cos, sin = np.cos(freqs), np.sin(freqs)
    A = (cos * scale).astype(np.float32)
    S = (sin * scale).astype(np.float32)
    tabB = np.concatenate([S[:, :32], -S[:, 32:]], -1)
    Ai = (cos / scale).astype(np.float32)
    Si = (sin / scale).astype(np.float32)
    tabBi = np.concatenate([Si[:, :32], -Si[:, 32:]], -1)
    return A, tabB, Ai, tabBi


_CACHED = {}


def _get_nc():
    if "nc" not in _CACHED:
        nc, stack, Tm = build_nc()
        _CACHED["nc"] = nc
    return _CACHED["nc"]


def kernel(L0_attn_x, L0_attn_W_K, L0_attn_W_Q, L0_attn_W_V, L0_attn_W_pred,
           L0_ffn_Wqkv, L0_ffn_q_scale, L0_ffn_k_scale, L0_ffn_Wo,
           L0_ffn_ln_g, L0_ffn_ln_b, L0_ffn_W1, L0_ffn_b1, L0_ffn_W2,
           L0_ffn_b2):
    x = np.ascontiguousarray(np.asarray(L0_attn_x, dtype=np.float32))
    Bsz = x.shape[0]
    com = dict(
        Wqkv=np.ascontiguousarray(L0_ffn_Wqkv, dtype=np.float32),
        Wo=np.ascontiguousarray(L0_ffn_Wo, dtype=np.float32),
        W1=np.ascontiguousarray(L0_ffn_W1, dtype=np.float32),
        W2=np.ascontiguousarray(L0_ffn_W2, dtype=np.float32),
        b1=np.ascontiguousarray(L0_ffn_b1, dtype=np.float32),
        b2=np.ascontiguousarray(L0_ffn_b2, dtype=np.float32),
        lng=np.ascontiguousarray(L0_ffn_ln_g, dtype=np.float32),
        lnb=np.ascontiguousarray(L0_ffn_ln_b, dtype=np.float32),
        qsc=np.ascontiguousarray(L0_ffn_q_scale, dtype=np.float32),
        ksc=np.ascontiguousarray(L0_ffn_k_scale, dtype=np.float32),
        WK=np.ascontiguousarray(L0_attn_W_K, dtype=np.float32),
        WQ=np.ascontiguousarray(L0_attn_W_Q, dtype=np.float32),
        WV=np.ascontiguousarray(L0_attn_W_V, dtype=np.float32),
        Wpred=np.ascontiguousarray(L0_attn_W_pred, dtype=np.float32),
    )
    tabs = {h: make_tables(h) for h in (0, 1)}
    in_maps = []
    for c in range(8):
        b, h = c // 2, c % 2
        if h == 1:
            x_ctx = x[b]
        else:
            x_ctx = np.concatenate(
                [np.zeros((2048, D), dtype=np.float32), x[b, :2048]], axis=0)
        A, tb, Ai, tbi = tabs[h]
        m = dict(com)
        m.update(
            x_ctx=np.ascontiguousarray(x_ctx),
            tabA=A, tabB=tb, tabAi=Ai, tabBi=tbi,
            pad_bias=np.full((128, 1), NEG if h == 0 else 0.0, dtype=np.float32),
            win0_bias=np.full((128, 1), NEG if h == 0 else 0.0, dtype=np.float32),
            halo_gate=np.full((128, 1), 0.0 if h == 0 else 1.0, dtype=np.float32),
        )
        in_maps.append(m)
    nc = _get_nc()
    res = run_bass_kernel_spmd(nc, in_maps, core_ids=list(range(8)))
    out = np.empty((Bsz, SCTX, D), dtype=np.float32)
    for c in range(8):
        b, h = c // 2, c % 2
        out[b, h * 2048:(h + 1) * 2048] = res.results[c]["out_own"]
    return out


# version 03
# speedup vs baseline: 1.3000x; 1.3000x over previous
import numpy as np
from contextlib import ExitStack

import concourse.bass as bass
import concourse.mybir as mybir
from concourse.bass_utils import run_bass_kernel_spmd

F32 = mybir.dt.float32
AF = mybir.ActivationFunctionType
ALU = mybir.AluOpType
AX = mybir.AxisListType

D = 1024
H = 16
DH = 64
SOWN = 2048
SCTX = 4096
NDEPTH = 6
DI = 2730
WIN = 256
NEG = -1.0e9
NF = 8
NCC = (DI + 127) // 128
LASTW = DI - 128 * (NCC - 1)
ENGINES = ("sync", "tensor", "vector", "scalar", "gpsimd")


class Sem:
    __slots__ = ("h", "n", "name", "is_q")

    def __init__(self, h, name):
        self.h = h
        self.n = 0
        self.name = name
        self.is_q = False


class Buf:
    __slots__ = ("w", "r", "name")

    def __init__(self, name=""):
        self.w = {}
        self.r = {}
        self.name = name


def _merge(dst, src):
    for s, c in src.items():
        if dst.get(s, -1) < c:
            dst[s] = c


class KB:
    def __init__(self, nc, stack):
        self.nc = nc
        self.stack = stack
        self.ops = {e: [] for e in ENGINES}
        self.waited = {e: {} for e in ENGINES}
        self.sems = []
        self.dsem = {e: self.sem("zd" + e[:2]) for e in ENGINES}
        self.qsem = {e: [self.sem("zq" + e[:2] + str(i)) for i in range(8)]
                     for e in ("sync", "scalar", "gpsimd", "vector")}
        for pool in self.qsem.values():
            for s_ in pool:
                s_.is_q = True
        self.qrr = {e: 0 for e in self.qsem}

    def sem(self, name):
        h = self.stack.enter_context(self.nc.semaphore(name))
        s = Sem(h, name)
        self.sems.append(s)
        return s

    def op(self, eng, fn, reads=(), writes=(), dma=False, last=True, extra=()):
        if dma:
            pool = self.qsem[eng]
            s = pool[self.qrr[eng]]
            self.qrr[eng] = (self.qrr[eng] + 1) % len(pool)
            k = 16
            extra = list(extra) + [(s, s.n)]
        else:
            s = self.dsem[eng]
            k = 1
        waits = {}
        for b in reads:
            _merge(waits, b.w)
        for b in writes:
            _merge(waits, b.w)
            _merge(waits, b.r)
        for sm, c in extra:
            if waits.get(sm, -1) < c:
                waits[sm] = c
        wd = self.waited[eng]
        for sm in list(waits):
            if sm.is_q:
                waits[sm] = sm.n
        ws = [(sm, c) for sm, c in waits.items() if wd.get(sm, 0) < c]
        for sm, c in ws:
            wd[sm] = c
        if last:
            s.n += k
            cover = s.n
        else:
            cover = s.n + k
        self.ops[eng].append((ws, fn, s if last else None, k))
        for b in writes:
            if last:
                b.w = {s: cover}
                b.r = {}
        for b in reads:
            if b.r.get(s, -1) < cover:
                b.r[s] = cover
        return (s, cover)

    def barrier(self):
        bar = self.sem("zb%d" % len(self.sems))
        for e in ENGINES:
            ws = [(self.dsem[e], self.dsem[e].n)] if self.dsem[e].n else []
            for q in self.qsem.get(e, ()):
                if q.n:
                    ws.append((q, q.n))
            self.ops[e].append((ws, (lambda eng: eng.nop()), bar, 1))
            bar.n += 1
        tgt = bar.n
        snap = {s: s.n for s in self.sems if s.n > 0}
        for e in ENGINES:
            self.ops[e].append(([(bar, tgt)], None, None, 0))
            w = dict(snap)
            w[bar] = tgt
            self.waited[e] = w
        return bar

    def finalize(self, block):
        for e in ENGINES:
            deco = getattr(block, e)
            ops = self.ops[e]

            def make(ops=ops):
                def run(engine):
                    for ws, fn, s, k in ops:
                        for sm, c in ws:
                            engine.wait_ge(sm.h, c)
                        if fn is not None:
                            instr = fn(engine)
                            if s is not None:
                                instr.then_inc(s.h, k)
                return run
            deco(make())


def build_nc(debug=0, depth=NDEPTH, do_outer=True, do_inner=True, do_final=True,
             pairs_outer=8):
    nc = bass.Bass("TRN2", target_bir_lowering=False)
    stack = ExitStack()
    kb = KB(nc, stack)
    T = {}
    B = {}

    def dram_in(name, shape):
        T[name] = nc.dram_tensor(name, shape, F32, kind="ExternalInput").ap()
        B[name] = Buf(name)

    def dram_out(name, shape):
        T[name] = nc.dram_tensor(name, shape, F32, kind="ExternalOutput").ap()
        B[name] = Buf(name)

    def dram_tmp(name, shape):
        T[name] = nc.dram_tensor(name, shape, F32, kind="Internal").ap()
        B[name] = Buf(name)

    def sb(name, shape, dtype=F32):
        T[name] = nc.alloc_sbuf_tensor(name, shape, dtype).ap()
        B[name] = Buf(name)

    def psum(name, shape):
        T[name] = nc.alloc_psum_tensor(name, shape, F32).ap()
        B[name] = Buf(name)

    dram_in("x_ctx", [SCTX, D])
    dram_in("Wqkv", [6, D, 3 * D])
    dram_in("Wo", [6, D, D])
    dram_in("W1", [6, D, 2 * DI])
    dram_in("W2", [6, DI, D])
    dram_in("b1", [6, 2 * DI])
    dram_in("b2", [6, D])
    dram_in("lng", [6, D])
    dram_in("lnb", [6, D])
    dram_in("qsc", [6, DH])
    dram_in("ksc", [6, DH])
    dram_in("WK", [H, H])
    dram_in("WQ", [H, H])
    dram_in("WV", [H, H])
    dram_in("Wpred", [H, DH, DH])
    dram_in("tabA", [2304, DH])
    dram_in("tabB", [2304, DH])
    dram_in("tabAi", [2304, DH])
    dram_in("tabBi", [2304, DH])
    dram_in("pad_bias", [128, 1])
    dram_in("win0_bias", [128, 1])
    dram_in("halo_gate", [128, 1])
    dram_out("out_own", [SOWN, D])

    dram_tmp("z_scr", [D, SOWN])
    dram_tmp("Wqkv_opt", [6, 8, 128, 8, 384])
    dram_tmp("Wo_opt", [6, 8, 128, 8, 128])
    dram_tmp("W1_opt", [6, 2, NCC, 128, 8, 128])
    dram_tmp("probs_hbm", [3, H, H])
    dram_tmp("rows_hbm", [2, 512])
    dram_tmp("cc1_in", [NF, 128, 257])
    dram_tmp("cc1_out", [2, NF, 128, 257])
    dram_tmp("cc2_in", [NF, 128, 1])
    dram_tmp("cc2_out", [2, NF, 128, 1])

    for r in range(NF):
        sb("XN%d" % r, [128, SCTX])
    sb("ident", [128, 128])
    sb("diag2", [128, 128])
    sb("ones64", [64, 64])
    sb("ones65", [65, 64])
    sb("ones_row", [1, 128])
    sb("ones_colf", [128, 1])
    sb("zeros_col", [128, 1])
    sb("probs", [16, 48])
    sb("prob_s", [16, 8])
    sb("prob_r", [16, 8])
    sb("colK", [128, 128])
    sb("colQ", [128, 128])
    sb("colV", [128, 128])
    sb("pad_bias_t", [128, 1])
    sb("epsc", [128, 1])
    sb("win0_bias_t", [128, 1])
    sb("halo_gate_t", [128, 1])
    sb("wpred2", [128, 128])
    sb("stat", [128, 64])
    sb("denr", [65, 512])
    sb("vecs", [128, 72])
    sb("qsks", [128, 256])
    sb("rowsS", [1, 512])
    sb("tmpt", [128, 128])
    sb("tmpt2", [128, 128])
    sb("xpc", [128, 8])
    sb("ARENA", [128, 17616])
    for i in range(8):
        psum("pb%d" % i, [128, 512])

    AR = T["ARENA"]
    seg_tab = {}

    def seg(name, off, cols):
        if name not in seg_tab:
            seg_tab[name] = Buf("sg_" + name)
        return AR[:, off:off + cols], seg_tab[name]

    def op(eng, fn, reads=(), writes=(), **kw):
        return kb.op(eng, fn, reads=reads, writes=writes, **kw)

    def bb(*names):
        return [B[n] for n in names]

    XN = [T["XN%d" % r] for r in range(NF)]
    XNB = [B["XN%d" % r] for r in range(NF)]
    PB = [T["pb%d" % i] for i in range(8)]
    PBB = [B["pb%d" % i] for i in range(8)]
    ST = T["stat"]
    STB = B["stat"]

    def XA(r):
        return XN[r][:, 2048:4096]

    def XB(r):
        return XN[r][:, 0:2048]

    op("vector", lambda e: e.memset(T["ones64"][:], 1.0), writes=bb("ones64"))
    op("vector", lambda e: e.memset(T["ones65"][:], 1.0), writes=bb("ones65"))
    op("vector", lambda e: e.memset(T["ones_row"][:], 1.0), writes=bb("ones_row"))
    op("vector", lambda e: e.memset(T["ones_colf"][:], 1.0), writes=bb("ones_colf"))
    op("vector", lambda e: e.memset(T["zeros_col"][:], 0.0), writes=bb("zeros_col"))
    op("vector", lambda e: e.memset(T["epsc"][:], 1e-5), writes=bb("epsc"))
    op("vector", lambda e: e.memset(T["ident"][:], 0.0), writes=bb("ident"))
    op("vector", lambda e: e.memset(T["diag2"][:], 0.0), writes=bb("diag2"))
    for ph in (0, 64):
        op("gpsimd",
           lambda e, ph=ph: e.affine_select(
               T["ident"][ph:ph + 64, ph:ph + 64], T["ones64"][:],
               pattern=[[-1, 64]], compare_op=ALU.is_equal, fill=0.0,
               base=0, channel_multiplier=1),
           reads=bb("ones64"), writes=bb("ident"))
        for ch in (0, 64):
            op("gpsimd",
               lambda e, ph=ph, ch=ch: e.affine_select(
                   T["diag2"][ph:ph + 64, ch:ch + 64], T["ones64"][:],
                   pattern=[[-1, 64]], compare_op=ALU.is_equal, fill=0.0,
                   base=0, channel_multiplier=1),
               reads=bb("ones64"), writes=bb("diag2"))
    for nm in ("pad_bias", "win0_bias", "halo_gate"):
        op("sync", lambda e, nm=nm: e.dma_start(out=T[nm + "_t"][:], in_=T[nm]),
           reads=bb(nm), writes=bb(nm + "_t"), dma=True)
    for idx, wn in enumerate(("WK", "WQ", "WV")):
        op("sync",
           lambda e, wn=wn, c0=idx * 16: e.dma_start(
               out=T["probs"][0:16, c0:c0 + 16], in_=T[wn]),
           reads=bb(wn), writes=bb("probs"), dma=True)
    for idx in range(3):
        op("scalar",
           lambda e, c0=idx * 16, idx=idx: e.activation(
               T["stat"][0:16, c0:c0 + 16], T["probs"][0:16, c0:c0 + 16],
               AF.Exp, accum_out=T["prob_s"][0:16, idx:idx + 1]),
           reads=bb("probs"), writes=bb("stat", "prob_s"))
    op("vector", lambda e: e.reciprocal(T["prob_r"][0:16, 0:3], T["prob_s"][0:16, 0:3]),
       reads=bb("prob_s"), writes=bb("prob_r"))
    for idx in range(3):
        op("vector",
           lambda e, c0=idx * 16, idx=idx: e.tensor_scalar(
               T["probs"][0:16, c0:c0 + 16], T["stat"][0:16, c0:c0 + 16],
               T["prob_r"][0:16, idx:idx + 1], None, ALU.mult),
           reads=bb("stat", "prob_r"), writes=bb("probs"))
    for idx in range(3):
        op("scalar",
           lambda e, idx=idx, c0=idx * 16: e.dma_start(
               out=T["probs_hbm"][idx], in_=T["probs"][0:16, c0:c0 + 16]),
           reads=bb("probs"), writes=bb("probs_hbm"), dma=True)
    for idx, cn in enumerate(("colK", "colQ", "colV")):
        for half in (0, 1):
            op("sync",
               lambda e, idx=idx, cn=cn, half=half: e.dma_start(
                   out=T[cn][half * 64:half * 64 + 64, 0:128]
                       .rearrange("p (i r) -> p i r", i=16),
                   in_=T["probs_hbm"][idx][:, half::2]
                       .unsqueeze(0).broadcast_to([64, 16, 8]),
                   allow_slow_non_contiguous=True),
               reads=bb("probs_hbm"), writes=bb(cn), dma=True)

    for l in range(NDEPTH):
        for hp in range(8):
            for rr in range(NF):
                op("scalar",
                   lambda e, l=l, hp=hp, rr=rr: e.dma_start(
                       out=T["Wqkv_opt"][l, hp][:, rr]
                           .rearrange("p (sg c) -> p sg c", sg=3),
                       in_=T["Wqkv"][l]
                           .rearrange("(r p) (sg c) -> r p sg c", p=128, sg=3)
                           [rr, :, :, hp * 128:(hp + 1) * 128]),
                   reads=bb("Wqkv"), writes=bb("Wqkv_opt"), dma=True)
        for o in range(NF):
            op("scalar",
               lambda e, l=l, o=o: e.dma_start(
                   out=T["Wo_opt"][l, o],
                   in_=T["Wo"][l]
                       .rearrange("(r p) (o c) -> o p r c", p=128, o=8)[o]),
               reads=bb("Wo"), writes=bb("Wo_opt"), dma=True)
        for ag in range(2):
            for c in range(NCC):
                wl = 128 if c < NCC - 1 else LASTW
                op("gpsimd",
                   lambda e, l=l, ag=ag, c=c, wl=wl: e.dma_start(
                       out=T["W1_opt"][l, ag, c][:, :, 0:wl],
                       in_=T["W1"][l][:, ag * DI + c * 128:ag * DI + c * 128 + wl]
                           .rearrange("(r p) cc -> p r cc", p=128),
                       allow_slow_non_contiguous=True),
                   reads=bb("W1"), writes=bb("W1_opt"), dma=True)

    xtm, xtm_b = seg("xtm", 0, 1024)
    xnm, xnm_b = seg("xnm", 1024, 1024)
    SQ1, SQ1_b = seg("SQ1", 2048, 1024)
    for tt in range(SCTX // 128):
        op("sync", lambda e, tt=tt: e.dma_start(
            out=xtm, in_=T["x_ctx"][tt * 128:(tt + 1) * 128, :]),
           reads=bb("x_ctx"), writes=[xtm_b], dma=True)
        op("scalar", lambda e: e.activation(SQ1, xtm, AF.Square,
                                            accum_out=ST[:, 1:2]),
           reads=[xtm_b], writes=[SQ1_b, STB])
        op("vector", lambda e: e.reduce_sum(ST[:, 0:1], xtm, axis=AX.X),
           reads=[xtm_b], writes=[STB])
        op("vector", lambda e: e.tensor_scalar(ST[:, 2:3], ST[:, 0:1],
                                               1.0 / D, None, ALU.mult),
           reads=[STB], writes=[STB])
        op("vector", lambda e: e.tensor_mul(ST[:, 3:4], ST[:, 2:3], ST[:, 2:3]),
           reads=[STB], writes=[STB])
        op("vector", lambda e: e.tensor_scalar(ST[:, 4:5], ST[:, 1:2],
                                               1.0 / D, None, ALU.mult),
           reads=[STB], writes=[STB])
        op("vector", lambda e: e.tensor_sub(ST[:, 5:6], ST[:, 4:5], ST[:, 3:4]),
           reads=[STB], writes=[STB])
        op("scalar", lambda e: e.activation(ST[:, 6:7], ST[:, 5:6], AF.Ln,
                                            bias=T["epsc"][:], scale=1.0),
           reads=[STB, B["epsc"]], writes=[STB])
        op("scalar", lambda e: e.activation(ST[:, 7:8], ST[:, 6:7], AF.Exp,
                                            scale=-0.5),
           reads=[STB], writes=[STB])
        op("vector", lambda e: e.tensor_mul(ST[:, 8:9], ST[:, 2:3], ST[:, 7:8]),
           reads=[STB], writes=[STB])
        op("vector", lambda e: e.tensor_scalar(ST[:, 9:10], ST[:, 8:9],
                                               -1.0, None, ALU.mult),
           reads=[STB], writes=[STB])
        op("vector", lambda e: e.tensor_scalar(xnm, xtm, ST[:, 7:8], ST[:, 9:10],
                                               ALU.mult, ALU.add),
           reads=[xtm_b, STB], writes=[xnm_b])
        for r in range(NF):
            bank = r % 2
            op("tensor", lambda e, r=r, bank=bank: e.transpose(
                PB[bank][:, 0:128], xnm[:, r * 128:(r + 1) * 128], T["ident"][:]),
               reads=[xnm_b, B["ident"]], writes=[PBB[bank]])
            op("vector", lambda e, r=r, tt=tt, bank=bank: e.tensor_copy(
                XN[r][:, tt * 128:(tt + 1) * 128], PB[bank][:, 0:128]),
               reads=[PBB[bank]], writes=[XNB[r]])

    if debug >= 1:
        dram_out("xn_dbg", [NF, 128, SCTX])
        for r in range(NF):
            op("gpsimd", lambda e, r=r: e.dma_start(out=T["xn_dbg"][r], in_=XN[r][:]),
               reads=[XNB[r]], writes=bb("xn_dbg"), dma=True)

    kb.barrier()

    if do_outer:
        KP, KP_b = seg("KP", 0, 4096)
        VA, VA_b = seg("VA", 4096, 2080)
        VB_, VB_b = seg("VB", 6176, 2080)
        QP, QP_b = seg("QP", 8256, 2048)
        P0, P0_b = seg("P0", 10304, 512)
        P1, P1_b = seg("P1", 10816, 512)
        KD, KD_b = seg("KD", 11328, 1024)
        QD, QD_b = seg("QD", 12352, 1024)
        VD, VD_b = seg("VD", 13376, 1024)
        ZT, ZT_b = seg("ZT", 14400, 1024)
        P2, P2_b = seg("P2", 15424, 512)
        P3, P3_b = seg("P3", 15936, 512)
        for hp in range(pairs_outer):
            iA = 2 * hp
            for r in range(NF):
                for iL in range(2):
                    cc = (iA + iL) * 8 + r
                    op("vector",
                       lambda e, r=r, iL=iL, cc=cc: e.tensor_scalar(
                           KD[:, r * 128 + iL * 64:r * 128 + iL * 64 + 64],
                           T["diag2"][:, iL * 64:iL * 64 + 64],
                           T["colK"][:, cc:cc + 1], None, ALU.mult),
                       reads=bb("diag2", "colK"), writes=[KD_b])
                    op("vector",
                       lambda e, r=r, iL=iL, cc=cc: e.tensor_scalar(
                           VD[:, r * 128 + iL * 64:r * 128 + iL * 64 + 64],
                           T["diag2"][:, iL * 64:iL * 64 + 64],
                           T["colV"][:, cc:cc + 1], None, ALU.mult),
                       reads=bb("diag2", "colV"), writes=[VD_b])
            for iL in range(2):
                for half in (0, 1):
                    op("sync",
                       lambda e, iL=iL, ih=iA + iL, half=half: e.dma_start(
                           out=T["wpred2"][half * 64:half * 64 + 64,
                                           iL * 64:iL * 64 + 64],
                           in_=T["Wpred"][ih]),
                       reads=bb("Wpred"), writes=bb("wpred2"), dma=True)
            for r in range(NF):
                for iL in range(2):
                    cc = (iA + iL) * 8 + r
                    op("vector",
                       lambda e, r=r, iL=iL, cc=cc: e.tensor_scalar(
                           QD[:, r * 128 + iL * 64:r * 128 + iL * 64 + 64],
                           T["wpred2"][:, iL * 64:iL * 64 + 64],
                           T["colQ"][:, cc:cc + 1], None, ALU.mult),
                       reads=bb("wpred2", "colQ"), writes=[QD_b])
            for mt in range(8):
                bank = 6 + (mt % 2)
                for r in range(NF):
                    op("tensor",
                       lambda e, r=r, mt=mt, bank=bank: e.matmul(
                           PB[bank][:], KD[:, r * 128:(r + 1) * 128],
                           XN[r][:, mt * 512:(mt + 1) * 512],
                           start=(r == 0), stop=(r == NF - 1)),
                       reads=[KD_b, XNB[r]], writes=[PBB[bank]],
                       last=(r == NF - 1))
                op("vector",
                   lambda e, mt=mt, bank=bank: e.tensor_copy(
                       KP[:, mt * 512:(mt + 1) * 512], PB[bank][:]),
                   reads=[PBB[bank]], writes=[KP_b])
            op("vector", lambda e: e.memset(
                VA.rearrange("p (t c) -> p t c", c=65)[:, :, 64:65], 1.0),
               writes=[VA_b])
            op("vector", lambda e: e.memset(
                VB_.rearrange("p (t c) -> p t c", c=65)[:, :, 64:65], 1.0),
               writes=[VB_b])
            for tt in range(32):
                bank = 6 + (tt % 2)
                for r in range(NF):
                    op("tensor",
                       lambda e, r=r, tt=tt, bank=bank: e.matmul(
                           PB[bank][:, 0:128], XN[r][:, tt * 128:(tt + 1) * 128],
                           VD[:, r * 128:(r + 1) * 128],
                           start=(r == 0), stop=(r == NF - 1)),
                       reads=[VD_b, XNB[r]], writes=[PBB[bank]],
                       last=(r == NF - 1))
                op("vector",
                   lambda e, tt=tt, bank=bank: e.tensor_copy(
                       VA[:, tt * 65:tt * 65 + 64], PB[bank][:, 0:64]),
                   reads=[PBB[bank]], writes=[VA_b])
                op("vector",
                   lambda e, tt=tt, bank=bank: e.tensor_copy(
                       VB_[:, tt * 65:tt * 65 + 64], PB[bank][:, 64:128]),
                   reads=[PBB[bank]], writes=[VB_b])
            for mt in range(4):
                bank = 6 + (mt % 2)
                for r in range(NF):
                    op("tensor",
                       lambda e, r=r, mt=mt, bank=bank: e.matmul(
                           PB[bank][:], QD[:, r * 128:(r + 1) * 128],
                           XN[r][:, 2048 + mt * 512:2048 + (mt + 1) * 512],
                           start=(r == 0), stop=(r == NF - 1)),
                       reads=[QD_b, XNB[r]], writes=[PBB[bank]],
                       last=(r == NF - 1))
                op("vector",
                   lambda e, mt=mt, bank=bank: e.tensor_copy(
                       QP[:, mt * 512:(mt + 1) * 512], PB[bank][:]),
                   reads=[PBB[bank]], writes=[QP_b])
            Pts = ((P0, P0_b), (P1, P1_b), (P2, P2_b), (P3, P3_b))
            for j in range(4):
                nd = 20 + 4 * j

                def emit_qk(d, iL, j=j):
                    sbank = (d % 2) * 2 + iL
                    op("tensor",
                       lambda e, d=d, j=j, iL=iL, sbank=sbank: e.matmul(
                           PB[sbank][:],
                           KP[iL * 64:iL * 64 + 64, d * 128:(d + 1) * 128],
                           QP[iL * 64:iL * 64 + 64, j * 512:(j + 1) * 512],
                           start=True, stop=True,
                           tile_position=(iL * 64, 0)),
                       reads=[KP_b, QP_b], writes=[PBB[sbank]])

                def emit_exp(d, iL, j=j, nd=nd):
                    stair = d - (nd - 4)
                    sbank = (d % 2) * 2 + iL
                    Pt, Pt_b = Pts[(d % 2) * 2 + iL]
                    bias = T["pad_bias_t"][:] if d < 16 else T["zeros_col"][:]
                    bias_b = B["pad_bias_t"] if d < 16 else B["zeros_col"]
                    op("scalar",
                       lambda e, Pt=Pt, sbank=sbank, bias=bias: e.activation(
                           Pt, PB[sbank][:], AF.Exp, bias=bias, scale=0.125),
                       reads=[PBB[sbank], bias_b], writes=[Pt_b])
                    if stair >= 0:
                        op("gpsimd",
                           lambda e, Pt=Pt, stair=stair: e.affine_select(
                               Pt, Pt, pattern=[[1, 512]],
                               compare_op=ALU.is_ge, fill=0.0,
                               base=-128 * stair, channel_multiplier=-1),
                           reads=[Pt_b], writes=[Pt_b])

                def emit_pv(d, iL, nd=nd):
                    zb = 4 + iL
                    Pt, Pt_b = Pts[(d % 2) * 2 + iL]
                    VV, VVb = (VA, VA_b) if iL == 0 else (VB_, VB_b)
                    op("tensor",
                       lambda e, d=d, zb=zb, Pt=Pt, VV=VV, nd=nd: e.matmul(
                           PB[zb][0:65, :], VV[:, d * 65:d * 65 + 65], Pt,
                           start=(d == 0), stop=(d == nd - 1)),
                       reads=[VVb, Pt_b], writes=[PBB[zb]])

                for d in range(nd + 1):
                    if d < nd:
                        emit_qk(d, 0)
                        emit_qk(d, 1)
                    if d > 0:
                        emit_pv(d - 1, 0)
                        emit_pv(d - 1, 1)
                    if d < nd:
                        emit_exp(d, 0)
                        emit_exp(d, 1)
                for iL in range(2):
                    zb = 4 + iL
                    op("vector",
                       lambda e, zb=zb: e.tensor_copy(
                           T["denr"][64:65, 0:512], PB[zb][64:65, 0:512]),
                       reads=[PBB[zb]], writes=bb("denr"))
                    op("tensor",
                       lambda e: e.matmul(
                           PB[6][0:64, :], T["ones65"][64:65, 0:64],
                           T["denr"][64:65, 0:512], start=True, stop=True),
                       reads=bb("denr", "ones65"), writes=[PBB[6]])
                    op("vector",
                       lambda e, iL=iL: e.reciprocal(
                           ZT[0:64, iL * 512:iL * 512 + 512], PB[6][0:64, :]),
                       reads=[PBB[6]], writes=[ZT_b])
                    op("vector",
                       lambda e, zb=zb, iL=iL: e.tensor_mul(
                           ZT[64:128, iL * 512:iL * 512 + 512],
                           PB[zb][0:64, :],
                           ZT[0:64, iL * 512:iL * 512 + 512]),
                       reads=[ZT_b, PBB[zb]], writes=[ZT_b])
                    op("gpsimd",
                       lambda e, hp=hp, iL=iL, j=j: e.dma_start(
                           out=T["z_scr"][hp * 128 + iL * 64:
                                          hp * 128 + iL * 64 + 64,
                                          j * 512:(j + 1) * 512],
                           in_=ZT[64:128, iL * 512:iL * 512 + 512]),
                       reads=[ZT_b], writes=bb("z_scr"), dma=True)

        if debug >= 2:
            dram_out("z_dbg", [D, SOWN])
            op("gpsimd", lambda e: e.dma_start(out=T["z_dbg"], in_=T["z_scr"]),
               reads=bb("z_scr"), writes=bb("z_dbg"), dma=True)

    kb.barrier()

    if do_inner:
        tA, tA_b = seg("tA", 0, 1024)
        tBt, tBt_b = seg("tBt", 1024, 1024)
        tAi, tAi_b = seg("tAi", 2048, 1152)
        tBi, tBi_b = seg("tBi", 3200, 1152)
        QF, QF_b = seg("QF", 4352, 2048)
        KF, KF_b = seg("KF", 6400, 2304)
        VAi, VAi_b = seg("VAi", 8704, 1170)
        VBi, VBi_b = seg("VBi", 9874, 1170)
        WQK, WQK_b = seg("WQK", 11044, 3072)
        QKVT, QKVT_b = seg("QKVT", 14116, 384)
        RT1, RT1_b = seg("RT1", 14500, 256)
        RT2, RT2_b = seg("RT2", 14756, 256)
        ROT1 = AR[:, 14116:14116 + 256]
        ROT1_b = QKVT_b
        P0o = AR[:, 14116:14116 + 256]
        P0o_b = QKVT_b
        P1o = AR[:, 14500:14500 + 256]
        P1o_b = RT1_b
        P2o = AR[:, 14756:14756 + 256]
        P2o_b = RT2_b
        P3o = AR[:, 11044:11044 + 256]
        P3o_b = WQK_b
        HF, HF_b = seg("HF", 15012, NF * 257)
        SHQ, SHQ_b = seg("SHQ", 17068, 512)
        WOB, WOB_b = seg("WOB", 0, 8192)
        STG, STG_b = seg("STG", 8192, 4096)
        GG, GG_b = seg("GG", 0, NCC * 512)
        W1As = [seg("W1A0", 11264, 1024), seg("W1A1", 12288, 1024)]
        W1Gs = [seg("W1G0", 13312, 1024), seg("W1G1", 14336, 1024)]
        W2Cs = [seg("W2C0", 15360, 512), seg("W2C1", 15872, 512)]
        GEL, GEL_b = seg("GEL", 16384, 512)
        ABC, ABC_b = AR[:, 2048:2560], GG_b
        CBC, CBC_b = AR[:, 2560:3072], GG_b
        SHF, SHF_b = AR[:, 3072:3584], GG_b
        RWB, RWB_b = AR[:, 3584:4096], GG_b

        for l in range(depth):
            for tn, dst, dstb, ntl, skip in (
                    ("tabA", tA, tA_b, 16, 2), ("tabB", tBt, tBt_b, 16, 2),
                    ("tabAi", tAi, tAi_b, 18, 0), ("tabBi", tBi, tBi_b, 18, 0)):
                op("sync",
                   lambda e, tn=tn, dst=dst, ntl=ntl, skip=skip: e.dma_start(
                       out=dst.rearrange("p (t c) -> p t c", c=64),
                       in_=T[tn][skip * 128:, :]
                           .rearrange("(t p) c -> p t c", p=128)),
                   reads=bb(tn), writes=[dstb], dma=True)
            for src, c0 in (("qsc", 0), ("qsc", 64), ("ksc", 128), ("ksc", 192)):
                op("sync",
                   lambda e, src=src, c0=c0, l=l: e.dma_start(
                       out=T["qsks"][:, c0:c0 + 64],
                       in_=T[src][l].unsqueeze(0).broadcast_to([128, DH])),
                   reads=bb(src), writes=bb("qsks"), dma=True)
            op("sync", lambda e, l=l: e.dma_start(
                out=T["vecs"][:, 0:21],
                in_=T["b1"][l][0:21 * 128].rearrange("(c p) -> p c", p=128),
                allow_slow_non_contiguous=True),
               reads=bb("b1"), writes=bb("vecs"), dma=True)
            op("sync", lambda e, l=l: e.dma_start(
                out=T["vecs"][0:LASTW, 21:22],
                in_=T["b1"][l][21 * 128:DI].unsqueeze(1),
                allow_slow_non_contiguous=True),
               reads=bb("b1"), writes=bb("vecs"), dma=True)
            op("sync", lambda e, l=l: e.dma_start(
                out=T["vecs"][:, 22:43],
                in_=T["b1"][l][DI:DI + 21 * 128].rearrange("(c p) -> p c", p=128),
                allow_slow_non_contiguous=True),
               reads=bb("b1"), writes=bb("vecs"), dma=True)
            op("sync", lambda e, l=l: e.dma_start(
                out=T["vecs"][0:LASTW, 43:44],
                in_=T["b1"][l][DI + 21 * 128:2 * DI].unsqueeze(1),
                allow_slow_non_contiguous=True),
               reads=bb("b1"), writes=bb("vecs"), dma=True)
            op("sync", lambda e, l=l: e.dma_start(
                out=T["vecs"][:, 44:52],
                in_=T["b2"][l].rearrange("(c p) -> p c", p=128),
                allow_slow_non_contiguous=True),
               reads=bb("b2"), writes=bb("vecs"), dma=True)
            op("sync", lambda e, l=l: e.dma_start(
                out=T["vecs"][:, 52:60],
                in_=T["lng"][l].rearrange("(c p) -> p c", p=128),
                allow_slow_non_contiguous=True),
               reads=bb("lng"), writes=bb("vecs"), dma=True)
            op("sync", lambda e, l=l: e.dma_start(
                out=T["vecs"][:, 60:68],
                in_=T["lnb"][l].rearrange("(c p) -> p c", p=128),
                allow_slow_non_contiguous=True),
               reads=bb("lnb"), writes=bb("vecs"), dma=True)

            for r in range(NF):
                op("gpsimd", lambda e, r=r: e.dma_start(
                    out=T["cc1_in"][r], in_=XA(r)[:, 1791:2048]),
                   reads=[XNB[r]], writes=bb("cc1_in"), dma=True)
            op("gpsimd",
               lambda e: e.collective_compute(
                   "AllGather", ALU.bypass,
                   replica_groups=[[0, 1], [2, 3], [4, 5], [6, 7]],
                   ins=[T["cc1_in"]], outs=[T["cc1_out"]]),
               reads=bb("cc1_in"), writes=bb("cc1_out"))
            for r in range(NF):
                op("scalar", lambda e, r=r: e.dma_start(
                    out=SHQ[:, 0:257], in_=T["cc1_out"][0, r]),
                   reads=bb("cc1_out"), writes=[SHQ_b], dma=True)
                op("vector", lambda e, r=r: e.tensor_scalar(
                    HF[:, r * 257:(r + 1) * 257], SHQ[:, 0:257],
                    T["halo_gate_t"][:], None, ALU.mult),
                   reads=[SHQ_b, B["halo_gate_t"]], writes=[HF_b])

            for r in range(4, NF):
                c0 = (r - 4) * 128
                op("vector", lambda e, r=r, c0=c0: e.tensor_copy(
                    SHQ[:, c0:c0 + 1], HF[:, r * 257 + 256:r * 257 + 257]),
                   reads=[HF_b], writes=[SHQ_b])
                op("vector", lambda e, r=r, c0=c0: e.tensor_copy(
                    SHQ[:, c0 + 1:c0 + 128], XA(r)[:, 0:127]),
                   reads=[XNB[r]], writes=[SHQ_b])

            def shifted_attn(r, tile):
                if tile == 0:
                    o = r * 257 + (1 if r < 4 else 0)
                    return HF[:, o:o + 128], HF_b
                if tile == 1:
                    o = r * 257 + (129 if r < 4 else 128)
                    return HF[:, o:o + 128], HF_b
                t = tile - 2
                if r < 4:
                    return XA(r)[:, t * 128:(t + 1) * 128], XNB[r]
                if t == 0:
                    c0 = (r - 4) * 128
                    return SHQ[:, c0:c0 + 128], SHQ_b
                return XA(r)[:, t * 128 - 1:(t + 1) * 128 - 1], XNB[r]

            for hp in range(8):
                op("sync",
                   lambda e, l=l, hp=hp: e.dma_start(
                       out=WQK,
                       in_=T["Wqkv_opt"][l, hp].rearrange("p r c -> p (r c)")),
                   reads=bb("Wqkv_opt"), writes=[WQK_b], dma=True)
                op("vector", lambda e: e.memset(
                    VAi.rearrange("p (t c) -> p t c", c=65)[:, :, 64:65], 1.0),
                   writes=[VAi_b])
                op("vector", lambda e: e.memset(
                    VBi.rearrange("p (t c) -> p t c", c=65)[:, :, 64:65], 1.0),
                   writes=[VBi_b])
                for tt in range(18):
                    bank = 6 + (tt % 2)
                    for r in range(NF):
                        lhs, lhs_b = shifted_attn(r, tt)
                        op("tensor",
                           lambda e, lhs=lhs, r=r, bank=bank: e.matmul(
                               PB[bank][:, 0:384], lhs,
                               WQK[:, r * 384:(r + 1) * 384],
                               start=(r == 0), stop=(r == NF - 1)),
                           reads=[lhs_b, WQK_b], writes=[PBB[bank]],
                           last=(r == NF - 1))
                    op("vector",
                       lambda e, bank=bank: e.tensor_copy(QKVT, PB[bank][:, 0:384]),
                       reads=[PBB[bank]], writes=[QKVT_b])
                    op("vector",
                       lambda e, tt=tt: e.tensor_copy(
                           VAi[:, tt * 65:tt * 65 + 64], QKVT[:, 256:320]),
                       reads=[QKVT_b], writes=[VAi_b])
                    op("vector",
                       lambda e, tt=tt: e.tensor_copy(
                           VBi[:, tt * 65:tt * 65 + 64], QKVT[:, 320:384]),
                       reads=[QKVT_b], writes=[VBi_b])
                    for q4 in range(4):
                        op("scalar",
                           lambda e, q4=q4: e.activation(
                               RT2[:, q4 * 64:q4 * 64 + 64],
                               QKVT[:, q4 * 64:q4 * 64 + 64], AF.Square,
                               accum_out=ST[:, q4:q4 + 1]),
                           reads=[QKVT_b], writes=[RT2_b, STB])
                    op("vector",
                       lambda e: e.tensor_scalar(ST[:, 4:8], ST[:, 0:4],
                                                 1e-24, None, ALU.max),
                       reads=[STB], writes=[STB])
                    op("scalar", lambda e: e.activation(ST[:, 8:12], ST[:, 4:8],
                                                        AF.Ln),
                       reads=[STB], writes=[STB])
                    op("scalar", lambda e: e.activation(ST[:, 12:16], ST[:, 8:12],
                                                        AF.Exp, scale=-0.5),
                       reads=[STB], writes=[STB])
                    for q4 in range(4):
                        op("vector",
                           lambda e, q4=q4: e.tensor_scalar(
                               RT1[:, q4 * 64:q4 * 64 + 64],
                               QKVT[:, q4 * 64:q4 * 64 + 64],
                               ST[:, 12 + q4:13 + q4], None, ALU.mult),
                           reads=[QKVT_b, STB], writes=[RT1_b])
                    op("vector",
                       lambda e: e.tensor_mul(RT2, RT1, T["qsks"][:]),
                       reads=[RT1_b, B["qsks"]], writes=[RT2_b])
                    sides = ([("q", 0, tA, tA_b, tBt, tBt_b, tt - 2)]
                             if tt >= 2 else [])
                    sides.append(("k", 128, tAi, tAi_b, tBi, tBi_b, tt))
                    for nmx, o0, tac, tac_b, tbc, tbc_b, ti in sides:
                        op("vector",
                           lambda e, o0=o0, tac=tac, ti=ti: e.tensor_mul(
                               RT1[:, o0:o0 + 128]
                                   .rearrange("p (h c) -> p h c", h=2),
                               RT2[:, o0:o0 + 128]
                                   .rearrange("p (h c) -> p h c", h=2),
                               tac[:, ti * 64:(ti + 1) * 64]
                                   .unsqueeze(1).broadcast_to([128, 2, 64])),
                           reads=[RT2_b, tac_b], writes=[RT1_b])
                        op("vector",
                           lambda e, tbc=tbc, ti=ti, o0=o0: e.tensor_mul(
                               ROT1[:, 0:128]
                                   .rearrange("p (h c) -> p h c", h=2),
                               RT2[:, o0:o0 + 128]
                                   .rearrange("p (h c) -> p h c", h=2),
                               tbc[:, ti * 64:(ti + 1) * 64]
                                   .unsqueeze(1).broadcast_to([128, 2, 64])),
                           reads=[RT2_b, tbc_b], writes=[ROT1_b])
                        op("vector",
                           lambda e, o0=o0: e.tensor_add(
                               RT2[:, o0:o0 + 128]
                                   .rearrange("p (h s c) -> p h s c", h=2, s=2),
                               RT1[:, o0:o0 + 128]
                                   .rearrange("p (h s c) -> p h s c", h=2, s=2),
                               _swap_ap(ROT1)),
                           reads=[RT1_b, ROT1_b], writes=[RT2_b])
                    if tt >= 2:
                        op("tensor", lambda e: e.transpose(
                            PB[2][:, 0:128], RT2[:, 0:128], T["ident"][:]),
                           reads=[RT2_b, B["ident"]], writes=[PBB[2]])
                        op("vector",
                           lambda e, tt=tt: e.tensor_copy(
                               QF[:, (tt - 2) * 128:(tt - 1) * 128],
                               PB[2][:, 0:128]),
                           reads=[PBB[2]], writes=[QF_b])
                    op("tensor", lambda e: e.transpose(
                        PB[3][:, 0:128], RT2[:, 128:256], T["ident"][:]),
                       reads=[RT2_b, B["ident"]], writes=[PBB[3]])
                    op("vector",
                       lambda e, tt=tt: e.tensor_copy(
                           KF[:, tt * 128:(tt + 1) * 128], PB[3][:, 0:128]),
                       reads=[PBB[3]], writes=[KF_b])
                Ptsi = ((P0o, P0o_b), (P1o, P1o_b), (P2o, P2o_b), (P3o, P3o_b))
                for w in range(8):
                    def emit_qk(st, iL, w=w):
                        kcol = w * 256 + st * 128
                        sbank = (st % 2) * 2 + iL
                        op("tensor",
                           lambda e, iL=iL, kcol=kcol, w=w, sbank=sbank:
                           e.matmul(
                               PB[sbank][:, 0:256],
                               KF[iL * 64:iL * 64 + 64, kcol:kcol + 128],
                               QF[iL * 64:iL * 64 + 64,
                                  w * 256:(w + 1) * 256],
                               start=True, stop=True,
                               tile_position=(iL * 64, 0)),
                           reads=[KF_b, QF_b], writes=[PBB[sbank]])

                    def emit_exp(st, iL, w=w):
                        sbank = (st % 2) * 2 + iL
                        Pt, Pt_b = Ptsi[(st % 2) * 2 + iL]
                        bias = (T["win0_bias_t"][:] if (w == 0 and st < 2)
                                else T["zeros_col"][:])
                        bias_b = (B["win0_bias_t"] if (w == 0 and st < 2)
                                  else B["zeros_col"])
                        op("scalar",
                           lambda e, Pt=Pt, sbank=sbank, bias=bias:
                           e.activation(Pt, PB[sbank][:, 0:256], AF.Exp,
                                        bias=bias, scale=8.0),
                           reads=[PBB[sbank], bias_b], writes=[Pt_b])
                        if st < 2:
                            args = dict(pattern=[[-1, 256]],
                                        base=128 * (st % 2),
                                        channel_multiplier=1)
                        else:
                            args = dict(pattern=[[1, 256]],
                                        base=-128 * (st % 2),
                                        channel_multiplier=-1)
                        op("gpsimd",
                           lambda e, Pt=Pt, args=args:
                           e.affine_select(
                               Pt, Pt, compare_op=ALU.is_ge, fill=0.0,
                               **args),
                           reads=[Pt_b], writes=[Pt_b])

                    def emit_pv(st, iL, w=w):
                        kt = 2 * w + st
                        zb = 4 + iL
                        Pt, Pt_b = Ptsi[(st % 2) * 2 + iL]
                        VV, VVb = (VAi, VAi_b) if iL == 0 else (VBi, VBi_b)
                        op("tensor",
                           lambda e, kt=kt, zb=zb, Pt=Pt, VV=VV, st=st:
                           e.matmul(
                               PB[zb][0:65, 0:256],
                               VV[:, kt * 65:kt * 65 + 65], Pt,
                               start=(st == 0), stop=(st == 3)),
                           reads=[VVb, Pt_b], writes=[PBB[zb]])

                    for st in range(5):
                        if st < 4:
                            emit_qk(st, 0)
                            emit_qk(st, 1)
                        if st > 0:
                            emit_pv(st - 1, 0)
                            emit_pv(st - 1, 1)
                        if st < 4:
                            emit_exp(st, 0)
                            emit_exp(st, 1)
                    for iL in range(2):
                        zb = 4 + iL
                        op("vector",
                           lambda e, zb=zb: e.tensor_copy(
                               T["denr"][64:65, 0:256], PB[zb][64:65, 0:256]),
                           reads=[PBB[zb]], writes=bb("denr"))
                        op("tensor",
                           lambda e: e.matmul(
                               PB[6][0:64, 0:256], T["ones65"][64:65, 0:64],
                               T["denr"][64:65, 0:256], start=True, stop=True),
                           reads=bb("denr", "ones65"), writes=[PBB[6]])
                        op("vector",
                           lambda e: e.reciprocal(
                               P1o[0:64, 0:256], PB[6][0:64, 0:256]),
                           reads=[PBB[6]], writes=[P1o_b])
                        op("vector",
                           lambda e, hp=hp, iL=iL, w=w, zb=zb: e.tensor_mul(
                               XB(hp)[iL * 64:iL * 64 + 64,
                                      w * 256:(w + 1) * 256],
                               PB[zb][0:64, 0:256], P1o[0:64, 0:256]),
                           reads=[P1o_b, PBB[zb]], writes=[XNB[hp]])

            kb.barrier()

            for o in range(NF):
                op("sync",
                   lambda e, l=l, o=o: e.dma_start(
                       out=WOB[:, o * 1024:(o + 1) * 1024],
                       in_=T["Wo_opt"][l, o].rearrange("p r c -> p (r c)")),
                   reads=bb("Wo_opt"), writes=[WOB_b], dma=True)
            for mt in range(4):
                for o in range(NF):
                    bank = o % 4
                    for r in range(NF):
                        op("tensor",
                           lambda e, o=o, r=r, mt=mt, bank=bank: e.matmul(
                               PB[bank][:],
                               WOB[:, o * 1024 + r * 128:o * 1024 + (r + 1) * 128],
                               XB(r)[:, mt * 512:(mt + 1) * 512],
                               start=(r == 0), stop=(r == NF - 1)),
                           reads=[WOB_b, XNB[r]], writes=[PBB[bank]],
                           last=(r == NF - 1))
                    op("vector",
                       lambda e, o=o, mt=mt, bank=bank: e.scalar_tensor_tensor(
                           STG[:, o * 512:(o + 1) * 512], PB[bank][:], 0.0,
                           XA(o)[:, mt * 512:(mt + 1) * 512],
                           ALU.add, ALU.add),
                       reads=[PBB[bank], XNB[o]], writes=[STG_b])
                for o in range(NF):
                    op("vector",
                       lambda e, o=o, mt=mt: e.tensor_copy(
                           XB(o)[:, mt * 512:(mt + 1) * 512],
                           STG[:, o * 512:(o + 1) * 512]),
                       reads=[STG_b], writes=[XNB[o]])

            for r in range(NF):
                op("gpsimd", lambda e, r=r: e.dma_start(
                    out=T["cc2_in"][r], in_=XB(r)[:, 2047:2048]),
                   reads=[XNB[r]], writes=bb("cc2_in"), dma=True)
            op("gpsimd",
               lambda e: e.collective_compute(
                   "AllGather", ALU.bypass,
                   replica_groups=[[0, 1], [2, 3], [4, 5], [6, 7]],
                   ins=[T["cc2_in"]], outs=[T["cc2_out"]]),
               reads=bb("cc2_in"), writes=bb("cc2_out"))
            for r in range(NF):
                op("scalar", lambda e, r=r: e.dma_start(
                    out=T["tmpt"][:, 0:1], in_=T["cc2_out"][0, r]),
                   reads=bb("cc2_out"), writes=bb("tmpt"), dma=True)
                op("vector", lambda e, r=r: e.tensor_scalar(
                    T["xpc"][:, r:r + 1], T["tmpt"][:, 0:1],
                    T["halo_gate_t"][:], None, ALU.mult),
                   reads=[B["tmpt"], B["halo_gate_t"]], writes=bb("xpc"))

            kb.barrier()

            for r in range(4, NF):
                c0 = (r - 4) * 128
                op("vector", lambda e, r=r, c0=c0: e.tensor_copy(
                    SHF[:, c0:c0 + 1], T["xpc"][:, r:r + 1]),
                   reads=bb("xpc"), writes=[SHF_b])
                op("vector", lambda e, r=r, c0=c0: e.tensor_copy(
                    SHF[:, c0 + 1:c0 + 128], XB(r)[:, 0:127]),
                   reads=[XNB[r]], writes=[SHF_b])

            def shifted_ffn(r, mt, lo, hi):
                if r < 4:
                    return XB(r)[:, mt * 512 + lo:mt * 512 + hi], XNB[r]
                base = mt * 512
                if base + lo == 0:
                    c0 = (r - 4) * 128
                    return SHF[:, c0 + lo:c0 + hi], SHF_b
                return XB(r)[:, base + lo - 1:base + hi - 1], XNB[r]

            for mt in range(4):
                for r in range(NF):
                    pieces = ([(0, 128), (128, 512)] if (mt == 0 and r >= 4)
                              else [(0, 512)])
                    for pi, (lo, hi) in enumerate(pieces):
                        first = (r == 0 and pi == 0)
                        last_ = (r == NF - 1 and pi == len(pieces) - 1)
                        src, src_b = shifted_ffn(r, mt, lo, hi)
                        op("tensor",
                           lambda e, src=src, lo=lo, hi=hi, first=first,
                           last_=last_: e.matmul(
                               PB[6][0:1, lo:hi], T["ones_colf"][:], src,
                               start=first, stop=last_),
                           reads=[src_b, B["ones_colf"]], writes=[PBB[6]])
                        sqo = (r % 2) * 512
                        op("scalar",
                           lambda e, src=src, sqo=sqo, lo=lo, hi=hi: e.activation(
                               GG[:, sqo + lo:sqo + hi], src, AF.Square),
                           reads=[src_b], writes=[GG_b])
                        op("tensor",
                           lambda e, sqo=sqo, lo=lo, hi=hi, first=first,
                           last_=last_: e.matmul(
                               PB[7][0:1, lo:hi], T["ones_colf"][:],
                               GG[:, sqo + lo:sqo + hi],
                               start=first, stop=last_),
                           reads=[GG_b, B["ones_colf"]], writes=[PBB[7]])
                op("vector", lambda e: e.tensor_scalar(
                    T["rowsS"][0:1, :], PB[6][0:1, :], 1.0 / D, None, ALU.mult),
                   reads=[PBB[6]], writes=bb("rowsS"))
                op("vector", lambda e: e.tensor_scalar(
                    RWB[0:1, 0:512], PB[7][0:1, :], 1.0 / D, None, ALU.mult),
                   reads=[PBB[7]], writes=[RWB_b])
                op("vector", lambda e: e.tensor_mul(
                    PB[4][0:1, :], T["rowsS"][0:1, :], T["rowsS"][0:1, :]),
                   reads=bb("rowsS"), writes=[PBB[4]])
                op("vector", lambda e: e.tensor_sub(
                    PB[5][0:1, :], RWB[0:1, 0:512], PB[4][0:1, :]),
                   reads=[RWB_b, PBB[4]], writes=[PBB[5]])
                op("scalar", lambda e: e.activation(
                    PB[4][0:1, :], PB[5][0:1, :], AF.Ln,
                    bias=T["epsc"][0:1, :], scale=1.0),
                   reads=[PBB[5], B["epsc"]], writes=[PBB[4]])
                op("scalar", lambda e: e.activation(
                    RWB[0:1, 0:512], PB[4][0:1, :], AF.Exp, scale=-0.5),
                   reads=[PBB[4]], writes=[RWB_b])
                op("vector", lambda e: e.tensor_mul(
                    PB[5][0:1, :], T["rowsS"][0:1, :], RWB[0:1, 0:512]),
                   reads=[B["rowsS"], RWB_b], writes=[PBB[5]])
                op("vector", lambda e: e.tensor_copy(
                    T["rowsS"][0:1, :], PB[5][0:1, :]),
                   reads=[PBB[5]], writes=bb("rowsS"))
                op("tensor", lambda e: e.matmul(
                    PB[6][:, 0:512], T["ones_row"][:], RWB[0:1, 0:512],
                    start=True, stop=True),
                   reads=[RWB_b, B["ones_row"]], writes=[PBB[6]])
                op("tensor", lambda e: e.matmul(
                    PB[7][:, 0:512], T["ones_row"][:], T["rowsS"][0:1, :],
                    start=True, stop=True),
                   reads=[B["rowsS"], B["ones_row"]], writes=[PBB[7]])
                op("vector", lambda e: e.tensor_copy(ABC, PB[6][:]),
                   reads=[PBB[6]], writes=[ABC_b])
                op("vector", lambda e: e.tensor_copy(CBC, PB[7][:]),
                   reads=[PBB[7]], writes=[CBC_b])
                for r in range(NF):
                    pieces = ([(0, 128), (128, 512)] if (mt == 0 and r >= 4)
                              else [(0, 512)])
                    for lo, hi in pieces:
                        src, src_b = shifted_ffn(r, mt, lo, hi)
                        op("vector",
                           lambda e, src=src, lo=lo, hi=hi: e.tensor_mul(
                               GG[:, 1024 + lo:1024 + hi], src, ABC[:, lo:hi]),
                           reads=[src_b, ABC_b], writes=[GG_b])
                    op("vector", lambda e: e.tensor_sub(
                        GG[:, 1536:2048], GG[:, 1024:1536], CBC),
                       reads=[GG_b, CBC_b], writes=[GG_b])
                    op("vector",
                       lambda e, r=r, mt=mt: e.tensor_scalar(
                           XA(r)[:, mt * 512:(mt + 1) * 512],
                           GG[:, 1536:2048],
                           T["vecs"][:, 52 + r:53 + r], T["vecs"][:, 60 + r:61 + r],
                           ALU.mult, ALU.add),
                       reads=[GG_b, B["vecs"]], writes=[XNB[r]])

            for mt in range(4):
                for c in range(NCC):
                    wlast = 128 if c < NCC - 1 else LASTW
                    W1A, W1A_b = W1As[c % 2]
                    W1G, W1G_b = W1Gs[c % 2]
                    W2C, W2C_b = W2Cs[c % 2]
                    op("sync", lambda e, l=l, c=c, wlast=wlast, W1A=W1A:
                       e.dma_start(
                           out=W1A[:, 0:8 * wlast]
                               .rearrange("p (r c) -> p r c", c=wlast),
                           in_=T["W1_opt"][l, 0, c][:, :, 0:wlast]),
                       reads=bb("W1_opt"), writes=[W1A_b], dma=True)
                    op("sync", lambda e, l=l, c=c, wlast=wlast, W1G=W1G:
                       e.dma_start(
                           out=W1G[:, 0:8 * wlast]
                               .rearrange("p (r c) -> p r c", c=wlast),
                           in_=T["W1_opt"][l, 1, c][:, :, 0:wlast]),
                       reads=bb("W1_opt"), writes=[W1G_b], dma=True)
                    op("sync", lambda e, l=l, c=c, wlast=wlast, W2C=W2C:
                       e.dma_start(
                           out=W2C[0:wlast, :],
                           in_=T["W2"][l][c * 128:c * 128 + wlast, 0:512]),
                       reads=bb("W2"), writes=[W2C_b], dma=True)
                    for r in range(NF):
                        op("tensor",
                           lambda e, r=r, mt=mt, wlast=wlast, W1A=W1A: e.matmul(
                               PB[6][0:wlast, :],
                               W1A[:, r * wlast:(r + 1) * wlast],
                               XA(r)[:, mt * 512:(mt + 1) * 512],
                               start=(r == 0), stop=(r == NF - 1)),
                           reads=[W1A_b, XNB[r]], writes=[PBB[6]],
                           last=(r == NF - 1))
                    for r in range(NF):
                        op("tensor",
                           lambda e, r=r, mt=mt, wlast=wlast, W1G=W1G: e.matmul(
                               PB[7][0:wlast, :],
                               W1G[:, r * wlast:(r + 1) * wlast],
                               XA(r)[:, mt * 512:(mt + 1) * 512],
                               start=(r == 0), stop=(r == NF - 1)),
                           reads=[W1G_b, XNB[r]], writes=[PBB[7]],
                           last=(r == NF - 1))
                    op("scalar",
                       lambda e, c=c, wlast=wlast: e.activation(
                           GEL[0:wlast, :], PB[7][0:wlast, :], AF.Gelu,
                           bias=T["vecs"][0:wlast, 22 + c:23 + c]),
                       reads=[PBB[7], B["vecs"]], writes=[GEL_b])
                    op("vector",
                       lambda e, c=c, wlast=wlast: e.scalar_tensor_tensor(
                           GG[0:wlast, c * 512:(c + 1) * 512],
                           PB[6][0:wlast, :],
                           T["vecs"][0:wlast, c:c + 1], GEL[0:wlast, :],
                           ALU.add, ALU.mult),
                       reads=[PBB[6], B["vecs"], GEL_b], writes=[GG_b])
                    for o in range(4):
                        op("tensor",
                           lambda e, c=c, o=o, wlast=wlast, W2C=W2C: e.matmul(
                               PB[o][:],
                               W2C[0:wlast, o * 128:(o + 1) * 128],
                               GG[0:wlast, c * 512:(c + 1) * 512],
                               start=(c == 0), stop=(c == NCC - 1)),
                           reads=[W2C_b, GG_b], writes=[PBB[o]])
                for o in range(4):
                    op("vector",
                       lambda e, o=o, mt=mt: e.scalar_tensor_tensor(
                           XA(o)[:, mt * 512:(mt + 1) * 512], PB[o][:],
                           T["vecs"][:, 44 + o:45 + o],
                           XB(o)[:, mt * 512:(mt + 1) * 512],
                           ALU.add, ALU.add),
                       reads=[PBB[o], B["vecs"], XNB[o]], writes=[XNB[o]])
                for c in range(NCC):
                    wlast = 128 if c < NCC - 1 else LASTW
                    W2C, W2C_b = W2Cs[c % 2]
                    op("sync", lambda e, l=l, c=c, wlast=wlast, W2C=W2C:
                       e.dma_start(
                           out=W2C[0:wlast, 0:512],
                           in_=T["W2"][l][c * 128:c * 128 + wlast, 512:1024]),
                       reads=bb("W2"), writes=[W2C_b], dma=True)
                    for o in range(4):
                        op("tensor",
                           lambda e, c=c, o=o, wlast=wlast, W2C=W2C: e.matmul(
                               PB[4 + o][:],
                               W2C[0:wlast, o * 128:(o + 1) * 128],
                               GG[0:wlast, c * 512:(c + 1) * 512],
                               start=(c == 0), stop=(c == NCC - 1)),
                           reads=[W2C_b, GG_b], writes=[PBB[4 + o]])
                for o in range(4):
                    op("vector",
                       lambda e, o=o, mt=mt: e.scalar_tensor_tensor(
                           XA(4 + o)[:, mt * 512:(mt + 1) * 512], PB[4 + o][:],
                           T["vecs"][:, 48 + o:49 + o],
                           XB(4 + o)[:, mt * 512:(mt + 1) * 512],
                           ALU.add, ALU.add),
                       reads=[PBB[4 + o], B["vecs"], XNB[4 + o]],
                       writes=[XNB[4 + o]])

            kb.barrier()

            if debug >= 3:
                dram_out("xl%d_dbg" % l, [NF, 128, SOWN])
                for r in range(NF):
                    op("gpsimd", lambda e, r=r, l=l: e.dma_start(
                        out=T["xl%d_dbg" % l][r], in_=XA(r)),
                       reads=[XNB[r]], writes=bb("xl%d_dbg" % l), dma=True)
                kb.barrier()

    if do_final:
        OTM, OTM_b = seg("OTM", 0, 1024)
        XOW, XOW_b = seg("XOW", 1024, 1024)
        for r in range(NF):
            op("sync", lambda e, r=r: e.dma_start(
                out=XB(r), in_=T["z_scr"][r * 128:(r + 1) * 128, :]),
               reads=bb("z_scr"), writes=[XNB[r]], dma=True)
        for tt in range(16):
            op("sync", lambda e, tt=tt: e.dma_start(
                out=XOW, in_=T["x_ctx"][2048 + tt * 128:2048 + (tt + 1) * 128, :]),
               reads=bb("x_ctx"), writes=[XOW_b], dma=True)
            for r in range(NF):
                tb = T["tmpt"] if r % 2 == 0 else T["tmpt2"]
                tbb = B["tmpt"] if r % 2 == 0 else B["tmpt2"]
                bank = 2 + (r % 2)
                op("vector", lambda e, r=r, tt=tt, tb=tb: e.tensor_add(
                    tb[:], XA(r)[:, tt * 128:(tt + 1) * 128],
                    XB(r)[:, tt * 128:(tt + 1) * 128]),
                   reads=[XNB[r]], writes=[tbb])
                op("tensor", lambda e, tb=tb, bank=bank: e.transpose(
                    PB[bank][:, 0:128], tb[:], T["ident"][:]),
                   reads=[tbb, B["ident"]], writes=[PBB[bank]])
                op("vector", lambda e, r=r, bank=bank: e.scalar_tensor_tensor(
                    OTM[:, r * 128:(r + 1) * 128], PB[bank][:, 0:128], 0.0,
                    XOW[:, r * 128:(r + 1) * 128], ALU.add, ALU.add),
                   reads=[PBB[bank], XOW_b], writes=[OTM_b])
            op("sync", lambda e, tt=tt: e.dma_start(
                out=T["out_own"][tt * 128:(tt + 1) * 128, :], in_=OTM),
               reads=[OTM_b], writes=bb("out_own"), dma=True)
    kb.barrier()

    block = stack.enter_context(nc.Block())
    kb.finalize(block)
    return nc, stack, T


def _swap_ap(rot1):
    ap = rot1[:, 0:128]
    return bass.AP(ap.tensor, ap.offset + 32,
                   [list(ap.ap[0]), [64, 2], [-32, 2], [1, 32]])


def make_tables(h):
    half = np.arange(0, DH, 2, dtype=np.float64)
    inv_freq = 1.0 / 10000.0 ** (half / DH)
    base_scale = (half + 0.4 * DH) / (1.4 * DH)
    start = h * 2048 - 256
    t = np.arange(start, start + 2304, dtype=np.float64)
    tc = np.clip(t, 0, None)
    freqs = tc[:, None] * inv_freq[None, :]
    freqs = np.concatenate([freqs, freqs], -1)
    power = (tc - SCTX // 2) / (WIN // 2)
    scale = base_scale[None, :] ** power[:, None]
    scale = np.concatenate([scale, scale], -1)
    cos, sin = np.cos(freqs), np.sin(freqs)
    A = (cos * scale).astype(np.float32)
    S = (sin * scale).astype(np.float32)
    tabB = np.concatenate([S[:, :32], -S[:, 32:]], -1)
    Ai = (cos / scale).astype(np.float32)
    Si = (sin / scale).astype(np.float32)
    tabBi = np.concatenate([Si[:, :32], -Si[:, 32:]], -1)
    return A, tabB, Ai, tabBi


_CACHED = {}


def _get_nc():
    if "nc" not in _CACHED:
        nc, stack, Tm = build_nc()
        _CACHED["nc"] = nc
    return _CACHED["nc"]


def kernel(L0_attn_x, L0_attn_W_K, L0_attn_W_Q, L0_attn_W_V, L0_attn_W_pred,
           L0_ffn_Wqkv, L0_ffn_q_scale, L0_ffn_k_scale, L0_ffn_Wo,
           L0_ffn_ln_g, L0_ffn_ln_b, L0_ffn_W1, L0_ffn_b1, L0_ffn_W2,
           L0_ffn_b2):
    x = np.ascontiguousarray(np.asarray(L0_attn_x, dtype=np.float32))
    Bsz = x.shape[0]
    com = dict(
        Wqkv=np.ascontiguousarray(L0_ffn_Wqkv, dtype=np.float32),
        Wo=np.ascontiguousarray(L0_ffn_Wo, dtype=np.float32),
        W1=np.ascontiguousarray(L0_ffn_W1, dtype=np.float32),
        W2=np.ascontiguousarray(L0_ffn_W2, dtype=np.float32),
        b1=np.ascontiguousarray(L0_ffn_b1, dtype=np.float32),
        b2=np.ascontiguousarray(L0_ffn_b2, dtype=np.float32),
        lng=np.ascontiguousarray(L0_ffn_ln_g, dtype=np.float32),
        lnb=np.ascontiguousarray(L0_ffn_ln_b, dtype=np.float32),
        qsc=np.ascontiguousarray(L0_ffn_q_scale, dtype=np.float32),
        ksc=np.ascontiguousarray(L0_ffn_k_scale, dtype=np.float32),
        WK=np.ascontiguousarray(L0_attn_W_K, dtype=np.float32),
        WQ=np.ascontiguousarray(L0_attn_W_Q, dtype=np.float32),
        WV=np.ascontiguousarray(L0_attn_W_V, dtype=np.float32),
        Wpred=np.ascontiguousarray(L0_attn_W_pred, dtype=np.float32),
    )
    tabs = {h: make_tables(h) for h in (0, 1)}
    in_maps = []
    for c in range(8):
        b, h = c // 2, c % 2
        if h == 1:
            x_ctx = x[b]
        else:
            x_ctx = np.concatenate(
                [np.zeros((2048, D), dtype=np.float32), x[b, :2048]], axis=0)
        A, tb, Ai, tbi = tabs[h]
        m = dict(com)
        m.update(
            x_ctx=np.ascontiguousarray(x_ctx),
            tabA=A, tabB=tb, tabAi=Ai, tabBi=tbi,
            pad_bias=np.full((128, 1), NEG if h == 0 else 0.0, dtype=np.float32),
            win0_bias=np.full((128, 1), NEG if h == 0 else 0.0, dtype=np.float32),
            halo_gate=np.full((128, 1), 0.0 if h == 0 else 1.0, dtype=np.float32),
        )
        in_maps.append(m)
    nc = _get_nc()
    res = run_bass_kernel_spmd(nc, in_maps, core_ids=list(range(8)))
    out = np.empty((Bsz, SCTX, D), dtype=np.float32)
    for c in range(8):
        b, h = c // 2, c % 2
        out[b, h * 2048:(h + 1) * 2048] = res.results[c]["out_own"]
    return out


# version 05
# speedup vs baseline: 1.3488x; 1.0376x over previous
import numpy as np
from contextlib import ExitStack

import concourse.bass as bass
import concourse.mybir as mybir
from concourse.bass_utils import run_bass_kernel_spmd

F32 = mybir.dt.float32
AF = mybir.ActivationFunctionType
ALU = mybir.AluOpType
AX = mybir.AxisListType

D = 1024
H = 16
DH = 64
SOWN = 2048
SCTX = 4096
NDEPTH = 6
DI = 2730
WIN = 256
NEG = -1.0e9
NF = 8
NCC = (DI + 127) // 128
LASTW = DI - 128 * (NCC - 1)
ENGINES = ("sync", "tensor", "vector", "scalar", "gpsimd")


class Sem:
    __slots__ = ("h", "n", "name", "is_q")

    def __init__(self, h, name):
        self.h = h
        self.n = 0
        self.name = name
        self.is_q = False


class Buf:
    __slots__ = ("w", "r", "name")

    def __init__(self, name=""):
        self.w = {}
        self.r = {}
        self.name = name


def _merge(dst, src):
    for s, c in src.items():
        if dst.get(s, -1) < c:
            dst[s] = c


class KB:
    def __init__(self, nc, stack):
        self.nc = nc
        self.stack = stack
        self.ops = {e: [] for e in ENGINES}
        self.waited = {e: {} for e in ENGINES}
        self.sems = []
        self.dsem = {e: self.sem("zd" + e[:2]) for e in ENGINES}
        self.qsem = {e: [self.sem("zq" + e[:2] + str(i)) for i in range(8)]
                     for e in ("sync", "scalar", "gpsimd", "vector")}
        for pool in self.qsem.values():
            for s_ in pool:
                s_.is_q = True
        self.qrr = {e: 0 for e in self.qsem}

    def sem(self, name):
        h = self.stack.enter_context(self.nc.semaphore(name))
        s = Sem(h, name)
        self.sems.append(s)
        return s

    def op(self, eng, fn, reads=(), writes=(), dma=False, last=True, extra=()):
        if dma:
            pool = self.qsem[eng]
            s = pool[self.qrr[eng]]
            self.qrr[eng] = (self.qrr[eng] + 1) % len(pool)
            k = 16
            extra = list(extra) + [(s, s.n)]
        else:
            s = self.dsem[eng]
            k = 1
        waits = {}
        for b in reads:
            _merge(waits, b.w)
        for b in writes:
            _merge(waits, b.w)
            _merge(waits, b.r)
        for sm, c in extra:
            if waits.get(sm, -1) < c:
                waits[sm] = c
        wd = self.waited[eng]
        for sm in list(waits):
            if sm.is_q:
                waits[sm] = sm.n
        ws = [(sm, c) for sm, c in waits.items() if wd.get(sm, 0) < c]
        for sm, c in ws:
            wd[sm] = c
        if last:
            s.n += k
            cover = s.n
        else:
            cover = s.n + k
        self.ops[eng].append((ws, fn, s if last else None, k))
        for b in writes:
            if last:
                b.w = {s: cover}
                b.r = {}
        for b in reads:
            if b.r.get(s, -1) < cover:
                b.r[s] = cover
        return (s, cover)

    def barrier(self):
        bar = self.sem("zb%d" % len(self.sems))
        for e in ENGINES:
            ws = [(self.dsem[e], self.dsem[e].n)] if self.dsem[e].n else []
            for q in self.qsem.get(e, ()):
                if q.n:
                    ws.append((q, q.n))
            self.ops[e].append((ws, (lambda eng: eng.nop()), bar, 1))
            bar.n += 1
        tgt = bar.n
        snap = {s: s.n for s in self.sems if s.n > 0}
        for e in ENGINES:
            self.ops[e].append(([(bar, tgt)], None, None, 0))
            w = dict(snap)
            w[bar] = tgt
            self.waited[e] = w
        return bar

    def finalize(self, block):
        for e in ENGINES:
            deco = getattr(block, e)
            ops = self.ops[e]

            def make(ops=ops):
                def run(engine):
                    for ws, fn, s, k in ops:
                        for sm, c in ws:
                            engine.wait_ge(sm.h, c)
                        if fn is not None:
                            instr = fn(engine)
                            if s is not None:
                                instr.then_inc(s.h, k)
                return run
            deco(make())


def build_nc(debug=0, depth=NDEPTH, do_outer=True, do_inner=True, do_final=True,
             pairs_outer=8):
    nc = bass.Bass("TRN2", target_bir_lowering=False)
    stack = ExitStack()
    kb = KB(nc, stack)
    T = {}
    B = {}

    def dram_in(name, shape):
        T[name] = nc.dram_tensor(name, shape, F32, kind="ExternalInput").ap()
        B[name] = Buf(name)

    def dram_out(name, shape):
        T[name] = nc.dram_tensor(name, shape, F32, kind="ExternalOutput").ap()
        B[name] = Buf(name)

    def dram_tmp(name, shape):
        T[name] = nc.dram_tensor(name, shape, F32, kind="Internal").ap()
        B[name] = Buf(name)

    def sb(name, shape, dtype=F32):
        T[name] = nc.alloc_sbuf_tensor(name, shape, dtype).ap()
        B[name] = Buf(name)

    def psum(name, shape):
        T[name] = nc.alloc_psum_tensor(name, shape, F32).ap()
        B[name] = Buf(name)

    dram_in("x_ctx", [SCTX, D])
    dram_in("Wqkv", [6, D, 3 * D])
    dram_in("Wo", [6, D, D])
    dram_in("W1", [6, D, 2 * DI])
    dram_in("W2", [6, DI, D])
    dram_in("b1", [6, 2 * DI])
    dram_in("b2", [6, D])
    dram_in("lng", [6, D])
    dram_in("lnb", [6, D])
    dram_in("qsc", [6, DH])
    dram_in("ksc", [6, DH])
    dram_in("WK", [H, H])
    dram_in("WQ", [H, H])
    dram_in("WV", [H, H])
    dram_in("Wpred", [H, DH, DH])
    dram_in("tabA", [2304, DH])
    dram_in("tabB", [2304, DH])
    dram_in("tabAi", [2304, DH])
    dram_in("tabBi", [2304, DH])
    dram_in("pad_bias", [128, 1])
    dram_in("win0_bias", [128, 1])
    dram_in("halo_gate", [128, 1])
    dram_out("out_own", [SOWN, D])

    dram_tmp("z_scr", [D, SOWN])
    dram_tmp("Wqkv_opt", [6, 8, 128, 8, 384])
    dram_tmp("Wo_opt", [6, 8, 128, 8, 128])
    dram_tmp("W1_opt", [6, 2, NCC, 128, 8, 128])
    dram_tmp("probs_hbm", [3, H, H])
    dram_tmp("rows_hbm", [2, 512])
    dram_tmp("cc1_in", [NF, 128, 257])
    dram_tmp("cc1_out", [2, NF, 128, 257])
    dram_tmp("cc2_in", [NF, 128, 1])
    dram_tmp("cc2_out", [2, NF, 128, 1])

    for r in range(NF):
        sb("XN%d" % r, [128, SCTX])
    sb("ident", [128, 128])
    sb("diag2", [128, 128])
    sb("ones64", [64, 64])
    sb("ones65", [65, 64])
    sb("ones_row", [1, 128])
    sb("ones_colf", [128, 1])
    sb("zeros_col", [128, 1])
    sb("probs", [16, 48])
    sb("prob_s", [16, 8])
    sb("prob_r", [16, 8])
    sb("colK", [128, 128])
    sb("colQ", [128, 128])
    sb("colV", [128, 128])
    sb("pad_bias_t", [128, 1])
    sb("epsc", [128, 1])
    sb("win0_bias_t", [128, 1])
    sb("halo_gate_t", [128, 1])
    sb("wpred2", [128, 128])
    sb("stat", [128, 64])
    sb("denr", [65, 512])
    sb("vecs", [128, 72])
    sb("qsks", [128, 256])
    sb("rowsS", [1, 512])
    sb("tmpt", [128, 128])
    sb("tmpt2", [128, 128])
    sb("xpc", [128, 8])
    sb("ARENA", [128, 17616])
    for i in range(8):
        psum("pb%d" % i, [128, 512])

    AR = T["ARENA"]
    seg_tab = {}

    def seg(name, off, cols):
        if name not in seg_tab:
            seg_tab[name] = Buf("sg_" + name)
        return AR[:, off:off + cols], seg_tab[name]

    def op(eng, fn, reads=(), writes=(), **kw):
        return kb.op(eng, fn, reads=reads, writes=writes, **kw)

    def bb(*names):
        return [B[n] for n in names]

    XN = [T["XN%d" % r] for r in range(NF)]
    XNB = [B["XN%d" % r] for r in range(NF)]
    PB = [T["pb%d" % i] for i in range(8)]
    PBB = [B["pb%d" % i] for i in range(8)]
    ST = T["stat"]
    STB = B["stat"]

    def XA(r):
        return XN[r][:, 2048:4096]

    def XB(r):
        return XN[r][:, 0:2048]

    op("vector", lambda e: e.memset(T["ones64"][:], 1.0), writes=bb("ones64"))
    op("vector", lambda e: e.memset(T["ones65"][:], 1.0), writes=bb("ones65"))
    op("vector", lambda e: e.memset(T["ones_row"][:], 1.0), writes=bb("ones_row"))
    op("vector", lambda e: e.memset(T["ones_colf"][:], 1.0), writes=bb("ones_colf"))
    op("vector", lambda e: e.memset(T["zeros_col"][:], 0.0), writes=bb("zeros_col"))
    op("vector", lambda e: e.memset(T["epsc"][:], 1e-5), writes=bb("epsc"))
    op("vector", lambda e: e.memset(T["ident"][:], 0.0), writes=bb("ident"))
    op("vector", lambda e: e.memset(T["diag2"][:], 0.0), writes=bb("diag2"))
    for ph in (0, 64):
        op("gpsimd",
           lambda e, ph=ph: e.affine_select(
               T["ident"][ph:ph + 64, ph:ph + 64], T["ones64"][:],
               pattern=[[-1, 64]], compare_op=ALU.is_equal, fill=0.0,
               base=0, channel_multiplier=1),
           reads=bb("ones64"), writes=bb("ident"))
        for ch in (0, 64):
            op("gpsimd",
               lambda e, ph=ph, ch=ch: e.affine_select(
                   T["diag2"][ph:ph + 64, ch:ch + 64], T["ones64"][:],
                   pattern=[[-1, 64]], compare_op=ALU.is_equal, fill=0.0,
                   base=0, channel_multiplier=1),
               reads=bb("ones64"), writes=bb("diag2"))
    for nm in ("pad_bias", "win0_bias", "halo_gate"):
        op("sync", lambda e, nm=nm: e.dma_start(out=T[nm + "_t"][:], in_=T[nm]),
           reads=bb(nm), writes=bb(nm + "_t"), dma=True)
    for idx, wn in enumerate(("WK", "WQ", "WV")):
        op("sync",
           lambda e, wn=wn, c0=idx * 16: e.dma_start(
               out=T["probs"][0:16, c0:c0 + 16], in_=T[wn]),
           reads=bb(wn), writes=bb("probs"), dma=True)
    for idx in range(3):
        op("scalar",
           lambda e, c0=idx * 16, idx=idx: e.activation(
               T["stat"][0:16, c0:c0 + 16], T["probs"][0:16, c0:c0 + 16],
               AF.Exp, accum_out=T["prob_s"][0:16, idx:idx + 1]),
           reads=bb("probs"), writes=bb("stat", "prob_s"))
    op("vector", lambda e: e.reciprocal(T["prob_r"][0:16, 0:3], T["prob_s"][0:16, 0:3]),
       reads=bb("prob_s"), writes=bb("prob_r"))
    for idx in range(3):
        op("vector",
           lambda e, c0=idx * 16, idx=idx: e.tensor_scalar(
               T["probs"][0:16, c0:c0 + 16], T["stat"][0:16, c0:c0 + 16],
               T["prob_r"][0:16, idx:idx + 1], None, ALU.mult),
           reads=bb("stat", "prob_r"), writes=bb("probs"))
    for idx in range(3):
        op("scalar",
           lambda e, idx=idx, c0=idx * 16: e.dma_start(
               out=T["probs_hbm"][idx], in_=T["probs"][0:16, c0:c0 + 16]),
           reads=bb("probs"), writes=bb("probs_hbm"), dma=True)
    for idx, cn in enumerate(("colK", "colQ", "colV")):
        for half in (0, 1):
            op("sync",
               lambda e, idx=idx, cn=cn, half=half: e.dma_start(
                   out=T[cn][half * 64:half * 64 + 64, 0:128]
                       .rearrange("p (i r) -> p i r", i=16),
                   in_=T["probs_hbm"][idx][:, half::2]
                       .unsqueeze(0).broadcast_to([64, 16, 8]),
                   allow_slow_non_contiguous=True),
               reads=bb("probs_hbm"), writes=bb(cn), dma=True)

    for l in range(NDEPTH):
        for hp in range(8):
            for rr in range(NF):
                op("scalar",
                   lambda e, l=l, hp=hp, rr=rr: e.dma_start(
                       out=T["Wqkv_opt"][l, hp][:, rr]
                           .rearrange("p (sg c) -> p sg c", sg=3),
                       in_=T["Wqkv"][l]
                           .rearrange("(r p) (sg c) -> r p sg c", p=128, sg=3)
                           [rr, :, :, hp * 128:(hp + 1) * 128]),
                   reads=bb("Wqkv"), writes=bb("Wqkv_opt"), dma=True)
        for o in range(NF):
            op("scalar",
               lambda e, l=l, o=o: e.dma_start(
                   out=T["Wo_opt"][l, o],
                   in_=T["Wo"][l]
                       .rearrange("(r p) (o c) -> o p r c", p=128, o=8)[o]),
               reads=bb("Wo"), writes=bb("Wo_opt"), dma=True)
        for ag in range(2):
            for c in range(NCC):
                wl = 128 if c < NCC - 1 else LASTW
                op("gpsimd",
                   lambda e, l=l, ag=ag, c=c, wl=wl: e.dma_start(
                       out=T["W1_opt"][l, ag, c][:, :, 0:wl],
                       in_=T["W1"][l][:, ag * DI + c * 128:ag * DI + c * 128 + wl]
                           .rearrange("(r p) cc -> p r cc", p=128),
                       allow_slow_non_contiguous=True),
                   reads=bb("W1"), writes=bb("W1_opt"), dma=True)

    xtm, xtm_b = seg("xtm", 0, 1024)
    xnm, xnm_b = seg("xnm", 1024, 1024)
    SQ1, SQ1_b = seg("SQ1", 2048, 1024)
    for tt in range(SCTX // 128):
        op("sync", lambda e, tt=tt: e.dma_start(
            out=xtm, in_=T["x_ctx"][tt * 128:(tt + 1) * 128, :]),
           reads=bb("x_ctx"), writes=[xtm_b], dma=True)
        op("scalar", lambda e: e.activation(SQ1, xtm, AF.Square,
                                            accum_out=ST[:, 1:2]),
           reads=[xtm_b], writes=[SQ1_b, STB])
        op("vector", lambda e: e.reduce_sum(ST[:, 0:1], xtm, axis=AX.X),
           reads=[xtm_b], writes=[STB])
        op("vector", lambda e: e.tensor_scalar(ST[:, 2:3], ST[:, 0:1],
                                               1.0 / D, None, ALU.mult),
           reads=[STB], writes=[STB])
        op("vector", lambda e: e.tensor_mul(ST[:, 3:4], ST[:, 2:3], ST[:, 2:3]),
           reads=[STB], writes=[STB])
        op("vector", lambda e: e.tensor_scalar(ST[:, 4:5], ST[:, 1:2],
                                               1.0 / D, None, ALU.mult),
           reads=[STB], writes=[STB])
        op("vector", lambda e: e.tensor_sub(ST[:, 5:6], ST[:, 4:5], ST[:, 3:4]),
           reads=[STB], writes=[STB])
        op("scalar", lambda e: e.activation(ST[:, 6:7], ST[:, 5:6], AF.Ln,
                                            bias=T["epsc"][:], scale=1.0),
           reads=[STB, B["epsc"]], writes=[STB])
        op("scalar", lambda e: e.activation(ST[:, 7:8], ST[:, 6:7], AF.Exp,
                                            scale=-0.5),
           reads=[STB], writes=[STB])
        op("vector", lambda e: e.tensor_mul(ST[:, 8:9], ST[:, 2:3], ST[:, 7:8]),
           reads=[STB], writes=[STB])
        op("vector", lambda e: e.tensor_scalar(ST[:, 9:10], ST[:, 8:9],
                                               -1.0, None, ALU.mult),
           reads=[STB], writes=[STB])
        op("vector", lambda e: e.tensor_scalar(xnm, xtm, ST[:, 7:8], ST[:, 9:10],
                                               ALU.mult, ALU.add),
           reads=[xtm_b, STB], writes=[xnm_b])
        for r in range(NF):
            bank = r % 2
            op("tensor", lambda e, r=r, bank=bank: e.transpose(
                PB[bank][:, 0:128], xnm[:, r * 128:(r + 1) * 128], T["ident"][:]),
               reads=[xnm_b, B["ident"]], writes=[PBB[bank]])
            op("vector", lambda e, r=r, tt=tt, bank=bank: e.tensor_copy(
                XN[r][:, tt * 128:(tt + 1) * 128], PB[bank][:, 0:128]),
               reads=[PBB[bank]], writes=[XNB[r]])

    if debug >= 1:
        dram_out("xn_dbg", [NF, 128, SCTX])
        for r in range(NF):
            op("gpsimd", lambda e, r=r: e.dma_start(out=T["xn_dbg"][r], in_=XN[r][:]),
               reads=[XNB[r]], writes=bb("xn_dbg"), dma=True)

    kb.barrier()

    if do_outer:
        KP, KP_b = seg("KP", 0, 4096)
        VA, VA_b = seg("VA", 4096, 2080)
        VB_, VB_b = seg("VB", 6176, 2080)
        QP, QP_b = seg("QP", 8256, 2048)
        P0, P0_b = seg("P0", 10304, 512)
        P1, P1_b = seg("P1", 10816, 512)
        KD, KD_b = seg("KD", 11328, 1024)
        QD, QD_b = seg("QD", 12352, 1024)
        VD, VD_b = seg("VD", 13376, 1024)
        ZT, ZT_b = seg("ZT", 14400, 1024)
        P2, P2_b = seg("P2", 15424, 512)
        P3, P3_b = seg("P3", 15936, 512)
        for hp in range(pairs_outer):
            iA = 2 * hp
            for r in range(NF):
                for iL in range(2):
                    cc = (iA + iL) * 8 + r
                    op("vector",
                       lambda e, r=r, iL=iL, cc=cc: e.tensor_scalar(
                           KD[:, r * 128 + iL * 64:r * 128 + iL * 64 + 64],
                           T["diag2"][:, iL * 64:iL * 64 + 64],
                           T["colK"][:, cc:cc + 1], None, ALU.mult),
                       reads=bb("diag2", "colK"), writes=[KD_b])
                    op("vector",
                       lambda e, r=r, iL=iL, cc=cc: e.tensor_scalar(
                           VD[:, r * 128 + iL * 64:r * 128 + iL * 64 + 64],
                           T["diag2"][:, iL * 64:iL * 64 + 64],
                           T["colV"][:, cc:cc + 1], None, ALU.mult),
                       reads=bb("diag2", "colV"), writes=[VD_b])
            for iL in range(2):
                for half in (0, 1):
                    op("sync",
                       lambda e, iL=iL, ih=iA + iL, half=half: e.dma_start(
                           out=T["wpred2"][half * 64:half * 64 + 64,
                                           iL * 64:iL * 64 + 64],
                           in_=T["Wpred"][ih]),
                       reads=bb("Wpred"), writes=bb("wpred2"), dma=True)
            for r in range(NF):
                for iL in range(2):
                    cc = (iA + iL) * 8 + r
                    op("vector",
                       lambda e, r=r, iL=iL, cc=cc: e.tensor_scalar(
                           QD[:, r * 128 + iL * 64:r * 128 + iL * 64 + 64],
                           T["wpred2"][:, iL * 64:iL * 64 + 64],
                           T["colQ"][:, cc:cc + 1], None, ALU.mult),
                       reads=bb("wpred2", "colQ"), writes=[QD_b])
            for mt in range(8):
                bank = 6 + (mt % 2)
                for r in range(NF):
                    op("tensor",
                       lambda e, r=r, mt=mt, bank=bank: e.matmul(
                           PB[bank][:], KD[:, r * 128:(r + 1) * 128],
                           XN[r][:, mt * 512:(mt + 1) * 512],
                           start=(r == 0), stop=(r == NF - 1)),
                       reads=[KD_b, XNB[r]], writes=[PBB[bank]],
                       last=(r == NF - 1))
                op("vector",
                   lambda e, mt=mt, bank=bank: e.tensor_copy(
                       KP[:, mt * 512:(mt + 1) * 512], PB[bank][:]),
                   reads=[PBB[bank]], writes=[KP_b])
            op("vector", lambda e: e.memset(
                VA.rearrange("p (t c) -> p t c", c=65)[:, :, 64:65], 1.0),
               writes=[VA_b])
            op("vector", lambda e: e.memset(
                VB_.rearrange("p (t c) -> p t c", c=65)[:, :, 64:65], 1.0),
               writes=[VB_b])
            for tt in range(32):
                bank = 6 + (tt % 2)
                for r in range(NF):
                    op("tensor",
                       lambda e, r=r, tt=tt, bank=bank: e.matmul(
                           PB[bank][:, 0:128], XN[r][:, tt * 128:(tt + 1) * 128],
                           VD[:, r * 128:(r + 1) * 128],
                           start=(r == 0), stop=(r == NF - 1)),
                       reads=[VD_b, XNB[r]], writes=[PBB[bank]],
                       last=(r == NF - 1))
                op("vector",
                   lambda e, tt=tt, bank=bank: e.tensor_copy(
                       VA[:, tt * 65:tt * 65 + 64], PB[bank][:, 0:64]),
                   reads=[PBB[bank]], writes=[VA_b])
                op("vector",
                   lambda e, tt=tt, bank=bank: e.tensor_copy(
                       VB_[:, tt * 65:tt * 65 + 64], PB[bank][:, 64:128]),
                   reads=[PBB[bank]], writes=[VB_b])
            for mt in range(4):
                bank = 6 + (mt % 2)
                for r in range(NF):
                    op("tensor",
                       lambda e, r=r, mt=mt, bank=bank: e.matmul(
                           PB[bank][:], QD[:, r * 128:(r + 1) * 128],
                           XN[r][:, 2048 + mt * 512:2048 + (mt + 1) * 512],
                           start=(r == 0), stop=(r == NF - 1)),
                       reads=[QD_b, XNB[r]], writes=[PBB[bank]],
                       last=(r == NF - 1))
                op("vector",
                   lambda e, mt=mt, bank=bank: e.tensor_copy(
                       QP[:, mt * 512:(mt + 1) * 512], PB[bank][:]),
                   reads=[PBB[bank]], writes=[QP_b])
            Pts = ((P0, P0_b), (P1, P1_b), (P2, P2_b), (P3, P3_b))
            for j in range(4):
                nd = 20 + 4 * j

                def emit_qk(d, iL, j=j):
                    sbank = (d % 2) * 2 + iL
                    op("tensor",
                       lambda e, d=d, j=j, iL=iL, sbank=sbank: e.matmul(
                           PB[sbank][:],
                           KP[iL * 64:iL * 64 + 64, d * 128:(d + 1) * 128],
                           QP[iL * 64:iL * 64 + 64, j * 512:(j + 1) * 512],
                           start=True, stop=True,
                           tile_position=(iL * 64, 0)),
                       reads=[KP_b, QP_b], writes=[PBB[sbank]])

                def emit_exp(d, iL, j=j, nd=nd):
                    stair = d - (nd - 4)
                    sbank = (d % 2) * 2 + iL
                    Pt, Pt_b = Pts[(d % 2) * 2 + iL]
                    bias = T["pad_bias_t"][:] if d < 16 else T["zeros_col"][:]
                    bias_b = B["pad_bias_t"] if d < 16 else B["zeros_col"]
                    op("scalar",
                       lambda e, Pt=Pt, sbank=sbank, bias=bias: e.activation(
                           Pt, PB[sbank][:], AF.Exp, bias=bias, scale=0.125),
                       reads=[PBB[sbank], bias_b], writes=[Pt_b])
                    if stair >= 0:
                        op("gpsimd",
                           lambda e, Pt=Pt, stair=stair: e.affine_select(
                               Pt, Pt, pattern=[[1, 512]],
                               compare_op=ALU.is_ge, fill=0.0,
                               base=-128 * stair, channel_multiplier=-1),
                           reads=[Pt_b], writes=[Pt_b])

                def emit_pv(d, iL, nd=nd, j=j):
                    zb = 4 + (j % 2) * 2 + iL
                    Pt, Pt_b = Pts[(d % 2) * 2 + iL]
                    VV, VVb = (VA, VA_b) if iL == 0 else (VB_, VB_b)
                    op("tensor",
                       lambda e, d=d, zb=zb, Pt=Pt, VV=VV, nd=nd: e.matmul(
                           PB[zb][0:65, :], VV[:, d * 65:d * 65 + 65], Pt,
                           start=(d == 0), stop=(d == nd - 1)),
                       reads=[VVb, Pt_b], writes=[PBB[zb]])

                for d in range(nd + 1):
                    if d < nd:
                        emit_qk(d, 0)
                        emit_qk(d, 1)
                    if d > 0:
                        emit_pv(d - 1, 0)
                        emit_pv(d - 1, 1)
                    if d < nd:
                        emit_exp(d, 0)
                        emit_exp(d, 1)
                for iL in range(2):
                    zb = 4 + (j % 2) * 2 + iL
                    op("vector",
                       lambda e, zb=zb: e.tensor_copy(
                           T["denr"][64:65, 0:512], PB[zb][64:65, 0:512]),
                       reads=[PBB[zb]], writes=bb("denr"))
                    op("tensor",
                       lambda e, zb=zb: e.matmul(
                           PB[zb][64:128, :], T["ones65"][64:65, 0:64],
                           T["denr"][64:65, 0:512], start=True, stop=True,
                           tile_position=(64, 64)),
                       reads=bb("denr", "ones65"), writes=[PBB[zb]])
                    op("vector",
                       lambda e, iL=iL, zb=zb: e.reciprocal(
                           ZT[0:64, iL * 512:iL * 512 + 512],
                           PB[zb][64:128, :]),
                       reads=[PBB[zb]], writes=[ZT_b])
                    op("vector",
                       lambda e, zb=zb, iL=iL: e.tensor_mul(
                           ZT[64:128, iL * 512:iL * 512 + 512],
                           PB[zb][0:64, :],
                           ZT[0:64, iL * 512:iL * 512 + 512]),
                       reads=[ZT_b, PBB[zb]], writes=[ZT_b])
                    op("gpsimd",
                       lambda e, hp=hp, iL=iL, j=j: e.dma_start(
                           out=T["z_scr"][hp * 128 + iL * 64:
                                          hp * 128 + iL * 64 + 64,
                                          j * 512:(j + 1) * 512],
                           in_=ZT[64:128, iL * 512:iL * 512 + 512]),
                       reads=[ZT_b], writes=bb("z_scr"), dma=True)

        if debug >= 2:
            dram_out("z_dbg", [D, SOWN])
            op("gpsimd", lambda e: e.dma_start(out=T["z_dbg"], in_=T["z_scr"]),
               reads=bb("z_scr"), writes=bb("z_dbg"), dma=True)

    kb.barrier()

    if do_inner:
        tA, tA_b = seg("tA", 0, 1024)
        tBt, tBt_b = seg("tBt", 1024, 1024)
        tAi, tAi_b = seg("tAi", 2048, 1152)
        tBi, tBi_b = seg("tBi", 3200, 1152)
        QF, QF_b = seg("QF", 4352, 2048)
        KF, KF_b = seg("KF", 6400, 2304)
        VAi, VAi_b = seg("VAi", 8704, 1170)
        VBi, VBi_b = seg("VBi", 9874, 1170)
        WQK, WQK_b = seg("WQK", 11044, 3072)
        QKVT, QKVT_b = seg("QKVT", 14116, 384)
        RT1, RT1_b = seg("RT1", 14500, 256)
        RT2, RT2_b = seg("RT2", 14756, 256)
        ROT1 = AR[:, 14116:14116 + 256]
        ROT1_b = QKVT_b
        P0o = AR[:, 14116:14116 + 256]
        P0o_b = QKVT_b
        P1o = AR[:, 14500:14500 + 256]
        P1o_b = RT1_b
        P2o = AR[:, 14756:14756 + 256]
        P2o_b = RT2_b
        P3o = AR[:, 11044:11044 + 256]
        P3o_b = WQK_b
        HF, HF_b = seg("HF", 15012, NF * 257)
        SHQ, SHQ_b = seg("SHQ", 17068, 512)
        WOB, WOB_b = seg("WOB", 0, 8192)
        STG, STG_b = seg("STG", 8192, 4096)
        GG, GG_b = seg("GG", 0, NCC * 512)
        GGB = [Buf("gg%d" % c) for c in range(NCC)]
        W1As = [seg("W1A0", 11264, 1024), seg("W1A1", 12288, 1024)]
        W1Gs = [seg("W1G0", 13312, 1024), seg("W1G1", 14336, 1024)]
        W2Cs = [seg("W2C0", 15360, 512), seg("W2C1", 15872, 512)]
        GEL, GEL_b = seg("GEL", 16384, 512)
        ABC, ABC_b = AR[:, 2048:2560], GGB[4]
        CBC, CBC_b = AR[:, 2560:3072], GGB[5]
        SHF, SHF_b = AR[:, 3072:3584], GGB[6]
        RWB, RWB_b = AR[:, 3584:4096], GGB[7]

        for l in range(depth):
            for tn, dst, dstb, ntl, skip in (
                    ("tabA", tA, tA_b, 16, 2), ("tabB", tBt, tBt_b, 16, 2),
                    ("tabAi", tAi, tAi_b, 18, 0), ("tabBi", tBi, tBi_b, 18, 0)):
                op("sync",
                   lambda e, tn=tn, dst=dst, ntl=ntl, skip=skip: e.dma_start(
                       out=dst.rearrange("p (t c) -> p t c", c=64),
                       in_=T[tn][skip * 128:, :]
                           .rearrange("(t p) c -> p t c", p=128)),
                   reads=bb(tn), writes=[dstb], dma=True)
            for src, c0 in (("qsc", 0), ("qsc", 64), ("ksc", 128), ("ksc", 192)):
                op("sync",
                   lambda e, src=src, c0=c0, l=l: e.dma_start(
                       out=T["qsks"][:, c0:c0 + 64],
                       in_=T[src][l].unsqueeze(0).broadcast_to([128, DH])),
                   reads=bb(src), writes=bb("qsks"), dma=True)
            op("sync", lambda e, l=l: e.dma_start(
                out=T["vecs"][:, 0:21],
                in_=T["b1"][l][0:21 * 128].rearrange("(c p) -> p c", p=128),
                allow_slow_non_contiguous=True),
               reads=bb("b1"), writes=bb("vecs"), dma=True)
            op("sync", lambda e, l=l: e.dma_start(
                out=T["vecs"][0:LASTW, 21:22],
                in_=T["b1"][l][21 * 128:DI].unsqueeze(1),
                allow_slow_non_contiguous=True),
               reads=bb("b1"), writes=bb("vecs"), dma=True)
            op("sync", lambda e, l=l: e.dma_start(
                out=T["vecs"][:, 22:43],
                in_=T["b1"][l][DI:DI + 21 * 128].rearrange("(c p) -> p c", p=128),
                allow_slow_non_contiguous=True),
               reads=bb("b1"), writes=bb("vecs"), dma=True)
            op("sync", lambda e, l=l: e.dma_start(
                out=T["vecs"][0:LASTW, 43:44],
                in_=T["b1"][l][DI + 21 * 128:2 * DI].unsqueeze(1),
                allow_slow_non_contiguous=True),
               reads=bb("b1"), writes=bb("vecs"), dma=True)
            op("sync", lambda e, l=l: e.dma_start(
                out=T["vecs"][:, 44:52],
                in_=T["b2"][l].rearrange("(c p) -> p c", p=128),
                allow_slow_non_contiguous=True),
               reads=bb("b2"), writes=bb("vecs"), dma=True)
            op("sync", lambda e, l=l: e.dma_start(
                out=T["vecs"][:, 52:60],
                in_=T["lng"][l].rearrange("(c p) -> p c", p=128),
                allow_slow_non_contiguous=True),
               reads=bb("lng"), writes=bb("vecs"), dma=True)
            op("sync", lambda e, l=l: e.dma_start(
                out=T["vecs"][:, 60:68],
                in_=T["lnb"][l].rearrange("(c p) -> p c", p=128),
                allow_slow_non_contiguous=True),
               reads=bb("lnb"), writes=bb("vecs"), dma=True)

            for r in range(NF):
                op("gpsimd", lambda e, r=r: e.dma_start(
                    out=T["cc1_in"][r], in_=XA(r)[:, 1791:2048]),
                   reads=[XNB[r]], writes=bb("cc1_in"), dma=True)
            op("gpsimd",
               lambda e: e.collective_compute(
                   "AllGather", ALU.bypass,
                   replica_groups=[[0, 1], [2, 3], [4, 5], [6, 7]],
                   ins=[T["cc1_in"]], outs=[T["cc1_out"]]),
               reads=bb("cc1_in"), writes=bb("cc1_out"))
            for r in range(NF):
                op("scalar", lambda e, r=r: e.dma_start(
                    out=SHQ[:, 0:257], in_=T["cc1_out"][0, r]),
                   reads=bb("cc1_out"), writes=[SHQ_b], dma=True)
                op("vector", lambda e, r=r: e.tensor_scalar(
                    HF[:, r * 257:(r + 1) * 257], SHQ[:, 0:257],
                    T["halo_gate_t"][:], None, ALU.mult),
                   reads=[SHQ_b, B["halo_gate_t"]], writes=[HF_b])

            for r in range(4, NF):
                c0 = (r - 4) * 128
                op("vector", lambda e, r=r, c0=c0: e.tensor_copy(
                    SHQ[:, c0:c0 + 1], HF[:, r * 257 + 256:r * 257 + 257]),
                   reads=[HF_b], writes=[SHQ_b])
                op("vector", lambda e, r=r, c0=c0: e.tensor_copy(
                    SHQ[:, c0 + 1:c0 + 128], XA(r)[:, 0:127]),
                   reads=[XNB[r]], writes=[SHQ_b])

            def shifted_attn(r, tile):
                if tile == 0:
                    o = r * 257 + (1 if r < 4 else 0)
                    return HF[:, o:o + 128], HF_b
                if tile == 1:
                    o = r * 257 + (129 if r < 4 else 128)
                    return HF[:, o:o + 128], HF_b
                t = tile - 2
                if r < 4:
                    return XA(r)[:, t * 128:(t + 1) * 128], XNB[r]
                if t == 0:
                    c0 = (r - 4) * 128
                    return SHQ[:, c0:c0 + 128], SHQ_b
                return XA(r)[:, t * 128 - 1:(t + 1) * 128 - 1], XNB[r]

            for hp in range(8):
                op("sync",
                   lambda e, l=l, hp=hp: e.dma_start(
                       out=WQK,
                       in_=T["Wqkv_opt"][l, hp].rearrange("p r c -> p (r c)")),
                   reads=bb("Wqkv_opt"), writes=[WQK_b], dma=True)
                op("vector", lambda e: e.memset(
                    VAi.rearrange("p (t c) -> p t c", c=65)[:, :, 64:65], 1.0),
                   writes=[VAi_b])
                op("vector", lambda e: e.memset(
                    VBi.rearrange("p (t c) -> p t c", c=65)[:, :, 64:65], 1.0),
                   writes=[VBi_b])
                for tt in range(18):
                    bank = 6 + (tt % 2)
                    for r in range(NF):
                        lhs, lhs_b = shifted_attn(r, tt)
                        op("tensor",
                           lambda e, lhs=lhs, r=r, bank=bank: e.matmul(
                               PB[bank][:, 0:384], lhs,
                               WQK[:, r * 384:(r + 1) * 384],
                               start=(r == 0), stop=(r == NF - 1)),
                           reads=[lhs_b, WQK_b], writes=[PBB[bank]],
                           last=(r == NF - 1))
                    op("vector",
                       lambda e, bank=bank: e.tensor_copy(QKVT, PB[bank][:, 0:384]),
                       reads=[PBB[bank]], writes=[QKVT_b])
                    op("vector",
                       lambda e, tt=tt: e.tensor_copy(
                           VAi[:, tt * 65:tt * 65 + 64], QKVT[:, 256:320]),
                       reads=[QKVT_b], writes=[VAi_b])
                    op("vector",
                       lambda e, tt=tt: e.tensor_copy(
                           VBi[:, tt * 65:tt * 65 + 64], QKVT[:, 320:384]),
                       reads=[QKVT_b], writes=[VBi_b])
                    for q4 in range(4):
                        op("scalar",
                           lambda e, q4=q4: e.activation(
                               RT2[:, q4 * 64:q4 * 64 + 64],
                               QKVT[:, q4 * 64:q4 * 64 + 64], AF.Square,
                               accum_out=ST[:, q4:q4 + 1]),
                           reads=[QKVT_b], writes=[RT2_b, STB])
                    op("vector",
                       lambda e: e.tensor_scalar(ST[:, 4:8], ST[:, 0:4],
                                                 1e-24, None, ALU.max),
                       reads=[STB], writes=[STB])
                    op("scalar", lambda e: e.activation(ST[:, 8:12], ST[:, 4:8],
                                                        AF.Ln),
                       reads=[STB], writes=[STB])
                    op("scalar", lambda e: e.activation(ST[:, 12:16], ST[:, 8:12],
                                                        AF.Exp, scale=-0.5),
                       reads=[STB], writes=[STB])
                    for q4 in range(4):
                        op("vector",
                           lambda e, q4=q4: e.tensor_scalar(
                               RT1[:, q4 * 64:q4 * 64 + 64],
                               QKVT[:, q4 * 64:q4 * 64 + 64],
                               ST[:, 12 + q4:13 + q4], None, ALU.mult),
                           reads=[QKVT_b, STB], writes=[RT1_b])
                    op("vector",
                       lambda e: e.tensor_mul(RT2, RT1, T["qsks"][:]),
                       reads=[RT1_b, B["qsks"]], writes=[RT2_b])
                    sides = ([("q", 0, tA, tA_b, tBt, tBt_b, tt - 2)]
                             if tt >= 2 else [])
                    sides.append(("k", 128, tAi, tAi_b, tBi, tBi_b, tt))
                    for nmx, o0, tac, tac_b, tbc, tbc_b, ti in sides:
                        op("vector",
                           lambda e, o0=o0, tac=tac, ti=ti: e.tensor_mul(
                               RT1[:, o0:o0 + 128]
                                   .rearrange("p (h c) -> p h c", h=2),
                               RT2[:, o0:o0 + 128]
                                   .rearrange("p (h c) -> p h c", h=2),
                               tac[:, ti * 64:(ti + 1) * 64]
                                   .unsqueeze(1).broadcast_to([128, 2, 64])),
                           reads=[RT2_b, tac_b], writes=[RT1_b])
                        op("vector",
                           lambda e, tbc=tbc, ti=ti, o0=o0: e.tensor_mul(
                               ROT1[:, 0:128]
                                   .rearrange("p (h c) -> p h c", h=2),
                               RT2[:, o0:o0 + 128]
                                   .rearrange("p (h c) -> p h c", h=2),
                               tbc[:, ti * 64:(ti + 1) * 64]
                                   .unsqueeze(1).broadcast_to([128, 2, 64])),
                           reads=[RT2_b, tbc_b], writes=[ROT1_b])
                        op("vector",
                           lambda e, o0=o0: e.tensor_add(
                               RT2[:, o0:o0 + 128]
                                   .rearrange("p (h s c) -> p h s c", h=2, s=2),
                               RT1[:, o0:o0 + 128]
                                   .rearrange("p (h s c) -> p h s c", h=2, s=2),
                               _swap_ap(ROT1)),
                           reads=[RT1_b, ROT1_b], writes=[RT2_b])
                    if tt >= 2:
                        op("tensor", lambda e: e.transpose(
                            PB[2][:, 0:128], RT2[:, 0:128], T["ident"][:]),
                           reads=[RT2_b, B["ident"]], writes=[PBB[2]])
                        op("vector",
                           lambda e, tt=tt: e.tensor_copy(
                               QF[:, (tt - 2) * 128:(tt - 1) * 128],
                               PB[2][:, 0:128]),
                           reads=[PBB[2]], writes=[QF_b])
                    op("tensor", lambda e: e.transpose(
                        PB[3][:, 0:128], RT2[:, 128:256], T["ident"][:]),
                       reads=[RT2_b, B["ident"]], writes=[PBB[3]])
                    op("vector",
                       lambda e, tt=tt: e.tensor_copy(
                           KF[:, tt * 128:(tt + 1) * 128], PB[3][:, 0:128]),
                       reads=[PBB[3]], writes=[KF_b])
                Ptsi = ((P0o, P0o_b), (P1o, P1o_b), (P2o, P2o_b), (P3o, P3o_b))
                for w in range(8):
                    def emit_qk(st, iL, w=w):
                        kcol = w * 256 + st * 128
                        sbank = (st % 2) * 2 + iL
                        op("tensor",
                           lambda e, iL=iL, kcol=kcol, w=w, sbank=sbank:
                           e.matmul(
                               PB[sbank][:, 0:256],
                               KF[iL * 64:iL * 64 + 64, kcol:kcol + 128],
                               QF[iL * 64:iL * 64 + 64,
                                  w * 256:(w + 1) * 256],
                               start=True, stop=True,
                               tile_position=(iL * 64, 0)),
                           reads=[KF_b, QF_b], writes=[PBB[sbank]])

                    def emit_exp(st, iL, w=w):
                        sbank = (st % 2) * 2 + iL
                        Pt, Pt_b = Ptsi[(st % 2) * 2 + iL]
                        bias = (T["win0_bias_t"][:] if (w == 0 and st < 2)
                                else T["zeros_col"][:])
                        bias_b = (B["win0_bias_t"] if (w == 0 and st < 2)
                                  else B["zeros_col"])
                        op("scalar",
                           lambda e, Pt=Pt, sbank=sbank, bias=bias:
                           e.activation(Pt, PB[sbank][:, 0:256], AF.Exp,
                                        bias=bias, scale=8.0),
                           reads=[PBB[sbank], bias_b], writes=[Pt_b])
                        if st < 2:
                            args = dict(pattern=[[-1, 256]],
                                        base=128 * (st % 2),
                                        channel_multiplier=1)
                        else:
                            args = dict(pattern=[[1, 256]],
                                        base=-128 * (st % 2),
                                        channel_multiplier=-1)
                        op("gpsimd",
                           lambda e, Pt=Pt, args=args:
                           e.affine_select(
                               Pt, Pt, compare_op=ALU.is_ge, fill=0.0,
                               **args),
                           reads=[Pt_b], writes=[Pt_b])

                    def emit_pv(st, iL, w=w):
                        kt = 2 * w + st
                        zb = 4 + (w % 2) * 2 + iL
                        Pt, Pt_b = Ptsi[(st % 2) * 2 + iL]
                        VV, VVb = (VAi, VAi_b) if iL == 0 else (VBi, VBi_b)
                        op("tensor",
                           lambda e, kt=kt, zb=zb, Pt=Pt, VV=VV, st=st:
                           e.matmul(
                               PB[zb][0:65, 0:256],
                               VV[:, kt * 65:kt * 65 + 65], Pt,
                               start=(st == 0), stop=(st == 3)),
                           reads=[VVb, Pt_b], writes=[PBB[zb]])

                    for st in range(5):
                        if st < 4:
                            emit_qk(st, 0)
                            emit_qk(st, 1)
                        if st > 0:
                            emit_pv(st - 1, 0)
                            emit_pv(st - 1, 1)
                        if st < 4:
                            emit_exp(st, 0)
                            emit_exp(st, 1)
                    for iL in range(2):
                        zb = 4 + (w % 2) * 2 + iL
                        op("vector",
                           lambda e, zb=zb: e.tensor_copy(
                               T["denr"][64:65, 0:256], PB[zb][64:65, 0:256]),
                           reads=[PBB[zb]], writes=bb("denr"))
                        op("tensor",
                           lambda e, zb=zb: e.matmul(
                               PB[zb][64:128, 0:256], T["ones65"][64:65, 0:64],
                               T["denr"][64:65, 0:256], start=True, stop=True,
                               tile_position=(64, 64)),
                           reads=bb("denr", "ones65"), writes=[PBB[zb]])
                        op("vector",
                           lambda e, zb=zb: e.reciprocal(
                               T["denr"][0:64, 0:256], PB[zb][64:128, 0:256]),
                           reads=[PBB[zb]], writes=bb("denr"))
                        op("vector",
                           lambda e, hp=hp, iL=iL, w=w, zb=zb: e.tensor_mul(
                               XB(hp)[iL * 64:iL * 64 + 64,
                                      w * 256:(w + 1) * 256],
                               PB[zb][0:64, 0:256], T["denr"][0:64, 0:256]),
                           reads=[B["denr"], PBB[zb]], writes=[XNB[hp]])

            kb.barrier()

            for o in range(NF):
                op("sync",
                   lambda e, l=l, o=o: e.dma_start(
                       out=WOB[:, o * 1024:(o + 1) * 1024],
                       in_=T["Wo_opt"][l, o].rearrange("p r c -> p (r c)")),
                   reads=bb("Wo_opt"), writes=[WOB_b], dma=True)
            for mt in range(4):
                for o in range(NF):
                    bank = (mt % 2) * 4 + o % 4
                    for r in range(NF):
                        op("tensor",
                           lambda e, o=o, r=r, mt=mt, bank=bank: e.matmul(
                               PB[bank][:],
                               WOB[:, o * 1024 + r * 128:o * 1024 + (r + 1) * 128],
                               XB(r)[:, mt * 512:(mt + 1) * 512],
                               start=(r == 0), stop=(r == NF - 1)),
                           reads=[WOB_b, XNB[r]], writes=[PBB[bank]],
                           last=(r == NF - 1))
                    op("vector",
                       lambda e, o=o, mt=mt, bank=bank: e.scalar_tensor_tensor(
                           STG[:, o * 512:(o + 1) * 512], PB[bank][:], 0.0,
                           XA(o)[:, mt * 512:(mt + 1) * 512],
                           ALU.add, ALU.add),
                       reads=[PBB[bank], XNB[o]], writes=[STG_b])
                for o in range(NF):
                    op("vector",
                       lambda e, o=o, mt=mt: e.tensor_copy(
                           XB(o)[:, mt * 512:(mt + 1) * 512],
                           STG[:, o * 512:(o + 1) * 512]),
                       reads=[STG_b], writes=[XNB[o]])

            for r in range(NF):
                op("gpsimd", lambda e, r=r: e.dma_start(
                    out=T["cc2_in"][r], in_=XB(r)[:, 2047:2048]),
                   reads=[XNB[r]], writes=bb("cc2_in"), dma=True)
            op("gpsimd",
               lambda e: e.collective_compute(
                   "AllGather", ALU.bypass,
                   replica_groups=[[0, 1], [2, 3], [4, 5], [6, 7]],
                   ins=[T["cc2_in"]], outs=[T["cc2_out"]]),
               reads=bb("cc2_in"), writes=bb("cc2_out"))
            for r in range(NF):
                op("scalar", lambda e, r=r: e.dma_start(
                    out=T["tmpt"][:, 0:1], in_=T["cc2_out"][0, r]),
                   reads=bb("cc2_out"), writes=bb("tmpt"), dma=True)
                op("vector", lambda e, r=r: e.tensor_scalar(
                    T["xpc"][:, r:r + 1], T["tmpt"][:, 0:1],
                    T["halo_gate_t"][:], None, ALU.mult),
                   reads=[B["tmpt"], B["halo_gate_t"]], writes=bb("xpc"))

            kb.barrier()

            for r in range(4, NF):
                c0 = (r - 4) * 128
                op("vector", lambda e, r=r, c0=c0: e.tensor_copy(
                    SHF[:, c0:c0 + 1], T["xpc"][:, r:r + 1]),
                   reads=bb("xpc"), writes=[SHF_b])
                op("vector", lambda e, r=r, c0=c0: e.tensor_copy(
                    SHF[:, c0 + 1:c0 + 128], XB(r)[:, 0:127]),
                   reads=[XNB[r]], writes=[SHF_b])

            def shifted_ffn(r, mt, lo, hi):
                if r < 4:
                    return XB(r)[:, mt * 512 + lo:mt * 512 + hi], XNB[r]
                base = mt * 512
                if base + lo == 0:
                    c0 = (r - 4) * 128
                    return SHF[:, c0 + lo:c0 + hi], SHF_b
                return XB(r)[:, base + lo - 1:base + hi - 1], XNB[r]

            for mt in range(4):
                for r in range(NF):
                    pieces = ([(0, 128), (128, 512)] if (mt == 0 and r >= 4)
                              else [(0, 512)])
                    for pi, (lo, hi) in enumerate(pieces):
                        first = (r == 0 and pi == 0)
                        last_ = (r == NF - 1 and pi == len(pieces) - 1)
                        src, src_b = shifted_ffn(r, mt, lo, hi)
                        op("tensor",
                           lambda e, src=src, lo=lo, hi=hi, first=first,
                           last_=last_: e.matmul(
                               PB[6][0:1, lo:hi], T["ones_colf"][:], src,
                               start=first, stop=last_),
                           reads=[src_b, B["ones_colf"]], writes=[PBB[6]])
                        sqo = (r % 2) * 512
                        op("scalar",
                           lambda e, src=src, sqo=sqo, lo=lo, hi=hi: e.activation(
                               GG[:, sqo + lo:sqo + hi], src, AF.Square),
                           reads=[src_b], writes=[GGB[sqo // 512]])
                        op("tensor",
                           lambda e, sqo=sqo, lo=lo, hi=hi, first=first,
                           last_=last_: e.matmul(
                               PB[7][0:1, lo:hi], T["ones_colf"][:],
                               GG[:, sqo + lo:sqo + hi],
                               start=first, stop=last_),
                           reads=[GGB[sqo // 512], B["ones_colf"]],
                           writes=[PBB[7]])
                op("vector", lambda e: e.tensor_scalar(
                    T["rowsS"][0:1, :], PB[6][0:1, :], 1.0 / D, None, ALU.mult),
                   reads=[PBB[6]], writes=bb("rowsS"))
                op("vector", lambda e: e.tensor_scalar(
                    RWB[0:1, 0:512], PB[7][0:1, :], 1.0 / D, None, ALU.mult),
                   reads=[PBB[7]], writes=[RWB_b])
                op("vector", lambda e: e.tensor_mul(
                    PB[4][0:1, :], T["rowsS"][0:1, :], T["rowsS"][0:1, :]),
                   reads=bb("rowsS"), writes=[PBB[4]])
                op("vector", lambda e: e.tensor_sub(
                    PB[5][0:1, :], RWB[0:1, 0:512], PB[4][0:1, :]),
                   reads=[RWB_b, PBB[4]], writes=[PBB[5]])
                op("scalar", lambda e: e.activation(
                    PB[4][0:1, :], PB[5][0:1, :], AF.Ln,
                    bias=T["epsc"][0:1, :], scale=1.0),
                   reads=[PBB[5], B["epsc"]], writes=[PBB[4]])
                op("scalar", lambda e: e.activation(
                    RWB[0:1, 0:512], PB[4][0:1, :], AF.Exp, scale=-0.5),
                   reads=[PBB[4]], writes=[RWB_b])
                op("vector", lambda e: e.tensor_mul(
                    PB[5][0:1, :], T["rowsS"][0:1, :], RWB[0:1, 0:512]),
                   reads=[B["rowsS"], RWB_b], writes=[PBB[5]])
                op("vector", lambda e: e.tensor_copy(
                    T["rowsS"][0:1, :], PB[5][0:1, :]),
                   reads=[PBB[5]], writes=bb("rowsS"))
                op("tensor", lambda e: e.matmul(
                    PB[6][:, 0:512], T["ones_row"][:], RWB[0:1, 0:512],
                    start=True, stop=True),
                   reads=[RWB_b, B["ones_row"]], writes=[PBB[6]])
                op("tensor", lambda e: e.matmul(
                    PB[7][:, 0:512], T["ones_row"][:], T["rowsS"][0:1, :],
                    start=True, stop=True),
                   reads=[B["rowsS"], B["ones_row"]], writes=[PBB[7]])
                op("vector", lambda e: e.tensor_copy(ABC, PB[6][:]),
                   reads=[PBB[6]], writes=[ABC_b])
                op("vector", lambda e: e.tensor_copy(CBC, PB[7][:]),
                   reads=[PBB[7]], writes=[CBC_b])
                for r in range(NF):
                    pieces = ([(0, 128), (128, 512)] if (mt == 0 and r >= 4)
                              else [(0, 512)])
                    for lo, hi in pieces:
                        src, src_b = shifted_ffn(r, mt, lo, hi)
                        op("vector",
                           lambda e, src=src, lo=lo, hi=hi: e.tensor_mul(
                               GG[:, 1024 + lo:1024 + hi], src, ABC[:, lo:hi]),
                           reads=[src_b, ABC_b], writes=[GGB[2]])
                    op("vector", lambda e: e.tensor_sub(
                        GG[:, 1536:2048], GG[:, 1024:1536], CBC),
                       reads=[GGB[2], CBC_b], writes=[GGB[3]])
                    op("vector",
                       lambda e, r=r, mt=mt: e.tensor_scalar(
                           XA(r)[:, mt * 512:(mt + 1) * 512],
                           GG[:, 1536:2048],
                           T["vecs"][:, 52 + r:53 + r], T["vecs"][:, 60 + r:61 + r],
                           ALU.mult, ALU.add),
                       reads=[GGB[3], B["vecs"]], writes=[XNB[r]])

            for mt in range(4):
                def emit_w2a(c):
                    wlast = 128 if c < NCC - 1 else LASTW
                    W2C, W2C_b = W2Cs[c % 2]
                    for o in range(4):
                        op("tensor",
                           lambda e, c=c, o=o, wlast=wlast, W2C=W2C: e.matmul(
                               PB[o][:],
                               W2C[0:wlast, o * 128:(o + 1) * 128],
                               GG[0:wlast, c * 512:(c + 1) * 512],
                               start=(c == 0), stop=(c == NCC - 1)),
                           reads=[W2C_b, GGB[c]], writes=[PBB[o]])

                for c in range(NCC + 1):
                    if c < NCC:
                        wlast = 128 if c < NCC - 1 else LASTW
                        W1A, W1A_b = W1As[c % 2]
                        W1G, W1G_b = W1Gs[c % 2]
                        W2C, W2C_b = W2Cs[c % 2]
                        op("sync", lambda e, l=l, c=c, wlast=wlast, W1A=W1A:
                           e.dma_start(
                               out=W1A[:, 0:8 * wlast]
                                   .rearrange("p (r c) -> p r c", c=wlast),
                               in_=T["W1_opt"][l, 0, c][:, :, 0:wlast]),
                           reads=bb("W1_opt"), writes=[W1A_b], dma=True)
                        op("sync", lambda e, l=l, c=c, wlast=wlast, W1G=W1G:
                           e.dma_start(
                               out=W1G[:, 0:8 * wlast]
                                   .rearrange("p (r c) -> p r c", c=wlast),
                               in_=T["W1_opt"][l, 1, c][:, :, 0:wlast]),
                           reads=bb("W1_opt"), writes=[W1G_b], dma=True)
                        op("sync", lambda e, l=l, c=c, wlast=wlast, W2C=W2C:
                           e.dma_start(
                               out=W2C[0:wlast, :],
                               in_=T["W2"][l][c * 128:c * 128 + wlast, 0:512]),
                           reads=bb("W2"), writes=[W2C_b], dma=True)
                        for r in range(NF):
                            op("tensor",
                               lambda e, r=r, mt=mt, wlast=wlast, W1A=W1A:
                               e.matmul(
                                   PB[6][0:wlast, :],
                                   W1A[:, r * wlast:(r + 1) * wlast],
                                   XA(r)[:, mt * 512:(mt + 1) * 512],
                                   start=(r == 0), stop=(r == NF - 1)),
                               reads=[W1A_b, XNB[r]], writes=[PBB[6]],
                               last=(r == NF - 1))
                        for r in range(NF):
                            op("tensor",
                               lambda e, r=r, mt=mt, wlast=wlast, W1G=W1G:
                               e.matmul(
                                   PB[7][0:wlast, :],
                                   W1G[:, r * wlast:(r + 1) * wlast],
                                   XA(r)[:, mt * 512:(mt + 1) * 512],
                                   start=(r == 0), stop=(r == NF - 1)),
                               reads=[W1G_b, XNB[r]], writes=[PBB[7]],
                               last=(r == NF - 1))
                    if c > 0:
                        emit_w2a(c - 1)
                    if c < NCC:
                        op("scalar",
                           lambda e, c=c, wlast=wlast: e.activation(
                               GEL[0:wlast, :], PB[7][0:wlast, :], AF.Gelu,
                               bias=T["vecs"][0:wlast, 22 + c:23 + c]),
                           reads=[PBB[7], B["vecs"]], writes=[GEL_b])
                        op("vector",
                           lambda e, c=c, wlast=wlast: e.scalar_tensor_tensor(
                               GG[0:wlast, c * 512:(c + 1) * 512],
                               PB[6][0:wlast, :],
                               T["vecs"][0:wlast, c:c + 1], GEL[0:wlast, :],
                               ALU.add, ALU.mult),
                           reads=[PBB[6], B["vecs"], GEL_b], writes=[GGB[c]])
                for o in range(4):
                    op("vector",
                       lambda e, o=o, mt=mt: e.scalar_tensor_tensor(
                           XA(o)[:, mt * 512:(mt + 1) * 512], PB[o][:],
                           T["vecs"][:, 44 + o:45 + o],
                           XB(o)[:, mt * 512:(mt + 1) * 512],
                           ALU.add, ALU.add),
                       reads=[PBB[o], B["vecs"], XNB[o]], writes=[XNB[o]])
                for c in range(NCC):
                    wlast = 128 if c < NCC - 1 else LASTW
                    W2C, W2C_b = W2Cs[c % 2]
                    op("sync", lambda e, l=l, c=c, wlast=wlast, W2C=W2C:
                       e.dma_start(
                           out=W2C[0:wlast, 0:512],
                           in_=T["W2"][l][c * 128:c * 128 + wlast, 512:1024]),
                       reads=bb("W2"), writes=[W2C_b], dma=True)
                    for o in range(4):
                        op("tensor",
                           lambda e, c=c, o=o, wlast=wlast, W2C=W2C: e.matmul(
                               PB[4 + o][:],
                               W2C[0:wlast, o * 128:(o + 1) * 128],
                               GG[0:wlast, c * 512:(c + 1) * 512],
                               start=(c == 0), stop=(c == NCC - 1)),
                           reads=[W2C_b, GGB[c]], writes=[PBB[4 + o]])
                for o in range(4):
                    op("vector",
                       lambda e, o=o, mt=mt: e.scalar_tensor_tensor(
                           XA(4 + o)[:, mt * 512:(mt + 1) * 512], PB[4 + o][:],
                           T["vecs"][:, 48 + o:49 + o],
                           XB(4 + o)[:, mt * 512:(mt + 1) * 512],
                           ALU.add, ALU.add),
                       reads=[PBB[4 + o], B["vecs"], XNB[4 + o]],
                       writes=[XNB[4 + o]])

            kb.barrier()

            if debug >= 3:
                dram_out("xl%d_dbg" % l, [NF, 128, SOWN])
                for r in range(NF):
                    op("gpsimd", lambda e, r=r, l=l: e.dma_start(
                        out=T["xl%d_dbg" % l][r], in_=XA(r)),
                       reads=[XNB[r]], writes=bb("xl%d_dbg" % l), dma=True)
                kb.barrier()

    if do_final:
        OTM, OTM_b = seg("OTM", 0, 1024)
        XOW, XOW_b = seg("XOW", 1024, 1024)
        for r in range(NF):
            op("sync", lambda e, r=r: e.dma_start(
                out=XB(r), in_=T["z_scr"][r * 128:(r + 1) * 128, :]),
               reads=bb("z_scr"), writes=[XNB[r]], dma=True)
        for tt in range(16):
            op("sync", lambda e, tt=tt: e.dma_start(
                out=XOW, in_=T["x_ctx"][2048 + tt * 128:2048 + (tt + 1) * 128, :]),
               reads=bb("x_ctx"), writes=[XOW_b], dma=True)
            for r in range(NF):
                tb = T["tmpt"] if r % 2 == 0 else T["tmpt2"]
                tbb = B["tmpt"] if r % 2 == 0 else B["tmpt2"]
                bank = 2 + (r % 2)
                op("vector", lambda e, r=r, tt=tt, tb=tb: e.tensor_add(
                    tb[:], XA(r)[:, tt * 128:(tt + 1) * 128],
                    XB(r)[:, tt * 128:(tt + 1) * 128]),
                   reads=[XNB[r]], writes=[tbb])
                op("tensor", lambda e, tb=tb, bank=bank: e.transpose(
                    PB[bank][:, 0:128], tb[:], T["ident"][:]),
                   reads=[tbb, B["ident"]], writes=[PBB[bank]])
                op("vector", lambda e, r=r, bank=bank: e.scalar_tensor_tensor(
                    OTM[:, r * 128:(r + 1) * 128], PB[bank][:, 0:128], 0.0,
                    XOW[:, r * 128:(r + 1) * 128], ALU.add, ALU.add),
                   reads=[PBB[bank], XOW_b], writes=[OTM_b])
            op("sync", lambda e, tt=tt: e.dma_start(
                out=T["out_own"][tt * 128:(tt + 1) * 128, :], in_=OTM),
               reads=[OTM_b], writes=bb("out_own"), dma=True)
    kb.barrier()

    block = stack.enter_context(nc.Block())
    kb.finalize(block)
    return nc, stack, T


def _swap_ap(rot1):
    ap = rot1[:, 0:128]
    return bass.AP(ap.tensor, ap.offset + 32,
                   [list(ap.ap[0]), [64, 2], [-32, 2], [1, 32]])


def make_tables(h):
    half = np.arange(0, DH, 2, dtype=np.float64)
    inv_freq = 1.0 / 10000.0 ** (half / DH)
    base_scale = (half + 0.4 * DH) / (1.4 * DH)
    start = h * 2048 - 256
    t = np.arange(start, start + 2304, dtype=np.float64)
    tc = np.clip(t, 0, None)
    freqs = tc[:, None] * inv_freq[None, :]
    freqs = np.concatenate([freqs, freqs], -1)
    power = (tc - SCTX // 2) / (WIN // 2)
    scale = base_scale[None, :] ** power[:, None]
    scale = np.concatenate([scale, scale], -1)
    cos, sin = np.cos(freqs), np.sin(freqs)
    A = (cos * scale).astype(np.float32)
    S = (sin * scale).astype(np.float32)
    tabB = np.concatenate([S[:, :32], -S[:, 32:]], -1)
    Ai = (cos / scale).astype(np.float32)
    Si = (sin / scale).astype(np.float32)
    tabBi = np.concatenate([Si[:, :32], -Si[:, 32:]], -1)
    return A, tabB, Ai, tabBi


_CACHED = {}


def _get_nc():
    if "nc" not in _CACHED:
        nc, stack, Tm = build_nc()
        _CACHED["nc"] = nc
    return _CACHED["nc"]


def kernel(L0_attn_x, L0_attn_W_K, L0_attn_W_Q, L0_attn_W_V, L0_attn_W_pred,
           L0_ffn_Wqkv, L0_ffn_q_scale, L0_ffn_k_scale, L0_ffn_Wo,
           L0_ffn_ln_g, L0_ffn_ln_b, L0_ffn_W1, L0_ffn_b1, L0_ffn_W2,
           L0_ffn_b2):
    x = np.ascontiguousarray(np.asarray(L0_attn_x, dtype=np.float32))
    Bsz = x.shape[0]
    com = dict(
        Wqkv=np.ascontiguousarray(L0_ffn_Wqkv, dtype=np.float32),
        Wo=np.ascontiguousarray(L0_ffn_Wo, dtype=np.float32),
        W1=np.ascontiguousarray(L0_ffn_W1, dtype=np.float32),
        W2=np.ascontiguousarray(L0_ffn_W2, dtype=np.float32),
        b1=np.ascontiguousarray(L0_ffn_b1, dtype=np.float32),
        b2=np.ascontiguousarray(L0_ffn_b2, dtype=np.float32),
        lng=np.ascontiguousarray(L0_ffn_ln_g, dtype=np.float32),
        lnb=np.ascontiguousarray(L0_ffn_ln_b, dtype=np.float32),
        qsc=np.ascontiguousarray(L0_ffn_q_scale, dtype=np.float32),
        ksc=np.ascontiguousarray(L0_ffn_k_scale, dtype=np.float32),
        WK=np.ascontiguousarray(L0_attn_W_K, dtype=np.float32),
        WQ=np.ascontiguousarray(L0_attn_W_Q, dtype=np.float32),
        WV=np.ascontiguousarray(L0_attn_W_V, dtype=np.float32),
        Wpred=np.ascontiguousarray(L0_attn_W_pred, dtype=np.float32),
    )
    tabs = {h: make_tables(h) for h in (0, 1)}
    in_maps = []
    for c in range(8):
        b, h = c // 2, c % 2
        if h == 1:
            x_ctx = x[b]
        else:
            x_ctx = np.concatenate(
                [np.zeros((2048, D), dtype=np.float32), x[b, :2048]], axis=0)
        A, tb, Ai, tbi = tabs[h]
        m = dict(com)
        m.update(
            x_ctx=np.ascontiguousarray(x_ctx),
            tabA=A, tabB=tb, tabAi=Ai, tabBi=tbi,
            pad_bias=np.full((128, 1), NEG if h == 0 else 0.0, dtype=np.float32),
            win0_bias=np.full((128, 1), NEG if h == 0 else 0.0, dtype=np.float32),
            halo_gate=np.full((128, 1), 0.0 if h == 0 else 1.0, dtype=np.float32),
        )
        in_maps.append(m)
    nc = _get_nc()
    res = run_bass_kernel_spmd(nc, in_maps, core_ids=list(range(8)))
    out = np.empty((Bsz, SCTX, D), dtype=np.float32)
    for c in range(8):
        b, h = c // 2, c % 2
        out[b, h * 2048:(h + 1) * 2048] = res.results[c]["out_own"]
    return out


# version 06
# speedup vs baseline: 1.3670x; 1.0135x over previous
import numpy as np
from contextlib import ExitStack

import concourse.bass as bass
import concourse.mybir as mybir
from concourse.bass_utils import run_bass_kernel_spmd

F32 = mybir.dt.float32
AF = mybir.ActivationFunctionType
ALU = mybir.AluOpType
AX = mybir.AxisListType

D = 1024
H = 16
DH = 64
SOWN = 2048
SCTX = 4096
NDEPTH = 6
DI = 2730
WIN = 256
NEG = -1.0e9
NF = 8
NCC = (DI + 127) // 128
LASTW = DI - 128 * (NCC - 1)
ENGINES = ("sync", "tensor", "vector", "scalar", "gpsimd")


class Sem:
    __slots__ = ("h", "n", "name", "is_q")

    def __init__(self, h, name):
        self.h = h
        self.n = 0
        self.name = name
        self.is_q = False


class Buf:
    __slots__ = ("w", "r", "name")

    def __init__(self, name=""):
        self.w = {}
        self.r = {}
        self.name = name


def _merge(dst, src):
    for s, c in src.items():
        if dst.get(s, -1) < c:
            dst[s] = c


class KB:
    def __init__(self, nc, stack):
        self.nc = nc
        self.stack = stack
        self.ops = {e: [] for e in ENGINES}
        self.waited = {e: {} for e in ENGINES}
        self.sems = []
        self.dsem = {e: self.sem("zd" + e[:2]) for e in ENGINES}
        self.qsem = {e: [self.sem("zq" + e[:2] + str(i)) for i in range(8)]
                     for e in ("sync", "scalar", "gpsimd", "vector")}
        for pool in self.qsem.values():
            for s_ in pool:
                s_.is_q = True
        self.qrr = {e: 0 for e in self.qsem}

    def sem(self, name):
        h = self.stack.enter_context(self.nc.semaphore(name))
        s = Sem(h, name)
        self.sems.append(s)
        return s

    def op(self, eng, fn, reads=(), writes=(), dma=False, last=True, extra=()):
        if dma:
            pool = self.qsem[eng]
            s = pool[self.qrr[eng]]
            self.qrr[eng] = (self.qrr[eng] + 1) % len(pool)
            k = 16
            extra = list(extra) + [(s, s.n)]
        else:
            s = self.dsem[eng]
            k = 1
        waits = {}
        for b in reads:
            _merge(waits, b.w)
        for b in writes:
            _merge(waits, b.w)
            _merge(waits, b.r)
        for sm, c in extra:
            if waits.get(sm, -1) < c:
                waits[sm] = c
        wd = self.waited[eng]
        for sm in list(waits):
            if sm.is_q:
                waits[sm] = sm.n
        ws = [(sm, c) for sm, c in waits.items() if wd.get(sm, 0) < c]
        for sm, c in ws:
            wd[sm] = c
        if last:
            s.n += k
            cover = s.n
        else:
            cover = s.n + k
        self.ops[eng].append((ws, fn, s if last else None, k))
        for b in writes:
            if last:
                b.w = {s: cover}
                b.r = {}
        for b in reads:
            if b.r.get(s, -1) < cover:
                b.r[s] = cover
        return (s, cover)

    def barrier(self):
        bar = self.sem("zb%d" % len(self.sems))
        for e in ENGINES:
            ws = [(self.dsem[e], self.dsem[e].n)] if self.dsem[e].n else []
            for q in self.qsem.get(e, ()):
                if q.n:
                    ws.append((q, q.n))
            self.ops[e].append((ws, (lambda eng: eng.nop()), bar, 1))
            bar.n += 1
        tgt = bar.n
        snap = {s: s.n for s in self.sems if s.n > 0}
        for e in ENGINES:
            self.ops[e].append(([(bar, tgt)], None, None, 0))
            w = dict(snap)
            w[bar] = tgt
            self.waited[e] = w
        return bar

    def finalize(self, block):
        for e in ENGINES:
            deco = getattr(block, e)
            ops = self.ops[e]

            def make(ops=ops):
                def run(engine):
                    for ws, fn, s, k in ops:
                        for sm, c in ws:
                            engine.wait_ge(sm.h, c)
                        if fn is not None:
                            instr = fn(engine)
                            if s is not None:
                                instr.then_inc(s.h, k)
                return run
            deco(make())


def build_nc(debug=0, depth=NDEPTH, do_outer=True, do_inner=True, do_final=True,
             pairs_outer=8):
    nc = bass.Bass("TRN2", target_bir_lowering=False)
    stack = ExitStack()
    kb = KB(nc, stack)
    T = {}
    B = {}

    def dram_in(name, shape):
        T[name] = nc.dram_tensor(name, shape, F32, kind="ExternalInput").ap()
        B[name] = Buf(name)

    def dram_out(name, shape):
        T[name] = nc.dram_tensor(name, shape, F32, kind="ExternalOutput").ap()
        B[name] = Buf(name)

    def dram_tmp(name, shape):
        T[name] = nc.dram_tensor(name, shape, F32, kind="Internal").ap()
        B[name] = Buf(name)

    def sb(name, shape, dtype=F32):
        T[name] = nc.alloc_sbuf_tensor(name, shape, dtype).ap()
        B[name] = Buf(name)

    def psum(name, shape):
        T[name] = nc.alloc_psum_tensor(name, shape, F32).ap()
        B[name] = Buf(name)

    dram_in("x_ctx", [SCTX, D])
    dram_in("Wqkv", [6, D, 3 * D])
    dram_in("Wo", [6, D, D])
    dram_in("W1", [6, D, 2 * DI])
    dram_in("W2", [6, DI, D])
    dram_in("b1", [6, 2 * DI])
    dram_in("b2", [6, D])
    dram_in("lng", [6, D])
    dram_in("lnb", [6, D])
    dram_in("qsc", [6, DH])
    dram_in("ksc", [6, DH])
    dram_in("WK", [H, H])
    dram_in("WQ", [H, H])
    dram_in("WV", [H, H])
    dram_in("Wpred", [H, DH, DH])
    dram_in("tabA", [2304, DH])
    dram_in("tabB", [2304, DH])
    dram_in("tabAi", [2304, DH])
    dram_in("tabBi", [2304, DH])
    dram_in("pad_bias", [128, 1])
    dram_in("win0_bias", [128, 1])
    dram_in("halo_gate", [128, 1])
    dram_out("out_own", [SOWN, D])

    dram_tmp("z_scr", [D, SOWN])
    dram_tmp("Wqkv_opt", [6, 8, 128, 8, 384])
    dram_tmp("Wo_opt", [6, 8, 128, 8, 128])
    dram_tmp("W1_opt", [6, 2, NCC, 128, 8, 128])
    dram_tmp("probs_hbm", [3, H, H])
    dram_tmp("rows_hbm", [2, 512])
    dram_tmp("cc1_in", [NF, 128, 257])
    dram_tmp("cc1_out", [2, NF, 128, 257])
    dram_tmp("cc2_in", [NF, 128, 1])
    dram_tmp("cc2_out", [2, NF, 128, 1])

    for r in range(NF):
        sb("XN%d" % r, [128, SCTX])
    sb("ident", [128, 128])
    sb("diag2", [128, 128])
    sb("ones64", [64, 64])
    sb("ones65", [65, 64])
    sb("ones_row", [1, 128])
    sb("ones_colf", [128, 1])
    sb("zeros_col", [128, 1])
    sb("probs", [16, 48])
    sb("prob_s", [16, 8])
    sb("prob_r", [16, 8])
    sb("colK", [128, 128])
    sb("colQ", [128, 128])
    sb("colV", [128, 128])
    sb("pad_bias_t", [128, 1])
    sb("epsc", [128, 1])
    sb("win0_bias_t", [128, 1])
    sb("halo_gate_t", [128, 1])
    sb("wpred2", [128, 128])
    sb("stat", [128, 64])
    sb("denr", [65, 512])
    sb("vecs", [128, 72])
    sb("qsks", [128, 256])
    sb("rowsS", [1, 512])
    sb("tmpt", [128, 128])
    sb("tmpt2", [128, 128])
    sb("xpc", [128, 8])
    sb("ARENA", [128, 17616])
    for i in range(8):
        psum("pb%d" % i, [128, 512])

    AR = T["ARENA"]
    seg_tab = {}

    def seg(name, off, cols):
        if name not in seg_tab:
            seg_tab[name] = Buf("sg_" + name)
        return AR[:, off:off + cols], seg_tab[name]

    def op(eng, fn, reads=(), writes=(), **kw):
        return kb.op(eng, fn, reads=reads, writes=writes, **kw)

    def bb(*names):
        return [B[n] for n in names]

    XN = [T["XN%d" % r] for r in range(NF)]
    XNB = [B["XN%d" % r] for r in range(NF)]
    PB = [T["pb%d" % i] for i in range(8)]
    PBB = [B["pb%d" % i] for i in range(8)]
    ST = T["stat"]
    STB = B["stat"]

    def XA(r):
        return XN[r][:, 2048:4096]

    def XB(r):
        return XN[r][:, 0:2048]

    op("vector", lambda e: e.memset(T["ones64"][:], 1.0), writes=bb("ones64"))
    op("vector", lambda e: e.memset(T["ones65"][:], 1.0), writes=bb("ones65"))
    op("vector", lambda e: e.memset(T["ones_row"][:], 1.0), writes=bb("ones_row"))
    op("vector", lambda e: e.memset(T["ones_colf"][:], 1.0), writes=bb("ones_colf"))
    op("vector", lambda e: e.memset(T["zeros_col"][:], 0.0), writes=bb("zeros_col"))
    op("vector", lambda e: e.memset(T["epsc"][:], 1e-5), writes=bb("epsc"))
    op("vector", lambda e: e.memset(T["ident"][:], 0.0), writes=bb("ident"))
    op("vector", lambda e: e.memset(T["diag2"][:], 0.0), writes=bb("diag2"))
    for ph in (0, 64):
        op("gpsimd",
           lambda e, ph=ph: e.affine_select(
               T["ident"][ph:ph + 64, ph:ph + 64], T["ones64"][:],
               pattern=[[-1, 64]], compare_op=ALU.is_equal, fill=0.0,
               base=0, channel_multiplier=1),
           reads=bb("ones64"), writes=bb("ident"))
        for ch in (0, 64):
            op("gpsimd",
               lambda e, ph=ph, ch=ch: e.affine_select(
                   T["diag2"][ph:ph + 64, ch:ch + 64], T["ones64"][:],
                   pattern=[[-1, 64]], compare_op=ALU.is_equal, fill=0.0,
                   base=0, channel_multiplier=1),
               reads=bb("ones64"), writes=bb("diag2"))
    for nm in ("pad_bias", "win0_bias", "halo_gate"):
        op("sync", lambda e, nm=nm: e.dma_start(out=T[nm + "_t"][:], in_=T[nm]),
           reads=bb(nm), writes=bb(nm + "_t"), dma=True)
    for idx, wn in enumerate(("WK", "WQ", "WV")):
        op("sync",
           lambda e, wn=wn, c0=idx * 16: e.dma_start(
               out=T["probs"][0:16, c0:c0 + 16], in_=T[wn]),
           reads=bb(wn), writes=bb("probs"), dma=True)
    for idx in range(3):
        op("scalar",
           lambda e, c0=idx * 16, idx=idx: e.activation(
               T["stat"][0:16, c0:c0 + 16], T["probs"][0:16, c0:c0 + 16],
               AF.Exp, accum_out=T["prob_s"][0:16, idx:idx + 1]),
           reads=bb("probs"), writes=bb("stat", "prob_s"))
    op("vector", lambda e: e.reciprocal(T["prob_r"][0:16, 0:3], T["prob_s"][0:16, 0:3]),
       reads=bb("prob_s"), writes=bb("prob_r"))
    for idx in range(3):
        op("vector",
           lambda e, c0=idx * 16, idx=idx: e.tensor_scalar(
               T["probs"][0:16, c0:c0 + 16], T["stat"][0:16, c0:c0 + 16],
               T["prob_r"][0:16, idx:idx + 1], None, ALU.mult),
           reads=bb("stat", "prob_r"), writes=bb("probs"))
    for idx in range(3):
        op("scalar",
           lambda e, idx=idx, c0=idx * 16: e.dma_start(
               out=T["probs_hbm"][idx], in_=T["probs"][0:16, c0:c0 + 16]),
           reads=bb("probs"), writes=bb("probs_hbm"), dma=True)
    for idx, cn in enumerate(("colK", "colQ", "colV")):
        for half in (0, 1):
            op("sync",
               lambda e, idx=idx, cn=cn, half=half: e.dma_start(
                   out=T[cn][half * 64:half * 64 + 64, 0:128]
                       .rearrange("p (i r) -> p i r", i=16),
                   in_=T["probs_hbm"][idx][:, half::2]
                       .unsqueeze(0).broadcast_to([64, 16, 8]),
                   allow_slow_non_contiguous=True),
               reads=bb("probs_hbm"), writes=bb(cn), dma=True)

    for l in range(NDEPTH):
        for hp in range(8):
            for rr in range(NF):
                op("scalar",
                   lambda e, l=l, hp=hp, rr=rr: e.dma_start(
                       out=T["Wqkv_opt"][l, hp][:, rr]
                           .rearrange("p (sg c) -> p sg c", sg=3),
                       in_=T["Wqkv"][l]
                           .rearrange("(r p) (sg c) -> r p sg c", p=128, sg=3)
                           [rr, :, :, hp * 128:(hp + 1) * 128]),
                   reads=bb("Wqkv"), writes=bb("Wqkv_opt"), dma=True)
        for o in range(NF):
            op("scalar",
               lambda e, l=l, o=o: e.dma_start(
                   out=T["Wo_opt"][l, o],
                   in_=T["Wo"][l]
                       .rearrange("(r p) (o c) -> o p r c", p=128, o=8)[o]),
               reads=bb("Wo"), writes=bb("Wo_opt"), dma=True)
        for ag in range(2):
            for c in range(NCC):
                wl = 128 if c < NCC - 1 else LASTW
                op("gpsimd",
                   lambda e, l=l, ag=ag, c=c, wl=wl: e.dma_start(
                       out=T["W1_opt"][l, ag, c][:, :, 0:wl],
                       in_=T["W1"][l][:, ag * DI + c * 128:ag * DI + c * 128 + wl]
                           .rearrange("(r p) cc -> p r cc", p=128),
                       allow_slow_non_contiguous=True),
                   reads=bb("W1"), writes=bb("W1_opt"), dma=True)

    xtm, xtm_b = seg("xtm", 0, 1024)
    xnm, xnm_b = seg("xnm", 1024, 1024)
    SQ1, SQ1_b = seg("SQ1", 2048, 1024)
    for tt in range(SCTX // 128):
        op("sync", lambda e, tt=tt: e.dma_start(
            out=xtm, in_=T["x_ctx"][tt * 128:(tt + 1) * 128, :]),
           reads=bb("x_ctx"), writes=[xtm_b], dma=True)
        op("scalar", lambda e: e.activation(SQ1, xtm, AF.Square,
                                            accum_out=ST[:, 1:2]),
           reads=[xtm_b], writes=[SQ1_b, STB])
        op("vector", lambda e: e.reduce_sum(ST[:, 0:1], xtm, axis=AX.X),
           reads=[xtm_b], writes=[STB])
        op("vector", lambda e: e.tensor_scalar(ST[:, 2:3], ST[:, 0:1],
                                               1.0 / D, None, ALU.mult),
           reads=[STB], writes=[STB])
        op("vector", lambda e: e.tensor_mul(ST[:, 3:4], ST[:, 2:3], ST[:, 2:3]),
           reads=[STB], writes=[STB])
        op("vector", lambda e: e.tensor_scalar(ST[:, 4:5], ST[:, 1:2],
                                               1.0 / D, None, ALU.mult),
           reads=[STB], writes=[STB])
        op("vector", lambda e: e.tensor_sub(ST[:, 5:6], ST[:, 4:5], ST[:, 3:4]),
           reads=[STB], writes=[STB])
        op("scalar", lambda e: e.activation(ST[:, 6:7], ST[:, 5:6], AF.Ln,
                                            bias=T["epsc"][:], scale=1.0),
           reads=[STB, B["epsc"]], writes=[STB])
        op("scalar", lambda e: e.activation(ST[:, 7:8], ST[:, 6:7], AF.Exp,
                                            scale=-0.5),
           reads=[STB], writes=[STB])
        op("vector", lambda e: e.tensor_mul(ST[:, 8:9], ST[:, 2:3], ST[:, 7:8]),
           reads=[STB], writes=[STB])
        op("vector", lambda e: e.tensor_scalar(ST[:, 9:10], ST[:, 8:9],
                                               -1.0, None, ALU.mult),
           reads=[STB], writes=[STB])
        op("vector", lambda e: e.tensor_scalar(xnm, xtm, ST[:, 7:8], ST[:, 9:10],
                                               ALU.mult, ALU.add),
           reads=[xtm_b, STB], writes=[xnm_b])
        for r in range(NF):
            bank = r % 2
            op("tensor", lambda e, r=r, bank=bank: e.transpose(
                PB[bank][:, 0:128], xnm[:, r * 128:(r + 1) * 128], T["ident"][:]),
               reads=[xnm_b, B["ident"]], writes=[PBB[bank]])
            op("vector", lambda e, r=r, tt=tt, bank=bank: e.tensor_copy(
                XN[r][:, tt * 128:(tt + 1) * 128], PB[bank][:, 0:128]),
               reads=[PBB[bank]], writes=[XNB[r]])

    if debug >= 1:
        dram_out("xn_dbg", [NF, 128, SCTX])
        for r in range(NF):
            op("gpsimd", lambda e, r=r: e.dma_start(out=T["xn_dbg"][r], in_=XN[r][:]),
               reads=[XNB[r]], writes=bb("xn_dbg"), dma=True)

    kb.barrier()

    if do_outer:
        KP, KP_b = seg("KP", 0, 4096)
        VA, VA_b = seg("VA", 4096, 2080)
        VB_, VB_b = seg("VB", 6176, 2080)
        QP, QP_b = seg("QP", 8256, 2048)
        P0, P0_b = seg("P0", 10304, 512)
        P1, P1_b = seg("P1", 10816, 512)
        KD, KD_b = seg("KD", 11328, 1024)
        QD, QD_b = seg("QD", 12352, 1024)
        VD, VD_b = seg("VD", 13376, 1024)
        ZT, ZT_b = seg("ZT", 14400, 1024)
        P2, P2_b = seg("P2", 15424, 512)
        P3, P3_b = seg("P3", 15936, 512)
        for hp in range(pairs_outer):
            iA = 2 * hp
            for r in range(NF):
                for iL in range(2):
                    cc = (iA + iL) * 8 + r
                    op("vector",
                       lambda e, r=r, iL=iL, cc=cc: e.tensor_scalar(
                           KD[:, r * 128 + iL * 64:r * 128 + iL * 64 + 64],
                           T["diag2"][:, iL * 64:iL * 64 + 64],
                           T["colK"][:, cc:cc + 1], None, ALU.mult),
                       reads=bb("diag2", "colK"), writes=[KD_b])
                    op("vector",
                       lambda e, r=r, iL=iL, cc=cc: e.tensor_scalar(
                           VD[:, r * 128 + iL * 64:r * 128 + iL * 64 + 64],
                           T["diag2"][:, iL * 64:iL * 64 + 64],
                           T["colV"][:, cc:cc + 1], None, ALU.mult),
                       reads=bb("diag2", "colV"), writes=[VD_b])
            for iL in range(2):
                for half in (0, 1):
                    op("sync",
                       lambda e, iL=iL, ih=iA + iL, half=half: e.dma_start(
                           out=T["wpred2"][half * 64:half * 64 + 64,
                                           iL * 64:iL * 64 + 64],
                           in_=T["Wpred"][ih]),
                       reads=bb("Wpred"), writes=bb("wpred2"), dma=True)
            for r in range(NF):
                for iL in range(2):
                    cc = (iA + iL) * 8 + r
                    op("vector",
                       lambda e, r=r, iL=iL, cc=cc: e.tensor_scalar(
                           QD[:, r * 128 + iL * 64:r * 128 + iL * 64 + 64],
                           T["wpred2"][:, iL * 64:iL * 64 + 64],
                           T["colQ"][:, cc:cc + 1], None, ALU.mult),
                       reads=bb("wpred2", "colQ"), writes=[QD_b])
            for mt in range(8):
                bank = 6 + (mt % 2)
                for r in range(NF):
                    op("tensor",
                       lambda e, r=r, mt=mt, bank=bank: e.matmul(
                           PB[bank][:], KD[:, r * 128:(r + 1) * 128],
                           XN[r][:, mt * 512:(mt + 1) * 512],
                           start=(r == 0), stop=(r == NF - 1)),
                       reads=[KD_b, XNB[r]], writes=[PBB[bank]],
                       last=(r == NF - 1))
                op("vector",
                   lambda e, mt=mt, bank=bank: e.tensor_copy(
                       KP[:, mt * 512:(mt + 1) * 512], PB[bank][:]),
                   reads=[PBB[bank]], writes=[KP_b])
            op("vector", lambda e: e.memset(
                VA.rearrange("p (t c) -> p t c", c=65)[:, :, 64:65], 1.0),
               writes=[VA_b])
            op("vector", lambda e: e.memset(
                VB_.rearrange("p (t c) -> p t c", c=65)[:, :, 64:65], 1.0),
               writes=[VB_b])
            for tt in range(32):
                bank = 6 + (tt % 2)
                for r in range(NF):
                    op("tensor",
                       lambda e, r=r, tt=tt, bank=bank: e.matmul(
                           PB[bank][:, 0:128], XN[r][:, tt * 128:(tt + 1) * 128],
                           VD[:, r * 128:(r + 1) * 128],
                           start=(r == 0), stop=(r == NF - 1)),
                       reads=[VD_b, XNB[r]], writes=[PBB[bank]],
                       last=(r == NF - 1))
                op("vector",
                   lambda e, tt=tt, bank=bank: e.tensor_copy(
                       VA[:, tt * 65:tt * 65 + 64], PB[bank][:, 0:64]),
                   reads=[PBB[bank]], writes=[VA_b])
                op("vector",
                   lambda e, tt=tt, bank=bank: e.tensor_copy(
                       VB_[:, tt * 65:tt * 65 + 64], PB[bank][:, 64:128]),
                   reads=[PBB[bank]], writes=[VB_b])
            for mt in range(4):
                bank = 6 + (mt % 2)
                for r in range(NF):
                    op("tensor",
                       lambda e, r=r, mt=mt, bank=bank: e.matmul(
                           PB[bank][:], QD[:, r * 128:(r + 1) * 128],
                           XN[r][:, 2048 + mt * 512:2048 + (mt + 1) * 512],
                           start=(r == 0), stop=(r == NF - 1)),
                       reads=[QD_b, XNB[r]], writes=[PBB[bank]],
                       last=(r == NF - 1))
                op("vector",
                   lambda e, mt=mt, bank=bank: e.tensor_copy(
                       QP[:, mt * 512:(mt + 1) * 512], PB[bank][:]),
                   reads=[PBB[bank]], writes=[QP_b])
            Pts = ((P0, P0_b), (P1, P1_b), (P2, P2_b), (P3, P3_b))
            for j in range(4):
                nd = 20 + 4 * j

                def emit_qk(d, iL, j=j):
                    sbank = (d % 2) * 2 + iL
                    op("tensor",
                       lambda e, d=d, j=j, iL=iL, sbank=sbank: e.matmul(
                           PB[sbank][:],
                           KP[iL * 64:iL * 64 + 64, d * 128:(d + 1) * 128],
                           QP[iL * 64:iL * 64 + 64, j * 512:(j + 1) * 512],
                           start=True, stop=True,
                           tile_position=(iL * 64, 0)),
                       reads=[KP_b, QP_b], writes=[PBB[sbank]])

                def emit_exp(d, iL, j=j, nd=nd):
                    stair = d - (nd - 4)
                    sbank = (d % 2) * 2 + iL
                    Pt, Pt_b = Pts[(d % 2) * 2 + iL]
                    bias = T["pad_bias_t"][:] if d < 16 else T["zeros_col"][:]
                    bias_b = B["pad_bias_t"] if d < 16 else B["zeros_col"]
                    op("scalar",
                       lambda e, Pt=Pt, sbank=sbank, bias=bias: e.activation(
                           Pt, PB[sbank][:], AF.Exp, bias=bias, scale=0.125),
                       reads=[PBB[sbank], bias_b], writes=[Pt_b])
                    if stair >= 0:
                        op("gpsimd",
                           lambda e, Pt=Pt, stair=stair: e.affine_select(
                               Pt, Pt, pattern=[[1, 512]],
                               compare_op=ALU.is_ge, fill=0.0,
                               base=-128 * stair, channel_multiplier=-1),
                           reads=[Pt_b], writes=[Pt_b])

                def emit_pv(d, iL, nd=nd, j=j):
                    zb = 4 + (j % 2) * 2 + iL
                    Pt, Pt_b = Pts[(d % 2) * 2 + iL]
                    VV, VVb = (VA, VA_b) if iL == 0 else (VB_, VB_b)
                    op("tensor",
                       lambda e, d=d, zb=zb, Pt=Pt, VV=VV, nd=nd: e.matmul(
                           PB[zb][0:65, :], VV[:, d * 65:d * 65 + 65], Pt,
                           start=(d == 0), stop=(d == nd - 1)),
                       reads=[VVb, Pt_b], writes=[PBB[zb]])

                for d in range(nd + 1):
                    if d < nd:
                        emit_qk(d, 0)
                        emit_qk(d, 1)
                    if d > 0:
                        emit_pv(d - 1, 0)
                        emit_pv(d - 1, 1)
                    if d < nd:
                        emit_exp(d, 0)
                        emit_exp(d, 1)
                for iL in range(2):
                    zb = 4 + (j % 2) * 2 + iL
                    op("vector",
                       lambda e, zb=zb: e.tensor_copy(
                           T["denr"][64:65, 0:512], PB[zb][64:65, 0:512]),
                       reads=[PBB[zb]], writes=bb("denr"))
                    bcb = 2 + iL
                    op("tensor",
                       lambda e, bcb=bcb: e.matmul(
                           PB[bcb][0:64, :], T["ones65"][64:65, 0:64],
                           T["denr"][64:65, 0:512], start=True, stop=True),
                       reads=bb("denr", "ones65"), writes=[PBB[bcb]])
                    op("vector",
                       lambda e, iL=iL, bcb=bcb: e.reciprocal(
                           ZT[0:64, iL * 512:iL * 512 + 512],
                           PB[bcb][0:64, :]),
                       reads=[PBB[bcb]], writes=[ZT_b])
                    op("vector",
                       lambda e, zb=zb, iL=iL: e.tensor_mul(
                           ZT[64:128, iL * 512:iL * 512 + 512],
                           PB[zb][0:64, :],
                           ZT[0:64, iL * 512:iL * 512 + 512]),
                       reads=[ZT_b, PBB[zb]], writes=[ZT_b])
                    op("gpsimd",
                       lambda e, hp=hp, iL=iL, j=j: e.dma_start(
                           out=T["z_scr"][hp * 128 + iL * 64:
                                          hp * 128 + iL * 64 + 64,
                                          j * 512:(j + 1) * 512],
                           in_=ZT[64:128, iL * 512:iL * 512 + 512]),
                       reads=[ZT_b], writes=bb("z_scr"), dma=True)

        if debug >= 2:
            dram_out("z_dbg", [D, SOWN])
            op("gpsimd", lambda e: e.dma_start(out=T["z_dbg"], in_=T["z_scr"]),
               reads=bb("z_scr"), writes=bb("z_dbg"), dma=True)

    kb.barrier()

    if do_inner:
        tA, tA_b = seg("tA", 0, 1024)
        tBt, tBt_b = seg("tBt", 1024, 1024)
        tAi, tAi_b = seg("tAi", 2048, 1152)
        tBi, tBi_b = seg("tBi", 3200, 1152)
        QF, QF_b = seg("QF", 4352, 2048)
        KF, KF_b = seg("KF", 6400, 2304)
        VAi, VAi_b = seg("VAi", 8704, 1170)
        VBi, VBi_b = seg("VBi", 9874, 1170)
        WQK, WQK_b = seg("WQK", 11044, 3072)
        QKVT, QKVT_b = seg("QKVT", 14116, 384)
        RT1, RT1_b = seg("RT1", 14500, 256)
        RT2, RT2_b = seg("RT2", 14756, 256)
        ROT1 = AR[:, 14116:14116 + 256]
        ROT1_b = QKVT_b
        P0o = AR[:, 14116:14116 + 256]
        P0o_b = QKVT_b
        P1o = AR[:, 14500:14500 + 256]
        P1o_b = RT1_b
        P2o = AR[:, 14756:14756 + 256]
        P2o_b = RT2_b
        P3o = AR[:, 11044:11044 + 256]
        P3o_b = WQK_b
        HF, HF_b = seg("HF", 15012, NF * 257)
        SHQ, SHQ_b = seg("SHQ", 17068, 512)
        WOB, WOB_b = seg("WOB", 0, 8192)
        STG, STG_b = seg("STG", 8192, 4096)
        GG, GG_b = seg("GG", 0, NCC * 512)
        GGB = [Buf("gg%d" % c) for c in range(NCC)]
        W1As = [seg("W1A0", 11264, 1024), seg("W1A1", 12288, 1024)]
        W1Gs = [seg("W1G0", 13312, 1024), seg("W1G1", 14336, 1024)]
        W2Cs = [seg("W2C0", 15360, 512), seg("W2C1", 15872, 512)]
        GEL, GEL_b = seg("GEL", 16384, 512)
        ABC, ABC_b = AR[:, 2048:2560], GGB[4]
        CBC, CBC_b = AR[:, 2560:3072], GGB[5]
        SHF, SHF_b = AR[:, 3072:3584], GGB[6]
        RWB, RWB_b = AR[:, 3584:4096], GGB[7]

        for l in range(depth):
            for tn, dst, dstb, ntl, skip in (
                    ("tabA", tA, tA_b, 16, 2), ("tabB", tBt, tBt_b, 16, 2),
                    ("tabAi", tAi, tAi_b, 18, 0), ("tabBi", tBi, tBi_b, 18, 0)):
                op("sync",
                   lambda e, tn=tn, dst=dst, ntl=ntl, skip=skip: e.dma_start(
                       out=dst.rearrange("p (t c) -> p t c", c=64),
                       in_=T[tn][skip * 128:, :]
                           .rearrange("(t p) c -> p t c", p=128)),
                   reads=bb(tn), writes=[dstb], dma=True)
            for src, c0 in (("qsc", 0), ("qsc", 64), ("ksc", 128), ("ksc", 192)):
                op("sync",
                   lambda e, src=src, c0=c0, l=l: e.dma_start(
                       out=T["qsks"][:, c0:c0 + 64],
                       in_=T[src][l].unsqueeze(0).broadcast_to([128, DH])),
                   reads=bb(src), writes=bb("qsks"), dma=True)
            op("sync", lambda e, l=l: e.dma_start(
                out=T["vecs"][:, 0:21],
                in_=T["b1"][l][0:21 * 128].rearrange("(c p) -> p c", p=128),
                allow_slow_non_contiguous=True),
               reads=bb("b1"), writes=bb("vecs"), dma=True)
            op("sync", lambda e, l=l: e.dma_start(
                out=T["vecs"][0:LASTW, 21:22],
                in_=T["b1"][l][21 * 128:DI].unsqueeze(1),
                allow_slow_non_contiguous=True),
               reads=bb("b1"), writes=bb("vecs"), dma=True)
            op("sync", lambda e, l=l: e.dma_start(
                out=T["vecs"][:, 22:43],
                in_=T["b1"][l][DI:DI + 21 * 128].rearrange("(c p) -> p c", p=128),
                allow_slow_non_contiguous=True),
               reads=bb("b1"), writes=bb("vecs"), dma=True)
            op("sync", lambda e, l=l: e.dma_start(
                out=T["vecs"][0:LASTW, 43:44],
                in_=T["b1"][l][DI + 21 * 128:2 * DI].unsqueeze(1),
                allow_slow_non_contiguous=True),
               reads=bb("b1"), writes=bb("vecs"), dma=True)
            op("sync", lambda e, l=l: e.dma_start(
                out=T["vecs"][:, 44:52],
                in_=T["b2"][l].rearrange("(c p) -> p c", p=128),
                allow_slow_non_contiguous=True),
               reads=bb("b2"), writes=bb("vecs"), dma=True)
            op("sync", lambda e, l=l: e.dma_start(
                out=T["vecs"][:, 52:60],
                in_=T["lng"][l].rearrange("(c p) -> p c", p=128),
                allow_slow_non_contiguous=True),
               reads=bb("lng"), writes=bb("vecs"), dma=True)
            op("sync", lambda e, l=l: e.dma_start(
                out=T["vecs"][:, 60:68],
                in_=T["lnb"][l].rearrange("(c p) -> p c", p=128),
                allow_slow_non_contiguous=True),
               reads=bb("lnb"), writes=bb("vecs"), dma=True)

            for r in range(NF):
                op("gpsimd", lambda e, r=r: e.dma_start(
                    out=T["cc1_in"][r], in_=XA(r)[:, 1791:2048]),
                   reads=[XNB[r]], writes=bb("cc1_in"), dma=True)
            op("gpsimd",
               lambda e: e.collective_compute(
                   "AllGather", ALU.bypass,
                   replica_groups=[[0, 1], [2, 3], [4, 5], [6, 7]],
                   ins=[T["cc1_in"]], outs=[T["cc1_out"]]),
               reads=bb("cc1_in"), writes=bb("cc1_out"))
            for r in range(NF):
                op("scalar", lambda e, r=r: e.dma_start(
                    out=SHQ[:, 0:257], in_=T["cc1_out"][0, r]),
                   reads=bb("cc1_out"), writes=[SHQ_b], dma=True)
                op("vector", lambda e, r=r: e.tensor_scalar(
                    HF[:, r * 257:(r + 1) * 257], SHQ[:, 0:257],
                    T["halo_gate_t"][:], None, ALU.mult),
                   reads=[SHQ_b, B["halo_gate_t"]], writes=[HF_b])

            for r in range(4, NF):
                c0 = (r - 4) * 128
                op("vector", lambda e, r=r, c0=c0: e.tensor_copy(
                    SHQ[:, c0:c0 + 1], HF[:, r * 257 + 256:r * 257 + 257]),
                   reads=[HF_b], writes=[SHQ_b])
                op("vector", lambda e, r=r, c0=c0: e.tensor_copy(
                    SHQ[:, c0 + 1:c0 + 128], XA(r)[:, 0:127]),
                   reads=[XNB[r]], writes=[SHQ_b])

            def shifted_attn(r, tile):
                if tile == 0:
                    o = r * 257 + (1 if r < 4 else 0)
                    return HF[:, o:o + 128], HF_b
                if tile == 1:
                    o = r * 257 + (129 if r < 4 else 128)
                    return HF[:, o:o + 128], HF_b
                t = tile - 2
                if r < 4:
                    return XA(r)[:, t * 128:(t + 1) * 128], XNB[r]
                if t == 0:
                    c0 = (r - 4) * 128
                    return SHQ[:, c0:c0 + 128], SHQ_b
                return XA(r)[:, t * 128 - 1:(t + 1) * 128 - 1], XNB[r]

            for hp in range(8):
                op("sync",
                   lambda e, l=l, hp=hp: e.dma_start(
                       out=WQK,
                       in_=T["Wqkv_opt"][l, hp].rearrange("p r c -> p (r c)")),
                   reads=bb("Wqkv_opt"), writes=[WQK_b], dma=True)
                op("vector", lambda e: e.memset(
                    VAi.rearrange("p (t c) -> p t c", c=65)[:, :, 64:65], 1.0),
                   writes=[VAi_b])
                op("vector", lambda e: e.memset(
                    VBi.rearrange("p (t c) -> p t c", c=65)[:, :, 64:65], 1.0),
                   writes=[VBi_b])
                for tt in range(18):
                    bank = 6 + (tt % 2)
                    for r in range(NF):
                        lhs, lhs_b = shifted_attn(r, tt)
                        op("tensor",
                           lambda e, lhs=lhs, r=r, bank=bank: e.matmul(
                               PB[bank][:, 0:384], lhs,
                               WQK[:, r * 384:(r + 1) * 384],
                               start=(r == 0), stop=(r == NF - 1)),
                           reads=[lhs_b, WQK_b], writes=[PBB[bank]],
                           last=(r == NF - 1))
                    op("vector",
                       lambda e, bank=bank: e.tensor_copy(QKVT, PB[bank][:, 0:384]),
                       reads=[PBB[bank]], writes=[QKVT_b])
                    op("vector",
                       lambda e, tt=tt: e.tensor_copy(
                           VAi[:, tt * 65:tt * 65 + 64], QKVT[:, 256:320]),
                       reads=[QKVT_b], writes=[VAi_b])
                    op("vector",
                       lambda e, tt=tt: e.tensor_copy(
                           VBi[:, tt * 65:tt * 65 + 64], QKVT[:, 320:384]),
                       reads=[QKVT_b], writes=[VBi_b])
                    for q4 in range(4):
                        op("scalar",
                           lambda e, q4=q4: e.activation(
                               RT2[:, q4 * 64:q4 * 64 + 64],
                               QKVT[:, q4 * 64:q4 * 64 + 64], AF.Square,
                               accum_out=ST[:, q4:q4 + 1]),
                           reads=[QKVT_b], writes=[RT2_b, STB])
                    op("vector",
                       lambda e: e.tensor_scalar(ST[:, 4:8], ST[:, 0:4],
                                                 1e-24, None, ALU.max),
                       reads=[STB], writes=[STB])
                    op("scalar", lambda e: e.activation(ST[:, 8:12], ST[:, 4:8],
                                                        AF.Ln),
                       reads=[STB], writes=[STB])
                    op("scalar", lambda e: e.activation(ST[:, 12:16], ST[:, 8:12],
                                                        AF.Exp, scale=-0.5),
                       reads=[STB], writes=[STB])
                    for q4 in range(4):
                        op("vector",
                           lambda e, q4=q4: e.tensor_scalar(
                               RT1[:, q4 * 64:q4 * 64 + 64],
                               QKVT[:, q4 * 64:q4 * 64 + 64],
                               ST[:, 12 + q4:13 + q4], None, ALU.mult),
                           reads=[QKVT_b, STB], writes=[RT1_b])
                    op("vector",
                       lambda e: e.tensor_mul(RT2, RT1, T["qsks"][:]),
                       reads=[RT1_b, B["qsks"]], writes=[RT2_b])
                    sides = ([("q", 0, tA, tA_b, tBt, tBt_b, tt - 2)]
                             if tt >= 2 else [])
                    sides.append(("k", 128, tAi, tAi_b, tBi, tBi_b, tt))
                    for nmx, o0, tac, tac_b, tbc, tbc_b, ti in sides:
                        op("vector",
                           lambda e, o0=o0, tac=tac, ti=ti: e.tensor_mul(
                               RT1[:, o0:o0 + 128]
                                   .rearrange("p (h c) -> p h c", h=2),
                               RT2[:, o0:o0 + 128]
                                   .rearrange("p (h c) -> p h c", h=2),
                               tac[:, ti * 64:(ti + 1) * 64]
                                   .unsqueeze(1).broadcast_to([128, 2, 64])),
                           reads=[RT2_b, tac_b], writes=[RT1_b])
                        op("vector",
                           lambda e, tbc=tbc, ti=ti, o0=o0: e.tensor_mul(
                               ROT1[:, 0:128]
                                   .rearrange("p (h c) -> p h c", h=2),
                               RT2[:, o0:o0 + 128]
                                   .rearrange("p (h c) -> p h c", h=2),
                               tbc[:, ti * 64:(ti + 1) * 64]
                                   .unsqueeze(1).broadcast_to([128, 2, 64])),
                           reads=[RT2_b, tbc_b], writes=[ROT1_b])
                        op("vector",
                           lambda e, o0=o0: e.tensor_add(
                               RT2[:, o0:o0 + 128]
                                   .rearrange("p (h s c) -> p h s c", h=2, s=2),
                               RT1[:, o0:o0 + 128]
                                   .rearrange("p (h s c) -> p h s c", h=2, s=2),
                               _swap_ap(ROT1)),
                           reads=[RT1_b, ROT1_b], writes=[RT2_b])
                    if tt >= 2:
                        op("tensor", lambda e: e.transpose(
                            PB[2][:, 0:128], RT2[:, 0:128], T["ident"][:]),
                           reads=[RT2_b, B["ident"]], writes=[PBB[2]])
                        op("vector",
                           lambda e, tt=tt: e.tensor_copy(
                               QF[:, (tt - 2) * 128:(tt - 1) * 128],
                               PB[2][:, 0:128]),
                           reads=[PBB[2]], writes=[QF_b])
                    op("tensor", lambda e: e.transpose(
                        PB[3][:, 0:128], RT2[:, 128:256], T["ident"][:]),
                       reads=[RT2_b, B["ident"]], writes=[PBB[3]])
                    op("vector",
                       lambda e, tt=tt: e.tensor_copy(
                           KF[:, tt * 128:(tt + 1) * 128], PB[3][:, 0:128]),
                       reads=[PBB[3]], writes=[KF_b])
                Ptsi = ((P0o, P0o_b), (P1o, P1o_b), (P2o, P2o_b), (P3o, P3o_b))
                for w in range(8):
                    def emit_qk(st, iL, w=w):
                        kcol = w * 256 + st * 128
                        sbank = (st % 2) * 2 + iL
                        op("tensor",
                           lambda e, iL=iL, kcol=kcol, w=w, sbank=sbank:
                           e.matmul(
                               PB[sbank][:, 0:256],
                               KF[iL * 64:iL * 64 + 64, kcol:kcol + 128],
                               QF[iL * 64:iL * 64 + 64,
                                  w * 256:(w + 1) * 256],
                               start=True, stop=True,
                               tile_position=(iL * 64, 0)),
                           reads=[KF_b, QF_b], writes=[PBB[sbank]])

                    def emit_exp(st, iL, w=w):
                        sbank = (st % 2) * 2 + iL
                        Pt, Pt_b = Ptsi[(st % 2) * 2 + iL]
                        bias = (T["win0_bias_t"][:] if (w == 0 and st < 2)
                                else T["zeros_col"][:])
                        bias_b = (B["win0_bias_t"] if (w == 0 and st < 2)
                                  else B["zeros_col"])
                        op("scalar",
                           lambda e, Pt=Pt, sbank=sbank, bias=bias:
                           e.activation(Pt, PB[sbank][:, 0:256], AF.Exp,
                                        bias=bias, scale=8.0),
                           reads=[PBB[sbank], bias_b], writes=[Pt_b])
                        if st < 2:
                            args = dict(pattern=[[-1, 256]],
                                        base=128 * (st % 2),
                                        channel_multiplier=1)
                        else:
                            args = dict(pattern=[[1, 256]],
                                        base=-128 * (st % 2),
                                        channel_multiplier=-1)
                        op("gpsimd",
                           lambda e, Pt=Pt, args=args:
                           e.affine_select(
                               Pt, Pt, compare_op=ALU.is_ge, fill=0.0,
                               **args),
                           reads=[Pt_b], writes=[Pt_b])

                    def emit_pv(st, iL, w=w):
                        kt = 2 * w + st
                        zb = 4 + (w % 2) * 2 + iL
                        Pt, Pt_b = Ptsi[(st % 2) * 2 + iL]
                        VV, VVb = (VAi, VAi_b) if iL == 0 else (VBi, VBi_b)
                        op("tensor",
                           lambda e, kt=kt, zb=zb, Pt=Pt, VV=VV, st=st:
                           e.matmul(
                               PB[zb][0:65, 0:256],
                               VV[:, kt * 65:kt * 65 + 65], Pt,
                               start=(st == 0), stop=(st == 3)),
                           reads=[VVb, Pt_b], writes=[PBB[zb]])

                    for st in range(5):
                        if st < 4:
                            emit_qk(st, 0)
                            emit_qk(st, 1)
                        if st > 0:
                            emit_pv(st - 1, 0)
                            emit_pv(st - 1, 1)
                        if st < 4:
                            emit_exp(st, 0)
                            emit_exp(st, 1)
                    for iL in range(2):
                        zb = 4 + (w % 2) * 2 + iL
                        op("vector",
                           lambda e, zb=zb: e.tensor_copy(
                               T["denr"][64:65, 0:256], PB[zb][64:65, 0:256]),
                           reads=[PBB[zb]], writes=bb("denr"))
                        bcb = 2 + iL
                        op("tensor",
                           lambda e, bcb=bcb: e.matmul(
                               PB[bcb][0:64, 0:256], T["ones65"][64:65, 0:64],
                               T["denr"][64:65, 0:256], start=True, stop=True),
                           reads=bb("denr", "ones65"), writes=[PBB[bcb]])
                        op("vector",
                           lambda e, bcb=bcb: e.reciprocal(
                               T["denr"][0:64, 0:256], PB[bcb][0:64, 0:256]),
                           reads=[PBB[bcb]], writes=bb("denr"))
                        op("vector",
                           lambda e, hp=hp, iL=iL, w=w, zb=zb: e.tensor_mul(
                               XB(hp)[iL * 64:iL * 64 + 64,
                                      w * 256:(w + 1) * 256],
                               PB[zb][0:64, 0:256], T["denr"][0:64, 0:256]),
                           reads=[B["denr"], PBB[zb]], writes=[XNB[hp]])

            kb.barrier()

            for o in range(NF):
                op("sync",
                   lambda e, l=l, o=o: e.dma_start(
                       out=WOB[:, o * 1024:(o + 1) * 1024],
                       in_=T["Wo_opt"][l, o].rearrange("p r c -> p (r c)")),
                   reads=bb("Wo_opt"), writes=[WOB_b], dma=True)
            for mt in range(4):
                for o in range(NF):
                    bank = (mt % 2) * 4 + o % 4
                    for r in range(NF):
                        op("tensor",
                           lambda e, o=o, r=r, mt=mt, bank=bank: e.matmul(
                               PB[bank][:],
                               WOB[:, o * 1024 + r * 128:o * 1024 + (r + 1) * 128],
                               XB(r)[:, mt * 512:(mt + 1) * 512],
                               start=(r == 0), stop=(r == NF - 1)),
                           reads=[WOB_b, XNB[r]], writes=[PBB[bank]],
                           last=(r == NF - 1))
                    op("vector",
                       lambda e, o=o, mt=mt, bank=bank: e.scalar_tensor_tensor(
                           STG[:, o * 512:(o + 1) * 512], PB[bank][:], 0.0,
                           XA(o)[:, mt * 512:(mt + 1) * 512],
                           ALU.add, ALU.add),
                       reads=[PBB[bank], XNB[o]], writes=[STG_b])
                for o in range(NF):
                    op("vector",
                       lambda e, o=o, mt=mt: e.tensor_copy(
                           XB(o)[:, mt * 512:(mt + 1) * 512],
                           STG[:, o * 512:(o + 1) * 512]),
                       reads=[STG_b], writes=[XNB[o]])

            for r in range(NF):
                op("gpsimd", lambda e, r=r: e.dma_start(
                    out=T["cc2_in"][r], in_=XB(r)[:, 2047:2048]),
                   reads=[XNB[r]], writes=bb("cc2_in"), dma=True)
            op("gpsimd",
               lambda e: e.collective_compute(
                   "AllGather", ALU.bypass,
                   replica_groups=[[0, 1], [2, 3], [4, 5], [6, 7]],
                   ins=[T["cc2_in"]], outs=[T["cc2_out"]]),
               reads=bb("cc2_in"), writes=bb("cc2_out"))
            for r in range(NF):
                op("scalar", lambda e, r=r: e.dma_start(
                    out=T["tmpt"][:, 0:1], in_=T["cc2_out"][0, r]),
                   reads=bb("cc2_out"), writes=bb("tmpt"), dma=True)
                op("vector", lambda e, r=r: e.tensor_scalar(
                    T["xpc"][:, r:r + 1], T["tmpt"][:, 0:1],
                    T["halo_gate_t"][:], None, ALU.mult),
                   reads=[B["tmpt"], B["halo_gate_t"]], writes=bb("xpc"))

            kb.barrier()

            for r in range(4, NF):
                c0 = (r - 4) * 128
                op("vector", lambda e, r=r, c0=c0: e.tensor_copy(
                    SHF[:, c0:c0 + 1], T["xpc"][:, r:r + 1]),
                   reads=bb("xpc"), writes=[SHF_b])
                op("vector", lambda e, r=r, c0=c0: e.tensor_copy(
                    SHF[:, c0 + 1:c0 + 128], XB(r)[:, 0:127]),
                   reads=[XNB[r]], writes=[SHF_b])

            def shifted_ffn(r, mt, lo, hi):
                if r < 4:
                    return XB(r)[:, mt * 512 + lo:mt * 512 + hi], XNB[r]
                base = mt * 512
                if base + lo == 0:
                    c0 = (r - 4) * 128
                    return SHF[:, c0 + lo:c0 + hi], SHF_b
                return XB(r)[:, base + lo - 1:base + hi - 1], XNB[r]

            for mt in range(4):
                for r in range(NF):
                    pieces = ([(0, 128), (128, 512)] if (mt == 0 and r >= 4)
                              else [(0, 512)])
                    for pi, (lo, hi) in enumerate(pieces):
                        first = (r == 0 and pi == 0)
                        last_ = (r == NF - 1 and pi == len(pieces) - 1)
                        src, src_b = shifted_ffn(r, mt, lo, hi)
                        op("tensor",
                           lambda e, src=src, lo=lo, hi=hi, first=first,
                           last_=last_: e.matmul(
                               PB[6][0:1, lo:hi], T["ones_colf"][:], src,
                               start=first, stop=last_),
                           reads=[src_b, B["ones_colf"]], writes=[PBB[6]])
                        sqo = (r % 2) * 512
                        op("scalar",
                           lambda e, src=src, sqo=sqo, lo=lo, hi=hi: e.activation(
                               GG[:, sqo + lo:sqo + hi], src, AF.Square),
                           reads=[src_b], writes=[GGB[sqo // 512]])
                        op("tensor",
                           lambda e, sqo=sqo, lo=lo, hi=hi, first=first,
                           last_=last_: e.matmul(
                               PB[7][0:1, lo:hi], T["ones_colf"][:],
                               GG[:, sqo + lo:sqo + hi],
                               start=first, stop=last_),
                           reads=[GGB[sqo // 512], B["ones_colf"]],
                           writes=[PBB[7]])
                op("vector", lambda e: e.tensor_scalar(
                    T["rowsS"][0:1, :], PB[6][0:1, :], 1.0 / D, None, ALU.mult),
                   reads=[PBB[6]], writes=bb("rowsS"))
                op("vector", lambda e: e.tensor_scalar(
                    RWB[0:1, 0:512], PB[7][0:1, :], 1.0 / D, None, ALU.mult),
                   reads=[PBB[7]], writes=[RWB_b])
                op("vector", lambda e: e.tensor_mul(
                    PB[4][0:1, :], T["rowsS"][0:1, :], T["rowsS"][0:1, :]),
                   reads=bb("rowsS"), writes=[PBB[4]])
                op("vector", lambda e: e.tensor_sub(
                    PB[5][0:1, :], RWB[0:1, 0:512], PB[4][0:1, :]),
                   reads=[RWB_b, PBB[4]], writes=[PBB[5]])
                op("scalar", lambda e: e.activation(
                    PB[4][0:1, :], PB[5][0:1, :], AF.Ln,
                    bias=T["epsc"][0:1, :], scale=1.0),
                   reads=[PBB[5], B["epsc"]], writes=[PBB[4]])
                op("scalar", lambda e: e.activation(
                    RWB[0:1, 0:512], PB[4][0:1, :], AF.Exp, scale=-0.5),
                   reads=[PBB[4]], writes=[RWB_b])
                op("vector", lambda e: e.tensor_mul(
                    PB[5][0:1, :], T["rowsS"][0:1, :], RWB[0:1, 0:512]),
                   reads=[B["rowsS"], RWB_b], writes=[PBB[5]])
                op("vector", lambda e: e.tensor_copy(
                    T["rowsS"][0:1, :], PB[5][0:1, :]),
                   reads=[PBB[5]], writes=bb("rowsS"))
                op("tensor", lambda e: e.matmul(
                    PB[6][:, 0:512], T["ones_row"][:], RWB[0:1, 0:512],
                    start=True, stop=True),
                   reads=[RWB_b, B["ones_row"]], writes=[PBB[6]])
                op("tensor", lambda e: e.matmul(
                    PB[7][:, 0:512], T["ones_row"][:], T["rowsS"][0:1, :],
                    start=True, stop=True),
                   reads=[B["rowsS"], B["ones_row"]], writes=[PBB[7]])
                op("vector", lambda e: e.tensor_copy(ABC, PB[6][:]),
                   reads=[PBB[6]], writes=[ABC_b])
                op("vector", lambda e: e.tensor_copy(CBC, PB[7][:]),
                   reads=[PBB[7]], writes=[CBC_b])
                for r in range(NF):
                    pieces = ([(0, 128), (128, 512)] if (mt == 0 and r >= 4)
                              else [(0, 512)])
                    for lo, hi in pieces:
                        src, src_b = shifted_ffn(r, mt, lo, hi)
                        op("vector",
                           lambda e, src=src, lo=lo, hi=hi: e.tensor_mul(
                               GG[:, 1024 + lo:1024 + hi], src, ABC[:, lo:hi]),
                           reads=[src_b, ABC_b], writes=[GGB[2]])
                    op("vector", lambda e: e.tensor_sub(
                        GG[:, 1536:2048], GG[:, 1024:1536], CBC),
                       reads=[GGB[2], CBC_b], writes=[GGB[3]])
                    op("vector",
                       lambda e, r=r, mt=mt: e.tensor_scalar(
                           XA(r)[:, mt * 512:(mt + 1) * 512],
                           GG[:, 1536:2048],
                           T["vecs"][:, 52 + r:53 + r], T["vecs"][:, 60 + r:61 + r],
                           ALU.mult, ALU.add),
                       reads=[GGB[3], B["vecs"]], writes=[XNB[r]])

            for mt in range(4):
                def emit_w2a(c):
                    wlast = 128 if c < NCC - 1 else LASTW
                    W2C, W2C_b = W2Cs[c % 2]
                    for o in range(4):
                        op("tensor",
                           lambda e, c=c, o=o, wlast=wlast, W2C=W2C: e.matmul(
                               PB[o][:],
                               W2C[0:wlast, o * 128:(o + 1) * 128],
                               GG[0:wlast, c * 512:(c + 1) * 512],
                               start=(c == 0), stop=(c == NCC - 1)),
                           reads=[W2C_b, GGB[c]], writes=[PBB[o]])

                for c in range(NCC + 1):
                    if c < NCC:
                        wlast = 128 if c < NCC - 1 else LASTW
                        W1A, W1A_b = W1As[c % 2]
                        W1G, W1G_b = W1Gs[c % 2]
                        W2C, W2C_b = W2Cs[c % 2]
                        op("sync", lambda e, l=l, c=c, wlast=wlast, W1A=W1A:
                           e.dma_start(
                               out=W1A[:, 0:8 * wlast]
                                   .rearrange("p (r c) -> p r c", c=wlast),
                               in_=T["W1_opt"][l, 0, c][:, :, 0:wlast]),
                           reads=bb("W1_opt"), writes=[W1A_b], dma=True)
                        op("sync", lambda e, l=l, c=c, wlast=wlast, W1G=W1G:
                           e.dma_start(
                               out=W1G[:, 0:8 * wlast]
                                   .rearrange("p (r c) -> p r c", c=wlast),
                               in_=T["W1_opt"][l, 1, c][:, :, 0:wlast]),
                           reads=bb("W1_opt"), writes=[W1G_b], dma=True)
                        op("sync", lambda e, l=l, c=c, wlast=wlast, W2C=W2C:
                           e.dma_start(
                               out=W2C[0:wlast, :],
                               in_=T["W2"][l][c * 128:c * 128 + wlast, 0:512]),
                           reads=bb("W2"), writes=[W2C_b], dma=True)
                        for r in range(NF):
                            op("tensor",
                               lambda e, r=r, mt=mt, wlast=wlast, W1A=W1A:
                               e.matmul(
                                   PB[6][0:wlast, :],
                                   W1A[:, r * wlast:(r + 1) * wlast],
                                   XA(r)[:, mt * 512:(mt + 1) * 512],
                                   start=(r == 0), stop=(r == NF - 1)),
                               reads=[W1A_b, XNB[r]], writes=[PBB[6]],
                               last=(r == NF - 1))
                        for r in range(NF):
                            op("tensor",
                               lambda e, r=r, mt=mt, wlast=wlast, W1G=W1G:
                               e.matmul(
                                   PB[7][0:wlast, :],
                                   W1G[:, r * wlast:(r + 1) * wlast],
                                   XA(r)[:, mt * 512:(mt + 1) * 512],
                                   start=(r == 0), stop=(r == NF - 1)),
                               reads=[W1G_b, XNB[r]], writes=[PBB[7]],
                               last=(r == NF - 1))
                    if c > 0:
                        emit_w2a(c - 1)
                    if c < NCC:
                        op("scalar",
                           lambda e, c=c, wlast=wlast: e.activation(
                               GEL[0:wlast, :], PB[7][0:wlast, :], AF.Gelu,
                               bias=T["vecs"][0:wlast, 22 + c:23 + c]),
                           reads=[PBB[7], B["vecs"]], writes=[GEL_b])
                        op("vector",
                           lambda e, c=c, wlast=wlast: e.scalar_tensor_tensor(
                               GG[0:wlast, c * 512:(c + 1) * 512],
                               PB[6][0:wlast, :],
                               T["vecs"][0:wlast, c:c + 1], GEL[0:wlast, :],
                               ALU.add, ALU.mult),
                           reads=[PBB[6], B["vecs"], GEL_b], writes=[GGB[c]])
                for o in range(4):
                    op("vector",
                       lambda e, o=o, mt=mt: e.scalar_tensor_tensor(
                           XA(o)[:, mt * 512:(mt + 1) * 512], PB[o][:],
                           T["vecs"][:, 44 + o:45 + o],
                           XB(o)[:, mt * 512:(mt + 1) * 512],
                           ALU.add, ALU.add),
                       reads=[PBB[o], B["vecs"], XNB[o]], writes=[XNB[o]])
                for c in range(NCC):
                    wlast = 128 if c < NCC - 1 else LASTW
                    W2C, W2C_b = W2Cs[c % 2]
                    op("sync", lambda e, l=l, c=c, wlast=wlast, W2C=W2C:
                       e.dma_start(
                           out=W2C[0:wlast, 0:512],
                           in_=T["W2"][l][c * 128:c * 128 + wlast, 512:1024]),
                       reads=bb("W2"), writes=[W2C_b], dma=True)
                    for o in range(4):
                        op("tensor",
                           lambda e, c=c, o=o, wlast=wlast, W2C=W2C: e.matmul(
                               PB[4 + o][:],
                               W2C[0:wlast, o * 128:(o + 1) * 128],
                               GG[0:wlast, c * 512:(c + 1) * 512],
                               start=(c == 0), stop=(c == NCC - 1)),
                           reads=[W2C_b, GGB[c]], writes=[PBB[4 + o]])
                for o in range(4):
                    op("vector",
                       lambda e, o=o, mt=mt: e.scalar_tensor_tensor(
                           XA(4 + o)[:, mt * 512:(mt + 1) * 512], PB[4 + o][:],
                           T["vecs"][:, 48 + o:49 + o],
                           XB(4 + o)[:, mt * 512:(mt + 1) * 512],
                           ALU.add, ALU.add),
                       reads=[PBB[4 + o], B["vecs"], XNB[4 + o]],
                       writes=[XNB[4 + o]])

            kb.barrier()

            if debug >= 3:
                dram_out("xl%d_dbg" % l, [NF, 128, SOWN])
                for r in range(NF):
                    op("gpsimd", lambda e, r=r, l=l: e.dma_start(
                        out=T["xl%d_dbg" % l][r], in_=XA(r)),
                       reads=[XNB[r]], writes=bb("xl%d_dbg" % l), dma=True)
                kb.barrier()

    if do_final:
        OTM, OTM_b = seg("OTM", 0, 1024)
        XOW, XOW_b = seg("XOW", 1024, 1024)
        for r in range(NF):
            op("sync", lambda e, r=r: e.dma_start(
                out=XB(r), in_=T["z_scr"][r * 128:(r + 1) * 128, :]),
               reads=bb("z_scr"), writes=[XNB[r]], dma=True)
        for tt in range(16):
            op("sync", lambda e, tt=tt: e.dma_start(
                out=XOW, in_=T["x_ctx"][2048 + tt * 128:2048 + (tt + 1) * 128, :]),
               reads=bb("x_ctx"), writes=[XOW_b], dma=True)
            for r in range(NF):
                tb = T["tmpt"] if r % 2 == 0 else T["tmpt2"]
                tbb = B["tmpt"] if r % 2 == 0 else B["tmpt2"]
                bank = 2 + (r % 2)
                op("vector", lambda e, r=r, tt=tt, tb=tb: e.tensor_add(
                    tb[:], XA(r)[:, tt * 128:(tt + 1) * 128],
                    XB(r)[:, tt * 128:(tt + 1) * 128]),
                   reads=[XNB[r]], writes=[tbb])
                op("tensor", lambda e, tb=tb, bank=bank: e.transpose(
                    PB[bank][:, 0:128], tb[:], T["ident"][:]),
                   reads=[tbb, B["ident"]], writes=[PBB[bank]])
                op("vector", lambda e, r=r, bank=bank: e.scalar_tensor_tensor(
                    OTM[:, r * 128:(r + 1) * 128], PB[bank][:, 0:128], 0.0,
                    XOW[:, r * 128:(r + 1) * 128], ALU.add, ALU.add),
                   reads=[PBB[bank], XOW_b], writes=[OTM_b])
            op("sync", lambda e, tt=tt: e.dma_start(
                out=T["out_own"][tt * 128:(tt + 1) * 128, :], in_=OTM),
               reads=[OTM_b], writes=bb("out_own"), dma=True)
    kb.barrier()

    block = stack.enter_context(nc.Block())
    kb.finalize(block)
    return nc, stack, T


def _swap_ap(rot1):
    ap = rot1[:, 0:128]
    return bass.AP(ap.tensor, ap.offset + 32,
                   [list(ap.ap[0]), [64, 2], [-32, 2], [1, 32]])


def make_tables(h):
    half = np.arange(0, DH, 2, dtype=np.float64)
    inv_freq = 1.0 / 10000.0 ** (half / DH)
    base_scale = (half + 0.4 * DH) / (1.4 * DH)
    start = h * 2048 - 256
    t = np.arange(start, start + 2304, dtype=np.float64)
    tc = np.clip(t, 0, None)
    freqs = tc[:, None] * inv_freq[None, :]
    freqs = np.concatenate([freqs, freqs], -1)
    power = (tc - SCTX // 2) / (WIN // 2)
    scale = base_scale[None, :] ** power[:, None]
    scale = np.concatenate([scale, scale], -1)
    cos, sin = np.cos(freqs), np.sin(freqs)
    A = (cos * scale).astype(np.float32)
    S = (sin * scale).astype(np.float32)
    tabB = np.concatenate([S[:, :32], -S[:, 32:]], -1)
    Ai = (cos / scale).astype(np.float32)
    Si = (sin / scale).astype(np.float32)
    tabBi = np.concatenate([Si[:, :32], -Si[:, 32:]], -1)
    return A, tabB, Ai, tabBi


_CACHED = {}


def _get_nc():
    if "nc" not in _CACHED:
        nc, stack, Tm = build_nc()
        _CACHED["nc"] = nc
    return _CACHED["nc"]


def kernel(L0_attn_x, L0_attn_W_K, L0_attn_W_Q, L0_attn_W_V, L0_attn_W_pred,
           L0_ffn_Wqkv, L0_ffn_q_scale, L0_ffn_k_scale, L0_ffn_Wo,
           L0_ffn_ln_g, L0_ffn_ln_b, L0_ffn_W1, L0_ffn_b1, L0_ffn_W2,
           L0_ffn_b2):
    x = np.ascontiguousarray(np.asarray(L0_attn_x, dtype=np.float32))
    Bsz = x.shape[0]
    com = dict(
        Wqkv=np.ascontiguousarray(L0_ffn_Wqkv, dtype=np.float32),
        Wo=np.ascontiguousarray(L0_ffn_Wo, dtype=np.float32),
        W1=np.ascontiguousarray(L0_ffn_W1, dtype=np.float32),
        W2=np.ascontiguousarray(L0_ffn_W2, dtype=np.float32),
        b1=np.ascontiguousarray(L0_ffn_b1, dtype=np.float32),
        b2=np.ascontiguousarray(L0_ffn_b2, dtype=np.float32),
        lng=np.ascontiguousarray(L0_ffn_ln_g, dtype=np.float32),
        lnb=np.ascontiguousarray(L0_ffn_ln_b, dtype=np.float32),
        qsc=np.ascontiguousarray(L0_ffn_q_scale, dtype=np.float32),
        ksc=np.ascontiguousarray(L0_ffn_k_scale, dtype=np.float32),
        WK=np.ascontiguousarray(L0_attn_W_K, dtype=np.float32),
        WQ=np.ascontiguousarray(L0_attn_W_Q, dtype=np.float32),
        WV=np.ascontiguousarray(L0_attn_W_V, dtype=np.float32),
        Wpred=np.ascontiguousarray(L0_attn_W_pred, dtype=np.float32),
    )
    tabs = {h: make_tables(h) for h in (0, 1)}
    in_maps = []
    for c in range(8):
        b, h = c // 2, c % 2
        if h == 1:
            x_ctx = x[b]
        else:
            x_ctx = np.concatenate(
                [np.zeros((2048, D), dtype=np.float32), x[b, :2048]], axis=0)
        A, tb, Ai, tbi = tabs[h]
        m = dict(com)
        m.update(
            x_ctx=np.ascontiguousarray(x_ctx),
            tabA=A, tabB=tb, tabAi=Ai, tabBi=tbi,
            pad_bias=np.full((128, 1), NEG if h == 0 else 0.0, dtype=np.float32),
            win0_bias=np.full((128, 1), NEG if h == 0 else 0.0, dtype=np.float32),
            halo_gate=np.full((128, 1), 0.0 if h == 0 else 1.0, dtype=np.float32),
        )
        in_maps.append(m)
    nc = _get_nc()
    res = run_bass_kernel_spmd(nc, in_maps, core_ids=list(range(8)))
    out = np.empty((Bsz, SCTX, D), dtype=np.float32)
    for c in range(8):
        b, h = c // 2, c % 2
        out[b, h * 2048:(h + 1) * 2048] = res.results[c]["out_own"]
    return out


# version 07
# speedup vs baseline: 1.5622x; 1.1428x over previous
import numpy as np
from contextlib import ExitStack

import concourse.bass as bass
import concourse.mybir as mybir
from concourse.bass_utils import run_bass_kernel_spmd

F32 = mybir.dt.float32
AF = mybir.ActivationFunctionType
ALU = mybir.AluOpType
AX = mybir.AxisListType

D = 1024
H = 16
DH = 64
SOWN = 2048
SCTX = 4096
NDEPTH = 6
DI = 2730
WIN = 256
NEG = -1.0e9
NF = 8
NCC = (DI + 127) // 128
LASTW = DI - 128 * (NCC - 1)
ENGINES = ("sync", "tensor", "vector", "scalar", "gpsimd")


class Sem:
    __slots__ = ("h", "n", "name", "is_q")

    def __init__(self, h, name):
        self.h = h
        self.n = 0
        self.name = name
        self.is_q = False


class Buf:
    __slots__ = ("w", "r", "name")

    def __init__(self, name=""):
        self.w = {}
        self.r = {}
        self.name = name


def _merge(dst, src):
    for s, c in src.items():
        if dst.get(s, -1) < c:
            dst[s] = c


class KB:
    def __init__(self, nc, stack):
        self.nc = nc
        self.stack = stack
        self.ops = {e: [] for e in ENGINES}
        self.waited = {e: {} for e in ENGINES}
        self.sems = []
        self.dsem = {e: self.sem("zd" + e[:2]) for e in ENGINES}
        self.qsem = {e: [self.sem("zq" + e[:2] + str(i)) for i in range(8)]
                     for e in ("sync", "scalar", "gpsimd", "vector")}
        for pool in self.qsem.values():
            for s_ in pool:
                s_.is_q = True
        self.qrr = {e: 0 for e in self.qsem}

    def sem(self, name):
        h = self.stack.enter_context(self.nc.semaphore(name))
        s = Sem(h, name)
        self.sems.append(s)
        return s

    def op(self, eng, fn, reads=(), writes=(), dma=False, last=True, extra=()):
        if dma:
            pool = self.qsem[eng]
            s = pool[self.qrr[eng]]
            self.qrr[eng] = (self.qrr[eng] + 1) % len(pool)
            k = 16
            extra = list(extra) + [(s, s.n)]
        else:
            s = self.dsem[eng]
            k = 1
        waits = {}
        for b in reads:
            _merge(waits, b.w)
        for b in writes:
            _merge(waits, b.w)
            _merge(waits, b.r)
        for sm, c in extra:
            if waits.get(sm, -1) < c:
                waits[sm] = c
        wd = self.waited[eng]
        for sm in list(waits):
            if sm.is_q:
                waits[sm] = sm.n
        ws = [(sm, c) for sm, c in waits.items() if wd.get(sm, 0) < c]
        for sm, c in ws:
            wd[sm] = c
        if last:
            s.n += k
            cover = s.n
        else:
            cover = s.n + k
        self.ops[eng].append((ws, fn, s if last else None, k))
        for b in writes:
            if last:
                b.w = {s: cover}
                b.r = {}
        for b in reads:
            if b.r.get(s, -1) < cover:
                b.r[s] = cover
        return (s, cover)

    def barrier(self):
        bar = self.sem("zb%d" % len(self.sems))
        for e in ENGINES:
            ws = [(self.dsem[e], self.dsem[e].n)] if self.dsem[e].n else []
            for q in self.qsem.get(e, ()):
                if q.n:
                    ws.append((q, q.n))
            self.ops[e].append((ws, (lambda eng: eng.nop()), bar, 1))
            bar.n += 1
        tgt = bar.n
        snap = {s: s.n for s in self.sems if s.n > 0}
        for e in ENGINES:
            self.ops[e].append(([(bar, tgt)], None, None, 0))
            w = dict(snap)
            w[bar] = tgt
            self.waited[e] = w
        return bar

    def finalize(self, block):
        for e in ENGINES:
            deco = getattr(block, e)
            ops = self.ops[e]

            def make(ops=ops):
                def run(engine):
                    for ws, fn, s, k in ops:
                        for sm, c in ws:
                            engine.wait_ge(sm.h, c)
                        if fn is not None:
                            instr = fn(engine)
                            if s is not None:
                                instr.then_inc(s.h, k)
                return run
            deco(make())


def build_nc(debug=0, depth=NDEPTH, do_outer=True, do_inner=True, do_final=True,
             pairs_outer=8):
    nc = bass.Bass("TRN2", target_bir_lowering=False)
    stack = ExitStack()
    kb = KB(nc, stack)
    T = {}
    B = {}

    def dram_in(name, shape):
        T[name] = nc.dram_tensor(name, shape, F32, kind="ExternalInput").ap()
        B[name] = Buf(name)

    def dram_out(name, shape):
        T[name] = nc.dram_tensor(name, shape, F32, kind="ExternalOutput").ap()
        B[name] = Buf(name)

    def dram_tmp(name, shape):
        T[name] = nc.dram_tensor(name, shape, F32, kind="Internal").ap()
        B[name] = Buf(name)

    def sb(name, shape, dtype=F32):
        T[name] = nc.alloc_sbuf_tensor(name, shape, dtype).ap()
        B[name] = Buf(name)

    def psum(name, shape):
        T[name] = nc.alloc_psum_tensor(name, shape, F32).ap()
        B[name] = Buf(name)

    dram_in("x_ctx", [SCTX, D])
    dram_in("Wqkv", [6, D, 3 * D])
    dram_in("Wo", [6, D, D])
    dram_in("W1", [6, D, 2 * DI])
    dram_in("W2", [6, DI, D])
    dram_in("b1", [6, 2 * DI])
    dram_in("b2", [6, D])
    dram_in("lng", [6, D])
    dram_in("lnb", [6, D])
    dram_in("qsc", [6, DH])
    dram_in("ksc", [6, DH])
    dram_in("WK", [H, H])
    dram_in("WQ", [H, H])
    dram_in("WV", [H, H])
    dram_in("Wpred", [H, DH, DH])
    dram_in("tabA", [2304, DH])
    dram_in("tabB", [2304, DH])
    dram_in("tabAi", [2304, DH])
    dram_in("tabBi", [2304, DH])
    dram_in("pad_bias", [128, 1])
    dram_in("win0_bias", [128, 1])
    dram_in("halo_gate", [128, 1])
    dram_out("out_own", [SOWN, D])

    dram_tmp("z_scr", [D, SOWN])
    dram_tmp("Wqkv_opt", [6, 8, 128, 8, 384])
    dram_tmp("Wo_opt", [6, 8, 128, 8, 128])
    dram_tmp("W1_opt", [6, 2, NCC, 128, 8, 128])
    dram_tmp("probs_hbm", [3, H, H])
    dram_tmp("rows_hbm", [2, 512])
    dram_tmp("cc1_in", [NF, 128, 257])
    dram_tmp("cc1_out", [2, NF, 128, 257])
    dram_tmp("cc2_in", [NF, 128, 1])
    dram_tmp("cc2_out", [2, NF, 128, 1])

    for r in range(NF):
        sb("XN%d" % r, [128, SCTX])
    sb("ident", [128, 128])
    sb("diag2", [128, 128])
    sb("ones64", [64, 64])
    sb("ones65", [65, 64])
    sb("ones_row", [1, 128])
    sb("ones_colf", [128, 1])
    sb("zeros_col", [128, 1])
    sb("probs", [16, 48])
    sb("prob_s", [16, 8])
    sb("prob_r", [16, 8])
    sb("colK", [128, 128])
    sb("colQ", [128, 128])
    sb("colV", [128, 128])
    sb("pad_bias_t", [128, 1])
    sb("epsc", [128, 1])
    sb("win0_bias_t", [128, 1])
    sb("halo_gate_t", [128, 1])
    sb("wpred2", [128, 128])
    sb("stat", [128, 64])
    sb("denr", [65, 512])
    sb("vecs", [128, 72])
    sb("qsks", [128, 256])
    sb("rowsS", [1, 512])
    sb("tmpt", [128, 128])
    sb("tmpt2", [128, 128])
    sb("xpc", [128, 8])
    sb("ARENA", [128, 17616])
    for i in range(8):
        psum("pb%d" % i, [128, 512])

    AR = T["ARENA"]
    seg_tab = {}

    def seg(name, off, cols):
        if name not in seg_tab:
            seg_tab[name] = Buf("sg_" + name)
        return AR[:, off:off + cols], seg_tab[name]

    def op(eng, fn, reads=(), writes=(), **kw):
        return kb.op(eng, fn, reads=reads, writes=writes, **kw)

    def bb(*names):
        return [B[n] for n in names]

    XN = [T["XN%d" % r] for r in range(NF)]
    XNB = [B["XN%d" % r] for r in range(NF)]
    PB = [T["pb%d" % i] for i in range(8)]
    PBB = [B["pb%d" % i] for i in range(8)]
    ST = T["stat"]
    STB = B["stat"]

    def XA(r):
        return XN[r][:, 2048:4096]

    def XB(r):
        return XN[r][:, 0:2048]

    op("vector", lambda e: e.memset(T["ones64"][:], 1.0), writes=bb("ones64"))
    op("vector", lambda e: e.memset(T["ones65"][:], 1.0), writes=bb("ones65"))
    op("vector", lambda e: e.memset(T["ones_row"][:], 1.0), writes=bb("ones_row"))
    op("vector", lambda e: e.memset(T["ones_colf"][:], 1.0), writes=bb("ones_colf"))
    op("vector", lambda e: e.memset(T["zeros_col"][:], 0.0), writes=bb("zeros_col"))
    op("vector", lambda e: e.memset(T["epsc"][:], 1e-5), writes=bb("epsc"))
    op("vector", lambda e: e.memset(T["ident"][:], 0.0), writes=bb("ident"))
    op("vector", lambda e: e.memset(T["diag2"][:], 0.0), writes=bb("diag2"))
    for ph in (0, 64):
        op("gpsimd",
           lambda e, ph=ph: e.affine_select(
               T["ident"][ph:ph + 64, ph:ph + 64], T["ones64"][:],
               pattern=[[-1, 64]], compare_op=ALU.is_equal, fill=0.0,
               base=0, channel_multiplier=1),
           reads=bb("ones64"), writes=bb("ident"))
        for ch in (0, 64):
            op("gpsimd",
               lambda e, ph=ph, ch=ch: e.affine_select(
                   T["diag2"][ph:ph + 64, ch:ch + 64], T["ones64"][:],
                   pattern=[[-1, 64]], compare_op=ALU.is_equal, fill=0.0,
                   base=0, channel_multiplier=1),
               reads=bb("ones64"), writes=bb("diag2"))
    for nm in ("pad_bias", "win0_bias", "halo_gate"):
        op("sync", lambda e, nm=nm: e.dma_start(out=T[nm + "_t"][:], in_=T[nm]),
           reads=bb(nm), writes=bb(nm + "_t"), dma=True)
    for idx, wn in enumerate(("WK", "WQ", "WV")):
        op("sync",
           lambda e, wn=wn, c0=idx * 16: e.dma_start(
               out=T["probs"][0:16, c0:c0 + 16], in_=T[wn]),
           reads=bb(wn), writes=bb("probs"), dma=True)
    for idx in range(3):
        op("scalar",
           lambda e, c0=idx * 16, idx=idx: e.activation(
               T["stat"][0:16, c0:c0 + 16], T["probs"][0:16, c0:c0 + 16],
               AF.Exp, accum_out=T["prob_s"][0:16, idx:idx + 1]),
           reads=bb("probs"), writes=bb("stat", "prob_s"))
    op("vector", lambda e: e.reciprocal(T["prob_r"][0:16, 0:3], T["prob_s"][0:16, 0:3]),
       reads=bb("prob_s"), writes=bb("prob_r"))
    for idx in range(3):
        op("vector",
           lambda e, c0=idx * 16, idx=idx: e.tensor_scalar(
               T["probs"][0:16, c0:c0 + 16], T["stat"][0:16, c0:c0 + 16],
               T["prob_r"][0:16, idx:idx + 1], None, ALU.mult),
           reads=bb("stat", "prob_r"), writes=bb("probs"))
    for idx in range(3):
        op("scalar",
           lambda e, idx=idx, c0=idx * 16: e.dma_start(
               out=T["probs_hbm"][idx], in_=T["probs"][0:16, c0:c0 + 16]),
           reads=bb("probs"), writes=bb("probs_hbm"), dma=True)
    for idx, cn in enumerate(("colK", "colQ", "colV")):
        for half in (0, 1):
            op("sync",
               lambda e, idx=idx, cn=cn, half=half: e.dma_start(
                   out=T[cn][half * 64:half * 64 + 64, 0:128]
                       .rearrange("p (i r) -> p i r", i=16),
                   in_=T["probs_hbm"][idx][:, half::2]
                       .unsqueeze(0).broadcast_to([64, 16, 8]),
                   allow_slow_non_contiguous=True),
               reads=bb("probs_hbm"), writes=bb(cn), dma=True)

    for l in range(NDEPTH):
        for hp in range(8):
            for rr in range(NF):
                op("scalar",
                   lambda e, l=l, hp=hp, rr=rr: e.dma_start(
                       out=T["Wqkv_opt"][l, hp][:, rr]
                           .rearrange("p (sg c) -> p sg c", sg=3),
                       in_=T["Wqkv"][l]
                           .rearrange("(r p) (sg c) -> r p sg c", p=128, sg=3)
                           [rr, :, :, hp * 128:(hp + 1) * 128]),
                   reads=bb("Wqkv"), writes=bb("Wqkv_opt"), dma=True)
        for o in range(NF):
            op("scalar",
               lambda e, l=l, o=o: e.dma_start(
                   out=T["Wo_opt"][l, o],
                   in_=T["Wo"][l]
                       .rearrange("(r p) (o c) -> o p r c", p=128, o=8)[o]),
               reads=bb("Wo"), writes=bb("Wo_opt"), dma=True)
        for ag in range(2):
            for c in range(NCC):
                wl = 128 if c < NCC - 1 else LASTW
                op("gpsimd",
                   lambda e, l=l, ag=ag, c=c, wl=wl: e.dma_start(
                       out=T["W1_opt"][l, ag, c][:, :, 0:wl],
                       in_=T["W1"][l][:, ag * DI + c * 128:ag * DI + c * 128 + wl]
                           .rearrange("(r p) cc -> p r cc", p=128),
                       allow_slow_non_contiguous=True),
                   reads=bb("W1"), writes=bb("W1_opt"), dma=True)

    xtm, xtm_b = seg("xtm", 0, 1024)
    xnm, xnm_b = seg("xnm", 1024, 1024)
    SQ1, SQ1_b = seg("SQ1", 2048, 1024)
    for tt in range(SCTX // 128):
        op("sync", lambda e, tt=tt: e.dma_start(
            out=xtm, in_=T["x_ctx"][tt * 128:(tt + 1) * 128, :]),
           reads=bb("x_ctx"), writes=[xtm_b], dma=True)
        op("scalar", lambda e: e.activation(SQ1, xtm, AF.Square,
                                            accum_out=ST[:, 1:2]),
           reads=[xtm_b], writes=[SQ1_b, STB])
        op("vector", lambda e: e.reduce_sum(ST[:, 0:1], xtm, axis=AX.X),
           reads=[xtm_b], writes=[STB])
        op("vector", lambda e: e.tensor_scalar(ST[:, 2:3], ST[:, 0:1],
                                               1.0 / D, None, ALU.mult),
           reads=[STB], writes=[STB])
        op("vector", lambda e: e.tensor_mul(ST[:, 3:4], ST[:, 2:3], ST[:, 2:3]),
           reads=[STB], writes=[STB])
        op("vector", lambda e: e.tensor_scalar(ST[:, 4:5], ST[:, 1:2],
                                               1.0 / D, None, ALU.mult),
           reads=[STB], writes=[STB])
        op("vector", lambda e: e.tensor_sub(ST[:, 5:6], ST[:, 4:5], ST[:, 3:4]),
           reads=[STB], writes=[STB])
        op("scalar", lambda e: e.activation(ST[:, 6:7], ST[:, 5:6], AF.Ln,
                                            bias=T["epsc"][:], scale=1.0),
           reads=[STB, B["epsc"]], writes=[STB])
        op("scalar", lambda e: e.activation(ST[:, 7:8], ST[:, 6:7], AF.Exp,
                                            scale=-0.5),
           reads=[STB], writes=[STB])
        op("vector", lambda e: e.tensor_mul(ST[:, 8:9], ST[:, 2:3], ST[:, 7:8]),
           reads=[STB], writes=[STB])
        op("vector", lambda e: e.tensor_scalar(ST[:, 9:10], ST[:, 8:9],
                                               -1.0, None, ALU.mult),
           reads=[STB], writes=[STB])
        op("vector", lambda e: e.tensor_scalar(xnm, xtm, ST[:, 7:8], ST[:, 9:10],
                                               ALU.mult, ALU.add),
           reads=[xtm_b, STB], writes=[xnm_b])
        for r in range(NF):
            bank = r % 2
            op("tensor", lambda e, r=r, bank=bank: e.transpose(
                PB[bank][:, 0:128], xnm[:, r * 128:(r + 1) * 128], T["ident"][:]),
               reads=[xnm_b, B["ident"]], writes=[PBB[bank]])
            op("vector", lambda e, r=r, tt=tt, bank=bank: e.tensor_copy(
                XN[r][:, tt * 128:(tt + 1) * 128], PB[bank][:, 0:128]),
               reads=[PBB[bank]], writes=[XNB[r]])

    if debug >= 1:
        dram_out("xn_dbg", [NF, 128, SCTX])
        for r in range(NF):
            op("gpsimd", lambda e, r=r: e.dma_start(out=T["xn_dbg"][r], in_=XN[r][:]),
               reads=[XNB[r]], writes=bb("xn_dbg"), dma=True)

    kb.barrier()

    if do_outer:
        KP, KP_b = seg("KP", 0, 4096)
        VA, VA_b = seg("VA", 4096, 2080)
        VB_, VB_b = seg("VB", 6176, 2080)
        QP, QP_b = seg("QP", 8256, 2048)
        P0, P0_b = seg("P0", 10304, 512)
        P1, P1_b = seg("P1", 10816, 512)
        KD, KD_b = seg("KD", 11328, 1024)
        QD, QD_b = seg("QD", 12352, 1024)
        VD, VD_b = seg("VD", 13376, 1024)
        ZT, ZT_b = seg("ZT", 14400, 1024)
        P2, P2_b = seg("P2", 15424, 512)
        P3, P3_b = seg("P3", 15936, 512)
        for hp in range(pairs_outer):
            iA = 2 * hp
            for r in range(NF):
                for iL in range(2):
                    cc = (iA + iL) * 8 + r
                    op("vector",
                       lambda e, r=r, iL=iL, cc=cc: e.tensor_scalar(
                           KD[:, r * 128 + iL * 64:r * 128 + iL * 64 + 64],
                           T["diag2"][:, iL * 64:iL * 64 + 64],
                           T["colK"][:, cc:cc + 1], None, ALU.mult),
                       reads=bb("diag2", "colK"), writes=[KD_b])
                    op("vector",
                       lambda e, r=r, iL=iL, cc=cc: e.tensor_scalar(
                           VD[:, r * 128 + iL * 64:r * 128 + iL * 64 + 64],
                           T["diag2"][:, iL * 64:iL * 64 + 64],
                           T["colV"][:, cc:cc + 1], None, ALU.mult),
                       reads=bb("diag2", "colV"), writes=[VD_b])
            for iL in range(2):
                for half in (0, 1):
                    op("sync",
                       lambda e, iL=iL, ih=iA + iL, half=half: e.dma_start(
                           out=T["wpred2"][half * 64:half * 64 + 64,
                                           iL * 64:iL * 64 + 64],
                           in_=T["Wpred"][ih]),
                       reads=bb("Wpred"), writes=bb("wpred2"), dma=True)
            for r in range(NF):
                for iL in range(2):
                    cc = (iA + iL) * 8 + r
                    op("vector",
                       lambda e, r=r, iL=iL, cc=cc: e.tensor_scalar(
                           QD[:, r * 128 + iL * 64:r * 128 + iL * 64 + 64],
                           T["wpred2"][:, iL * 64:iL * 64 + 64],
                           T["colQ"][:, cc:cc + 1], None, ALU.mult),
                       reads=bb("wpred2", "colQ"), writes=[QD_b])
            for mt in range(8):
                bank = 6 + (mt % 2)
                for r in range(NF):
                    op("tensor",
                       lambda e, r=r, mt=mt, bank=bank: e.matmul(
                           PB[bank][:], KD[:, r * 128:(r + 1) * 128],
                           XN[r][:, mt * 512:(mt + 1) * 512],
                           start=(r == 0), stop=(r == NF - 1)),
                       reads=[KD_b, XNB[r]], writes=[PBB[bank]],
                       last=(r == NF - 1))
                op("vector",
                   lambda e, mt=mt, bank=bank: e.tensor_copy(
                       KP[:, mt * 512:(mt + 1) * 512], PB[bank][:]),
                   reads=[PBB[bank]], writes=[KP_b])
            op("vector", lambda e: e.memset(
                VA.rearrange("p (t c) -> p t c", c=65)[:, :, 64:65], 1.0),
               writes=[VA_b])
            op("vector", lambda e: e.memset(
                VB_.rearrange("p (t c) -> p t c", c=65)[:, :, 64:65], 1.0),
               writes=[VB_b])
            for tt in range(32):
                bank = 6 + (tt % 2)
                for r in range(NF):
                    op("tensor",
                       lambda e, r=r, tt=tt, bank=bank: e.matmul(
                           PB[bank][:, 0:128], XN[r][:, tt * 128:(tt + 1) * 128],
                           VD[:, r * 128:(r + 1) * 128],
                           start=(r == 0), stop=(r == NF - 1)),
                       reads=[VD_b, XNB[r]], writes=[PBB[bank]],
                       last=(r == NF - 1))
                op("vector",
                   lambda e, tt=tt, bank=bank: e.tensor_copy(
                       VA[:, tt * 65:tt * 65 + 64], PB[bank][:, 0:64]),
                   reads=[PBB[bank]], writes=[VA_b])
                op("vector",
                   lambda e, tt=tt, bank=bank: e.tensor_copy(
                       VB_[:, tt * 65:tt * 65 + 64], PB[bank][:, 64:128]),
                   reads=[PBB[bank]], writes=[VB_b])
            for mt in range(4):
                bank = 6 + (mt % 2)
                for r in range(NF):
                    op("tensor",
                       lambda e, r=r, mt=mt, bank=bank: e.matmul(
                           PB[bank][:], QD[:, r * 128:(r + 1) * 128],
                           XN[r][:, 2048 + mt * 512:2048 + (mt + 1) * 512],
                           start=(r == 0), stop=(r == NF - 1)),
                       reads=[QD_b, XNB[r]], writes=[PBB[bank]],
                       last=(r == NF - 1))
                op("vector",
                   lambda e, mt=mt, bank=bank: e.tensor_copy(
                       QP[:, mt * 512:(mt + 1) * 512], PB[bank][:]),
                   reads=[PBB[bank]], writes=[QP_b])
            Pts = ((P0, P0_b), (P1, P1_b), (P2, P2_b), (P3, P3_b))
            for j in range(4):
                nd = 20 + 4 * j

                def emit_qk(d, iL, j=j):
                    sbank = (d % 2) * 2 + iL
                    op("tensor",
                       lambda e, d=d, j=j, iL=iL, sbank=sbank: e.matmul(
                           PB[sbank][:],
                           KP[iL * 64:iL * 64 + 64, d * 128:(d + 1) * 128],
                           QP[iL * 64:iL * 64 + 64, j * 512:(j + 1) * 512],
                           start=True, stop=True,
                           tile_position=(iL * 64, 0)),
                       reads=[KP_b, QP_b], writes=[PBB[sbank]])

                def emit_exp(d, iL, j=j, nd=nd):
                    stair = d - (nd - 4)
                    sbank = (d % 2) * 2 + iL
                    Pt, Pt_b = Pts[(d % 2) * 2 + iL]
                    bias = T["pad_bias_t"][:] if d < 16 else T["zeros_col"][:]
                    bias_b = B["pad_bias_t"] if d < 16 else B["zeros_col"]
                    op("scalar",
                       lambda e, Pt=Pt, sbank=sbank, bias=bias: e.activation(
                           Pt, PB[sbank][:], AF.Exp, bias=bias, scale=0.125),
                       reads=[PBB[sbank], bias_b], writes=[Pt_b])
                    if stair >= 0:
                        op("gpsimd",
                           lambda e, Pt=Pt, stair=stair: e.affine_select(
                               Pt, Pt, pattern=[[1, 512]],
                               compare_op=ALU.is_ge, fill=0.0,
                               base=-128 * stair, channel_multiplier=-1),
                           reads=[Pt_b], writes=[Pt_b])

                def emit_pv(d, iL, nd=nd, j=j):
                    zb = 4 + (j % 2) * 2 + iL
                    Pt, Pt_b = Pts[(d % 2) * 2 + iL]
                    VV, VVb = (VA, VA_b) if iL == 0 else (VB_, VB_b)
                    op("tensor",
                       lambda e, d=d, zb=zb, Pt=Pt, VV=VV, nd=nd: e.matmul(
                           PB[zb][0:65, :], VV[:, d * 65:d * 65 + 65], Pt,
                           start=(d == 0), stop=(d == nd - 1)),
                       reads=[VVb, Pt_b], writes=[PBB[zb]])

                for d in range(nd + 1):
                    if d < nd:
                        emit_qk(d, 0)
                        emit_qk(d, 1)
                    if d > 0:
                        emit_pv(d - 1, 0)
                        emit_pv(d - 1, 1)
                    if d < nd:
                        emit_exp(d, 0)
                        emit_exp(d, 1)
                for iL in range(2):
                    zb = 4 + (j % 2) * 2 + iL
                    op("vector",
                       lambda e, zb=zb: e.tensor_copy(
                           T["denr"][64:65, 0:512], PB[zb][64:65, 0:512]),
                       reads=[PBB[zb]], writes=bb("denr"))
                    bcb = 2 + iL
                    op("tensor",
                       lambda e, bcb=bcb: e.matmul(
                           PB[bcb][0:64, :], T["ones65"][64:65, 0:64],
                           T["denr"][64:65, 0:512], start=True, stop=True),
                       reads=bb("denr", "ones65"), writes=[PBB[bcb]])
                    op("vector",
                       lambda e, iL=iL, bcb=bcb: e.reciprocal(
                           ZT[0:64, iL * 512:iL * 512 + 512],
                           PB[bcb][0:64, :]),
                       reads=[PBB[bcb]], writes=[ZT_b])
                    op("vector",
                       lambda e, zb=zb, iL=iL: e.tensor_mul(
                           ZT[64:128, iL * 512:iL * 512 + 512],
                           PB[zb][0:64, :],
                           ZT[0:64, iL * 512:iL * 512 + 512]),
                       reads=[ZT_b, PBB[zb]], writes=[ZT_b])
                    op("gpsimd",
                       lambda e, hp=hp, iL=iL, j=j: e.dma_start(
                           out=T["z_scr"][hp * 128 + iL * 64:
                                          hp * 128 + iL * 64 + 64,
                                          j * 512:(j + 1) * 512],
                           in_=ZT[64:128, iL * 512:iL * 512 + 512]),
                       reads=[ZT_b], writes=bb("z_scr"), dma=True)

        if debug >= 2:
            dram_out("z_dbg", [D, SOWN])
            op("gpsimd", lambda e: e.dma_start(out=T["z_dbg"], in_=T["z_scr"]),
               reads=bb("z_scr"), writes=bb("z_dbg"), dma=True)

    kb.barrier()

    if do_inner:
        tA, tA_b = seg("tA", 0, 1024)
        tBt, tBt_b = seg("tBt", 1024, 1024)
        tAi, tAi_b = seg("tAi", 2048, 1152)
        tBi, tBi_b = seg("tBi", 3200, 1152)
        QF, QF_b = seg("QF", 4352, 2048)
        KF, KF_b = seg("KF", 6400, 2304)
        VAi, VAi_b = seg("VAi", 8704, 1170)
        VBi, VBi_b = seg("VBi", 9874, 1170)
        WQK, WQK_b = seg("WQK", 11044, 3072)
        QKVT, QKVT_b = seg("QKVT", 14116, 384)
        RT1, _RT1_b = seg("RT1", 14500, 256)
        RT2, _RT2_b = seg("RT2", 14756, 256)
        QB_q, QB_k, QB_v = Buf("qkv_q"), Buf("qkv_k"), Buf("qkv_v")
        RT1q, RT1k = Buf("rt1q"), Buf("rt1k")
        RT2q, RT2k = Buf("rt2q"), Buf("rt2k")
        ROT1q = AR[:, 14116:14116 + 128]
        ROT1k = AR[:, 14244:14244 + 128]
        P0o = AR[:, 17068:17068 + 256]
        P1o, P1o_bs = AR[:, 14500:14500 + 256], [RT1q, RT1k]
        P2o, P2o_bs = AR[:, 14756:14756 + 256], [RT2q, RT2k]
        P3o = AR[:, 11044:11044 + 256]
        HF, HF_b = seg("HF", 15012, NF * 257)
        SHQ, SHQ_b = seg("SHQ", 17068, 512)
        P0o_bs = [SHQ_b]
        P3o_bs = [WQK_b]
        WOB, WOB_b = seg("WOB", 0, 8192)
        STG, STG_b = seg("STG", 8192, 4096)
        GG, GG_b = seg("GG", 0, NCC * 512)
        GGB = [Buf("gg%d" % c) for c in range(NCC)]
        W1As = [seg("W1A0", 11264, 1024), seg("W1A1", 12288, 1024)]
        W1Gs = [seg("W1G0", 13312, 1024), seg("W1G1", 14336, 1024)]
        W2Cs = [seg("W2C0", 15360, 512), seg("W2C1", 15872, 512)]
        GEL, GEL_b = seg("GEL", 16384, 512)
        ABC, ABC_b = AR[:, 2048:2560], GGB[4]
        CBC, CBC_b = AR[:, 2560:3072], GGB[5]
        SHF, SHF_b = AR[:, 3072:3584], GGB[6]
        RWB, RWB_b = AR[:, 3584:4096], GGB[7]

        for l in range(depth):
            for tn, dst, dstb, ntl, skip in (
                    ("tabA", tA, tA_b, 16, 2), ("tabB", tBt, tBt_b, 16, 2),
                    ("tabAi", tAi, tAi_b, 18, 0), ("tabBi", tBi, tBi_b, 18, 0)):
                op("sync",
                   lambda e, tn=tn, dst=dst, ntl=ntl, skip=skip: e.dma_start(
                       out=dst.rearrange("p (t c) -> p t c", c=64),
                       in_=T[tn][skip * 128:, :]
                           .rearrange("(t p) c -> p t c", p=128)),
                   reads=bb(tn), writes=[dstb], dma=True)
            for src, c0 in (("qsc", 0), ("qsc", 64), ("ksc", 128), ("ksc", 192)):
                op("sync",
                   lambda e, src=src, c0=c0, l=l: e.dma_start(
                       out=T["qsks"][:, c0:c0 + 64],
                       in_=T[src][l].unsqueeze(0).broadcast_to([128, DH])),
                   reads=bb(src), writes=bb("qsks"), dma=True)
            op("sync", lambda e, l=l: e.dma_start(
                out=T["vecs"][:, 0:21],
                in_=T["b1"][l][0:21 * 128].rearrange("(c p) -> p c", p=128),
                allow_slow_non_contiguous=True),
               reads=bb("b1"), writes=bb("vecs"), dma=True)
            op("sync", lambda e, l=l: e.dma_start(
                out=T["vecs"][0:LASTW, 21:22],
                in_=T["b1"][l][21 * 128:DI].unsqueeze(1),
                allow_slow_non_contiguous=True),
               reads=bb("b1"), writes=bb("vecs"), dma=True)
            op("sync", lambda e, l=l: e.dma_start(
                out=T["vecs"][:, 22:43],
                in_=T["b1"][l][DI:DI + 21 * 128].rearrange("(c p) -> p c", p=128),
                allow_slow_non_contiguous=True),
               reads=bb("b1"), writes=bb("vecs"), dma=True)
            op("sync", lambda e, l=l: e.dma_start(
                out=T["vecs"][0:LASTW, 43:44],
                in_=T["b1"][l][DI + 21 * 128:2 * DI].unsqueeze(1),
                allow_slow_non_contiguous=True),
               reads=bb("b1"), writes=bb("vecs"), dma=True)
            op("sync", lambda e, l=l: e.dma_start(
                out=T["vecs"][:, 44:52],
                in_=T["b2"][l].rearrange("(c p) -> p c", p=128),
                allow_slow_non_contiguous=True),
               reads=bb("b2"), writes=bb("vecs"), dma=True)
            op("sync", lambda e, l=l: e.dma_start(
                out=T["vecs"][:, 52:60],
                in_=T["lng"][l].rearrange("(c p) -> p c", p=128),
                allow_slow_non_contiguous=True),
               reads=bb("lng"), writes=bb("vecs"), dma=True)
            op("sync", lambda e, l=l: e.dma_start(
                out=T["vecs"][:, 60:68],
                in_=T["lnb"][l].rearrange("(c p) -> p c", p=128),
                allow_slow_non_contiguous=True),
               reads=bb("lnb"), writes=bb("vecs"), dma=True)

            for r in range(NF):
                op("gpsimd", lambda e, r=r: e.dma_start(
                    out=T["cc1_in"][r], in_=XA(r)[:, 1791:2048]),
                   reads=[XNB[r]], writes=bb("cc1_in"), dma=True)
            op("gpsimd",
               lambda e: e.collective_compute(
                   "AllGather", ALU.bypass,
                   replica_groups=[[0, 1], [2, 3], [4, 5], [6, 7]],
                   ins=[T["cc1_in"]], outs=[T["cc1_out"]]),
               reads=bb("cc1_in"), writes=bb("cc1_out"))
            for r in range(NF):
                op("scalar", lambda e, r=r: e.dma_start(
                    out=SHQ[:, 0:257], in_=T["cc1_out"][0, r]),
                   reads=bb("cc1_out"), writes=[SHQ_b], dma=True)
                op("vector", lambda e, r=r: e.tensor_scalar(
                    HF[:, r * 257:(r + 1) * 257], SHQ[:, 0:257],
                    T["halo_gate_t"][:], None, ALU.mult),
                   reads=[SHQ_b, B["halo_gate_t"]], writes=[HF_b])


            def shifted_attn(r, tile):
                if tile == 0:
                    o = r * 257 + (1 if r < 4 else 0)
                    return HF[:, o:o + 128], HF_b
                if tile == 1:
                    o = r * 257 + (129 if r < 4 else 128)
                    return HF[:, o:o + 128], HF_b
                t = tile - 2
                if r < 4:
                    return XA(r)[:, t * 128:(t + 1) * 128], XNB[r]
                if t == 0:
                    c0 = (r - 4) * 128
                    return SHQ[:, c0:c0 + 128], SHQ_b
                return XA(r)[:, t * 128 - 1:(t + 1) * 128 - 1], XNB[r]

            for hp in range(8):
                for r in range(4, NF):
                    c0 = (r - 4) * 128
                    op("vector", lambda e, r=r, c0=c0: e.tensor_copy(
                        SHQ[:, c0:c0 + 1], HF[:, r * 257 + 256:r * 257 + 257]),
                       reads=[HF_b], writes=[SHQ_b])
                    op("vector", lambda e, r=r, c0=c0: e.tensor_copy(
                        SHQ[:, c0 + 1:c0 + 128], XA(r)[:, 0:127]),
                       reads=[XNB[r]], writes=[SHQ_b])
                op("sync",
                   lambda e, l=l, hp=hp: e.dma_start(
                       out=WQK,
                       in_=T["Wqkv_opt"][l, hp].rearrange("p r c -> p (r c)")),
                   reads=bb("Wqkv_opt"), writes=[WQK_b], dma=True)
                op("vector", lambda e: e.memset(
                    VAi.rearrange("p (t c) -> p t c", c=65)[:, :, 64:65], 1.0),
                   writes=[VAi_b])
                op("vector", lambda e: e.memset(
                    VBi.rearrange("p (t c) -> p t c", c=65)[:, :, 64:65], 1.0),
                   writes=[VBi_b])
                pend_tr = [None]

                def chain(tt, bank):
                    op("vector",
                       lambda e, bank=bank: e.tensor_copy(QKVT, PB[bank][:, 0:384]),
                       reads=[PBB[bank]], writes=[QB_q, QB_k, QB_v])
                    op("vector",
                       lambda e, tt=tt: e.tensor_copy(
                           VAi[:, tt * 65:tt * 65 + 64], QKVT[:, 256:320]),
                       reads=[QB_v], writes=[VAi_b])
                    op("vector",
                       lambda e, tt=tt: e.tensor_copy(
                           VBi[:, tt * 65:tt * 65 + 64], QKVT[:, 320:384]),
                       reads=[QB_v], writes=[VBi_b])
                    for q4 in range(4):
                        op("scalar",
                           lambda e, q4=q4: e.activation(
                               RT2[:, q4 * 64:q4 * 64 + 64],
                               QKVT[:, q4 * 64:q4 * 64 + 64], AF.Square,
                               accum_out=ST[:, q4:q4 + 1]),
                           reads=[QB_q if q4 < 2 else QB_k],
                           writes=[RT2q if q4 < 2 else RT2k, STB])
                    op("vector",
                       lambda e: e.tensor_scalar(ST[:, 4:8], ST[:, 0:4],
                                                 1e-24, None, ALU.max),
                       reads=[STB], writes=[STB])
                    op("scalar", lambda e: e.activation(ST[:, 8:12], ST[:, 4:8],
                                                        AF.Ln),
                       reads=[STB], writes=[STB])
                    op("scalar", lambda e: e.activation(ST[:, 12:16], ST[:, 8:12],
                                                        AF.Exp, scale=-0.5),
                       reads=[STB], writes=[STB])
                    for q4 in range(4):
                        eng = "vector" if q4 < 2 else "gpsimd"
                        op(eng,
                           lambda e, q4=q4: e.tensor_scalar(
                               RT1[:, q4 * 64:q4 * 64 + 64],
                               QKVT[:, q4 * 64:q4 * 64 + 64],
                               ST[:, 12 + q4:13 + q4], None, ALU.mult),
                           reads=[QB_q if q4 < 2 else QB_k, STB],
                           writes=[RT1q if q4 < 2 else RT1k])
                    op("vector",
                       lambda e: e.tensor_mul(RT2[:, 0:128], RT1[:, 0:128],
                                              T["qsks"][:, 0:128]),
                       reads=[RT1q, B["qsks"]], writes=[RT2q])
                    op("gpsimd",
                       lambda e: e.tensor_mul(RT2[:, 128:256], RT1[:, 128:256],
                                              T["qsks"][:, 128:256]),
                       reads=[RT1k, B["qsks"]], writes=[RT2k])
                    sides = ([("vector", 0, tA, tA_b, tBt, tBt_b, tt - 2,
                               ROT1q, 14116, RT1q, RT2q, QB_q)]
                             if tt >= 2 else [])
                    sides.append(("gpsimd", 128, tAi, tAi_b, tBi, tBi_b, tt,
                                  ROT1k, 14244, RT1k, RT2k, QB_k))
                    for eng, o0, tac, tac_b, tbc, tbc_b, ti, rot, rotoff, \
                            b1, b2, bq in sides:
                        op(eng,
                           lambda e, o0=o0, tac=tac, ti=ti: e.tensor_mul(
                               RT1[:, o0:o0 + 128]
                                   .rearrange("p (h c) -> p h c", h=2),
                               RT2[:, o0:o0 + 128]
                                   .rearrange("p (h c) -> p h c", h=2),
                               tac[:, ti * 64:(ti + 1) * 64]
                                   .unsqueeze(1).broadcast_to([128, 2, 64])),
                           reads=[b2, tac_b], writes=[b1])
                        op(eng,
                           lambda e, tbc=tbc, ti=ti, o0=o0, rot=rot: e.tensor_mul(
                               rot.rearrange("p (h c) -> p h c", h=2),
                               RT2[:, o0:o0 + 128]
                                   .rearrange("p (h c) -> p h c", h=2),
                               tbc[:, ti * 64:(ti + 1) * 64]
                                   .unsqueeze(1).broadcast_to([128, 2, 64])),
                           reads=[b2, tbc_b], writes=[bq])
                        op(eng,
                           lambda e, o0=o0, rotoff=rotoff: e.tensor_add(
                               RT2[:, o0:o0 + 128]
                                   .rearrange("p (h s c) -> p h s c", h=2, s=2),
                               RT1[:, o0:o0 + 128]
                                   .rearrange("p (h s c) -> p h s c", h=2, s=2),
                               _swap_ap(AR, rotoff)),
                           reads=[b1, bq], writes=[b2])

                def make_tr(tt):
                    def emit():
                        if tt >= 2:
                            op("tensor", lambda e: e.transpose(
                                PB[2][:, 0:128], RT2[:, 0:128], T["ident"][:]),
                               reads=[RT2q, B["ident"]], writes=[PBB[2]])
                        op("tensor", lambda e: e.transpose(
                            PB[3][:, 0:128], RT2[:, 128:256], T["ident"][:]),
                           reads=[RT2k, B["ident"]], writes=[PBB[3]])
                        if tt >= 2:
                            op("vector",
                               lambda e, tt=tt: e.tensor_copy(
                                   QF[:, (tt - 2) * 128:(tt - 1) * 128],
                                   PB[2][:, 0:128]),
                               reads=[PBB[2]], writes=[QF_b])
                        op("vector",
                           lambda e, tt=tt: e.tensor_copy(
                               KF[:, tt * 128:(tt + 1) * 128], PB[3][:, 0:128]),
                           reads=[PBB[3]], writes=[KF_b])
                    return emit

                for tt in range(19):
                    if tt < 18:
                        bank = 6 + (tt % 2)
                        for r in range(NF):
                            lhs, lhs_b = shifted_attn(r, tt)
                            op("tensor",
                               lambda e, lhs=lhs, r=r, bank=bank: e.matmul(
                                   PB[bank][:, 0:384], lhs,
                                   WQK[:, r * 384:(r + 1) * 384],
                                   start=(r == 0), stop=(r == NF - 1)),
                               reads=[lhs_b, WQK_b], writes=[PBB[bank]],
                               last=(r == NF - 1))
                    if pend_tr[0] is not None:
                        pend_tr[0]()
                        pend_tr[0] = None
                    if tt < 18:
                        chain(tt, bank)
                        pend_tr[0] = make_tr(tt)
                Ptsi = ((P0o, P0o_bs), (P1o, P1o_bs), (P2o, P2o_bs),
                        (P3o, P3o_bs))
                pend_ev = [None]
                for w in range(8):
                    def emit_qk(st, iL, w=w):
                        kcol = w * 256 + st * 128
                        sbank = (st % 2) * 2 + iL
                        op("tensor",
                           lambda e, iL=iL, kcol=kcol, w=w, sbank=sbank:
                           e.matmul(
                               PB[sbank][:, 0:256],
                               KF[iL * 64:iL * 64 + 64, kcol:kcol + 128],
                               QF[iL * 64:iL * 64 + 64,
                                  w * 256:(w + 1) * 256],
                               start=True, stop=True,
                               tile_position=(iL * 64, 0)),
                           reads=[KF_b, QF_b], writes=[PBB[sbank]])

                    def emit_exp(st, iL, w=w):
                        sbank = (st % 2) * 2 + iL
                        Pt, Pt_bs = Ptsi[(st % 2) * 2 + iL]
                        bias = (T["win0_bias_t"][:] if (w == 0 and st < 2)
                                else T["zeros_col"][:])
                        bias_b = (B["win0_bias_t"] if (w == 0 and st < 2)
                                  else B["zeros_col"])
                        op("scalar",
                           lambda e, Pt=Pt, sbank=sbank, bias=bias:
                           e.activation(Pt, PB[sbank][:, 0:256], AF.Exp,
                                        bias=bias, scale=8.0),
                           reads=[PBB[sbank], bias_b], writes=list(Pt_bs))
                        if st < 2:
                            args = dict(pattern=[[-1, 256]],
                                        base=128 * (st % 2),
                                        channel_multiplier=1)
                        else:
                            args = dict(pattern=[[1, 256]],
                                        base=-128 * (st % 2),
                                        channel_multiplier=-1)
                        op("gpsimd",
                           lambda e, Pt=Pt, args=args:
                           e.affine_select(
                               Pt, Pt, compare_op=ALU.is_ge, fill=0.0,
                               **args),
                           reads=list(Pt_bs), writes=list(Pt_bs))

                    def emit_pv(st, iL, w=w):
                        kt = 2 * w + st
                        zb = 4 + (w % 2) * 2 + iL
                        Pt, Pt_bs = Ptsi[(st % 2) * 2 + iL]
                        VV, VVb = (VAi, VAi_b) if iL == 0 else (VBi, VBi_b)
                        op("tensor",
                           lambda e, kt=kt, zb=zb, Pt=Pt, VV=VV, st=st:
                           e.matmul(
                               PB[zb][0:65, 0:256],
                               VV[:, kt * 65:kt * 65 + 65], Pt,
                               start=(st == 0), stop=(st == 3)),
                           reads=[VVb] + list(Pt_bs), writes=[PBB[zb]])

                    def make_evac(w=w, hp=hp):
                        def emit():
                            for iL in range(2):
                                zb = 4 + (w % 2) * 2 + iL
                                op("vector",
                                   lambda e, zb=zb: e.tensor_copy(
                                       T["denr"][64:65, 0:256],
                                       PB[zb][64:65, 0:256]),
                                   reads=[PBB[zb]], writes=bb("denr"))
                                bcb = 2 + (w % 2)
                                op("tensor",
                                   lambda e, bcb=bcb: e.matmul(
                                       PB[bcb][0:64, 0:256],
                                       T["ones65"][64:65, 0:64],
                                       T["denr"][64:65, 0:256],
                                       start=True, stop=True),
                                   reads=bb("denr", "ones65"),
                                   writes=[PBB[bcb]])
                                op("vector",
                                   lambda e, bcb=bcb: e.reciprocal(
                                       T["denr"][0:64, 0:256],
                                       PB[bcb][0:64, 0:256]),
                                   reads=[PBB[bcb]], writes=bb("denr"))
                                op("vector",
                                   lambda e, hp=hp, iL=iL, w=w, zb=zb:
                                   e.tensor_mul(
                                       XB(hp)[iL * 64:iL * 64 + 64,
                                              w * 256:(w + 1) * 256],
                                       PB[zb][0:64, 0:256],
                                       T["denr"][0:64, 0:256]),
                                   reads=[B["denr"], PBB[zb]],
                                   writes=[XNB[hp]])
                        return emit

                    for st in range(5):
                        if st < 4:
                            emit_qk(st, 0)
                            emit_qk(st, 1)
                        if st == 0 and pend_ev[0] is not None:
                            pend_ev[0]()
                            pend_ev[0] = None
                        if st > 0:
                            emit_pv(st - 1, 0)
                            emit_pv(st - 1, 1)
                        if st < 4:
                            emit_exp(st, 0)
                            emit_exp(st, 1)
                    pend_ev[0] = make_evac()
                if pend_ev[0] is not None:
                    pend_ev[0]()
                    pend_ev[0] = None

            kb.barrier()

            for o in range(NF):
                op("sync",
                   lambda e, l=l, o=o: e.dma_start(
                       out=WOB[:, o * 1024:(o + 1) * 1024],
                       in_=T["Wo_opt"][l, o].rearrange("p r c -> p (r c)")),
                   reads=bb("Wo_opt"), writes=[WOB_b], dma=True)
            for mt in range(4):
                for o in range(NF):
                    bank = (mt % 2) * 4 + o % 4
                    for r in range(NF):
                        op("tensor",
                           lambda e, o=o, r=r, mt=mt, bank=bank: e.matmul(
                               PB[bank][:],
                               WOB[:, o * 1024 + r * 128:o * 1024 + (r + 1) * 128],
                               XB(r)[:, mt * 512:(mt + 1) * 512],
                               start=(r == 0), stop=(r == NF - 1)),
                           reads=[WOB_b, XNB[r]], writes=[PBB[bank]],
                           last=(r == NF - 1))
                    op("vector",
                       lambda e, o=o, mt=mt, bank=bank: e.scalar_tensor_tensor(
                           STG[:, o * 512:(o + 1) * 512], PB[bank][:], 0.0,
                           XA(o)[:, mt * 512:(mt + 1) * 512],
                           ALU.add, ALU.add),
                       reads=[PBB[bank], XNB[o]], writes=[STG_b])
                for o in range(NF):
                    op("vector",
                       lambda e, o=o, mt=mt: e.tensor_copy(
                           XB(o)[:, mt * 512:(mt + 1) * 512],
                           STG[:, o * 512:(o + 1) * 512]),
                       reads=[STG_b], writes=[XNB[o]])

            for r in range(NF):
                op("gpsimd", lambda e, r=r: e.dma_start(
                    out=T["cc2_in"][r], in_=XB(r)[:, 2047:2048]),
                   reads=[XNB[r]], writes=bb("cc2_in"), dma=True)
            op("gpsimd",
               lambda e: e.collective_compute(
                   "AllGather", ALU.bypass,
                   replica_groups=[[0, 1], [2, 3], [4, 5], [6, 7]],
                   ins=[T["cc2_in"]], outs=[T["cc2_out"]]),
               reads=bb("cc2_in"), writes=bb("cc2_out"))
            for r in range(NF):
                op("scalar", lambda e, r=r: e.dma_start(
                    out=T["tmpt"][:, 0:1], in_=T["cc2_out"][0, r]),
                   reads=bb("cc2_out"), writes=bb("tmpt"), dma=True)
                op("vector", lambda e, r=r: e.tensor_scalar(
                    T["xpc"][:, r:r + 1], T["tmpt"][:, 0:1],
                    T["halo_gate_t"][:], None, ALU.mult),
                   reads=[B["tmpt"], B["halo_gate_t"]], writes=bb("xpc"))

            kb.barrier()

            for r in range(4, NF):
                c0 = (r - 4) * 128
                op("vector", lambda e, r=r, c0=c0: e.tensor_copy(
                    SHF[:, c0:c0 + 1], T["xpc"][:, r:r + 1]),
                   reads=bb("xpc"), writes=[SHF_b])
                op("vector", lambda e, r=r, c0=c0: e.tensor_copy(
                    SHF[:, c0 + 1:c0 + 128], XB(r)[:, 0:127]),
                   reads=[XNB[r]], writes=[SHF_b])

            def shifted_ffn(r, mt, lo, hi):
                if r < 4:
                    return XB(r)[:, mt * 512 + lo:mt * 512 + hi], XNB[r]
                base = mt * 512
                if base + lo == 0:
                    c0 = (r - 4) * 128
                    return SHF[:, c0 + lo:c0 + hi], SHF_b
                return XB(r)[:, base + lo - 1:base + hi - 1], XNB[r]

            for mt in range(4):
                for r in range(NF):
                    pieces = ([(0, 128), (128, 512)] if (mt == 0 and r >= 4)
                              else [(0, 512)])
                    for pi, (lo, hi) in enumerate(pieces):
                        first = (r == 0 and pi == 0)
                        last_ = (r == NF - 1 and pi == len(pieces) - 1)
                        src, src_b = shifted_ffn(r, mt, lo, hi)
                        op("tensor",
                           lambda e, src=src, lo=lo, hi=hi, first=first,
                           last_=last_: e.matmul(
                               PB[6][0:1, lo:hi], T["ones_colf"][:], src,
                               start=first, stop=last_),
                           reads=[src_b, B["ones_colf"]], writes=[PBB[6]])
                        sqo = (r % 2) * 512
                        op("scalar",
                           lambda e, src=src, sqo=sqo, lo=lo, hi=hi: e.activation(
                               GG[:, sqo + lo:sqo + hi], src, AF.Square),
                           reads=[src_b], writes=[GGB[sqo // 512]])
                        op("tensor",
                           lambda e, sqo=sqo, lo=lo, hi=hi, first=first,
                           last_=last_: e.matmul(
                               PB[7][0:1, lo:hi], T["ones_colf"][:],
                               GG[:, sqo + lo:sqo + hi],
                               start=first, stop=last_),
                           reads=[GGB[sqo // 512], B["ones_colf"]],
                           writes=[PBB[7]])
                op("vector", lambda e: e.tensor_scalar(
                    T["rowsS"][0:1, :], PB[6][0:1, :], 1.0 / D, None, ALU.mult),
                   reads=[PBB[6]], writes=bb("rowsS"))
                op("vector", lambda e: e.tensor_scalar(
                    RWB[0:1, 0:512], PB[7][0:1, :], 1.0 / D, None, ALU.mult),
                   reads=[PBB[7]], writes=[RWB_b])
                op("vector", lambda e: e.tensor_mul(
                    PB[4][0:1, :], T["rowsS"][0:1, :], T["rowsS"][0:1, :]),
                   reads=bb("rowsS"), writes=[PBB[4]])
                op("vector", lambda e: e.tensor_sub(
                    PB[5][0:1, :], RWB[0:1, 0:512], PB[4][0:1, :]),
                   reads=[RWB_b, PBB[4]], writes=[PBB[5]])
                op("scalar", lambda e: e.activation(
                    PB[4][0:1, :], PB[5][0:1, :], AF.Ln,
                    bias=T["epsc"][0:1, :], scale=1.0),
                   reads=[PBB[5], B["epsc"]], writes=[PBB[4]])
                op("scalar", lambda e: e.activation(
                    RWB[0:1, 0:512], PB[4][0:1, :], AF.Exp, scale=-0.5),
                   reads=[PBB[4]], writes=[RWB_b])
                op("vector", lambda e: e.tensor_mul(
                    PB[5][0:1, :], T["rowsS"][0:1, :], RWB[0:1, 0:512]),
                   reads=[B["rowsS"], RWB_b], writes=[PBB[5]])
                op("vector", lambda e: e.tensor_copy(
                    T["rowsS"][0:1, :], PB[5][0:1, :]),
                   reads=[PBB[5]], writes=bb("rowsS"))
                op("tensor", lambda e: e.matmul(
                    PB[6][:, 0:512], T["ones_row"][:], RWB[0:1, 0:512],
                    start=True, stop=True),
                   reads=[RWB_b, B["ones_row"]], writes=[PBB[6]])
                op("tensor", lambda e: e.matmul(
                    PB[7][:, 0:512], T["ones_row"][:], T["rowsS"][0:1, :],
                    start=True, stop=True),
                   reads=[B["rowsS"], B["ones_row"]], writes=[PBB[7]])
                op("vector", lambda e: e.tensor_copy(ABC, PB[6][:]),
                   reads=[PBB[6]], writes=[ABC_b])
                op("vector", lambda e: e.tensor_copy(CBC, PB[7][:]),
                   reads=[PBB[7]], writes=[CBC_b])
                for r in range(NF):
                    pieces = ([(0, 128), (128, 512)] if (mt == 0 and r >= 4)
                              else [(0, 512)])
                    for lo, hi in pieces:
                        src, src_b = shifted_ffn(r, mt, lo, hi)
                        op("vector",
                           lambda e, src=src, lo=lo, hi=hi: e.tensor_mul(
                               GG[:, 1024 + lo:1024 + hi], src, ABC[:, lo:hi]),
                           reads=[src_b, ABC_b], writes=[GGB[2]])
                    op("vector", lambda e: e.tensor_sub(
                        GG[:, 1536:2048], GG[:, 1024:1536], CBC),
                       reads=[GGB[2], CBC_b], writes=[GGB[3]])
                    op("vector",
                       lambda e, r=r, mt=mt: e.tensor_scalar(
                           XA(r)[:, mt * 512:(mt + 1) * 512],
                           GG[:, 1536:2048],
                           T["vecs"][:, 52 + r:53 + r], T["vecs"][:, 60 + r:61 + r],
                           ALU.mult, ALU.add),
                       reads=[GGB[3], B["vecs"]], writes=[XNB[r]])

            for mt in range(4):
                def emit_w2a(c):
                    wlast = 128 if c < NCC - 1 else LASTW
                    W2C, W2C_b = W2Cs[c % 2]
                    for o in range(4):
                        op("tensor",
                           lambda e, c=c, o=o, wlast=wlast, W2C=W2C: e.matmul(
                               PB[o][:],
                               W2C[0:wlast, o * 128:(o + 1) * 128],
                               GG[0:wlast, c * 512:(c + 1) * 512],
                               start=(c == 0), stop=(c == NCC - 1)),
                           reads=[W2C_b, GGB[c]], writes=[PBB[o]])

                for c in range(NCC + 1):
                    if c < NCC:
                        wlast = 128 if c < NCC - 1 else LASTW
                        W1A, W1A_b = W1As[c % 2]
                        W1G, W1G_b = W1Gs[c % 2]
                        W2C, W2C_b = W2Cs[c % 2]
                        op("sync", lambda e, l=l, c=c, wlast=wlast, W1A=W1A:
                           e.dma_start(
                               out=W1A[:, 0:8 * wlast]
                                   .rearrange("p (r c) -> p r c", c=wlast),
                               in_=T["W1_opt"][l, 0, c][:, :, 0:wlast]),
                           reads=bb("W1_opt"), writes=[W1A_b], dma=True)
                        op("sync", lambda e, l=l, c=c, wlast=wlast, W1G=W1G:
                           e.dma_start(
                               out=W1G[:, 0:8 * wlast]
                                   .rearrange("p (r c) -> p r c", c=wlast),
                               in_=T["W1_opt"][l, 1, c][:, :, 0:wlast]),
                           reads=bb("W1_opt"), writes=[W1G_b], dma=True)
                        op("sync", lambda e, l=l, c=c, wlast=wlast, W2C=W2C:
                           e.dma_start(
                               out=W2C[0:wlast, :],
                               in_=T["W2"][l][c * 128:c * 128 + wlast, 0:512]),
                           reads=bb("W2"), writes=[W2C_b], dma=True)
                        for r in range(NF):
                            op("tensor",
                               lambda e, r=r, mt=mt, wlast=wlast, W1A=W1A:
                               e.matmul(
                                   PB[6][0:wlast, :],
                                   W1A[:, r * wlast:(r + 1) * wlast],
                                   XA(r)[:, mt * 512:(mt + 1) * 512],
                                   start=(r == 0), stop=(r == NF - 1)),
                               reads=[W1A_b, XNB[r]], writes=[PBB[6]],
                               last=(r == NF - 1))
                        for r in range(NF):
                            op("tensor",
                               lambda e, r=r, mt=mt, wlast=wlast, W1G=W1G:
                               e.matmul(
                                   PB[7][0:wlast, :],
                                   W1G[:, r * wlast:(r + 1) * wlast],
                                   XA(r)[:, mt * 512:(mt + 1) * 512],
                                   start=(r == 0), stop=(r == NF - 1)),
                               reads=[W1G_b, XNB[r]], writes=[PBB[7]],
                               last=(r == NF - 1))
                    if c > 0:
                        emit_w2a(c - 1)
                    if c < NCC:
                        op("scalar",
                           lambda e, c=c, wlast=wlast: e.activation(
                               GEL[0:wlast, :], PB[7][0:wlast, :], AF.Gelu,
                               bias=T["vecs"][0:wlast, 22 + c:23 + c]),
                           reads=[PBB[7], B["vecs"]], writes=[GEL_b])
                        op("vector",
                           lambda e, c=c, wlast=wlast: e.scalar_tensor_tensor(
                               GG[0:wlast, c * 512:(c + 1) * 512],
                               PB[6][0:wlast, :],
                               T["vecs"][0:wlast, c:c + 1], GEL[0:wlast, :],
                               ALU.add, ALU.mult),
                           reads=[PBB[6], B["vecs"], GEL_b], writes=[GGB[c]])
                for o in range(4):
                    op("vector",
                       lambda e, o=o, mt=mt: e.scalar_tensor_tensor(
                           XA(o)[:, mt * 512:(mt + 1) * 512], PB[o][:],
                           T["vecs"][:, 44 + o:45 + o],
                           XB(o)[:, mt * 512:(mt + 1) * 512],
                           ALU.add, ALU.add),
                       reads=[PBB[o], B["vecs"], XNB[o]], writes=[XNB[o]])
                for c in range(NCC):
                    wlast = 128 if c < NCC - 1 else LASTW
                    W2C, W2C_b = W2Cs[c % 2]
                    op("sync", lambda e, l=l, c=c, wlast=wlast, W2C=W2C:
                       e.dma_start(
                           out=W2C[0:wlast, 0:512],
                           in_=T["W2"][l][c * 128:c * 128 + wlast, 512:1024]),
                       reads=bb("W2"), writes=[W2C_b], dma=True)
                    for o in range(4):
                        op("tensor",
                           lambda e, c=c, o=o, wlast=wlast, W2C=W2C: e.matmul(
                               PB[4 + o][:],
                               W2C[0:wlast, o * 128:(o + 1) * 128],
                               GG[0:wlast, c * 512:(c + 1) * 512],
                               start=(c == 0), stop=(c == NCC - 1)),
                           reads=[W2C_b, GGB[c]], writes=[PBB[4 + o]])
                for o in range(4):
                    op("vector",
                       lambda e, o=o, mt=mt: e.scalar_tensor_tensor(
                           XA(4 + o)[:, mt * 512:(mt + 1) * 512], PB[4 + o][:],
                           T["vecs"][:, 48 + o:49 + o],
                           XB(4 + o)[:, mt * 512:(mt + 1) * 512],
                           ALU.add, ALU.add),
                       reads=[PBB[4 + o], B["vecs"], XNB[4 + o]],
                       writes=[XNB[4 + o]])

            kb.barrier()

            if debug >= 3:
                dram_out("xl%d_dbg" % l, [NF, 128, SOWN])
                for r in range(NF):
                    op("gpsimd", lambda e, r=r, l=l: e.dma_start(
                        out=T["xl%d_dbg" % l][r], in_=XA(r)),
                       reads=[XNB[r]], writes=bb("xl%d_dbg" % l), dma=True)
                kb.barrier()

    if do_final:
        OTM, OTM_b = seg("OTM", 0, 1024)
        XOW, XOW_b = seg("XOW", 1024, 1024)
        for r in range(NF):
            op("sync", lambda e, r=r: e.dma_start(
                out=XB(r), in_=T["z_scr"][r * 128:(r + 1) * 128, :]),
               reads=bb("z_scr"), writes=[XNB[r]], dma=True)
        for tt in range(16):
            op("sync", lambda e, tt=tt: e.dma_start(
                out=XOW, in_=T["x_ctx"][2048 + tt * 128:2048 + (tt + 1) * 128, :]),
               reads=bb("x_ctx"), writes=[XOW_b], dma=True)
            for r in range(NF):
                tb = T["tmpt"] if r % 2 == 0 else T["tmpt2"]
                tbb = B["tmpt"] if r % 2 == 0 else B["tmpt2"]
                bank = 2 + (r % 2)
                op("vector", lambda e, r=r, tt=tt, tb=tb: e.tensor_add(
                    tb[:], XA(r)[:, tt * 128:(tt + 1) * 128],
                    XB(r)[:, tt * 128:(tt + 1) * 128]),
                   reads=[XNB[r]], writes=[tbb])
                op("tensor", lambda e, tb=tb, bank=bank: e.transpose(
                    PB[bank][:, 0:128], tb[:], T["ident"][:]),
                   reads=[tbb, B["ident"]], writes=[PBB[bank]])
                op("vector", lambda e, r=r, bank=bank: e.scalar_tensor_tensor(
                    OTM[:, r * 128:(r + 1) * 128], PB[bank][:, 0:128], 0.0,
                    XOW[:, r * 128:(r + 1) * 128], ALU.add, ALU.add),
                   reads=[PBB[bank], XOW_b], writes=[OTM_b])
            op("sync", lambda e, tt=tt: e.dma_start(
                out=T["out_own"][tt * 128:(tt + 1) * 128, :], in_=OTM),
               reads=[OTM_b], writes=bb("out_own"), dma=True)
    kb.barrier()

    block = stack.enter_context(nc.Block())
    kb.finalize(block)
    return nc, stack, T


def _swap_ap(arena, col0):
    ap = arena[:, col0:col0 + 128]
    return bass.AP(ap.tensor, ap.offset + 32,
                   [list(ap.ap[0]), [64, 2], [-32, 2], [1, 32]])


def make_tables(h):
    half = np.arange(0, DH, 2, dtype=np.float64)
    inv_freq = 1.0 / 10000.0 ** (half / DH)
    base_scale = (half + 0.4 * DH) / (1.4 * DH)
    start = h * 2048 - 256
    t = np.arange(start, start + 2304, dtype=np.float64)
    tc = np.clip(t, 0, None)
    freqs = tc[:, None] * inv_freq[None, :]
    freqs = np.concatenate([freqs, freqs], -1)
    power = (tc - SCTX // 2) / (WIN // 2)
    scale = base_scale[None, :] ** power[:, None]
    scale = np.concatenate([scale, scale], -1)
    cos, sin = np.cos(freqs), np.sin(freqs)
    A = (cos * scale).astype(np.float32)
    S = (sin * scale).astype(np.float32)
    tabB = np.concatenate([S[:, :32], -S[:, 32:]], -1)
    Ai = (cos / scale).astype(np.float32)
    Si = (sin / scale).astype(np.float32)
    tabBi = np.concatenate([Si[:, :32], -Si[:, 32:]], -1)
    return A, tabB, Ai, tabBi


_CACHED = {}


def _get_nc():
    if "nc" not in _CACHED:
        nc, stack, Tm = build_nc()
        _CACHED["nc"] = nc
    return _CACHED["nc"]


def kernel(L0_attn_x, L0_attn_W_K, L0_attn_W_Q, L0_attn_W_V, L0_attn_W_pred,
           L0_ffn_Wqkv, L0_ffn_q_scale, L0_ffn_k_scale, L0_ffn_Wo,
           L0_ffn_ln_g, L0_ffn_ln_b, L0_ffn_W1, L0_ffn_b1, L0_ffn_W2,
           L0_ffn_b2):
    x = np.ascontiguousarray(np.asarray(L0_attn_x, dtype=np.float32))
    Bsz = x.shape[0]
    com = dict(
        Wqkv=np.ascontiguousarray(L0_ffn_Wqkv, dtype=np.float32),
        Wo=np.ascontiguousarray(L0_ffn_Wo, dtype=np.float32),
        W1=np.ascontiguousarray(L0_ffn_W1, dtype=np.float32),
        W2=np.ascontiguousarray(L0_ffn_W2, dtype=np.float32),
        b1=np.ascontiguousarray(L0_ffn_b1, dtype=np.float32),
        b2=np.ascontiguousarray(L0_ffn_b2, dtype=np.float32),
        lng=np.ascontiguousarray(L0_ffn_ln_g, dtype=np.float32),
        lnb=np.ascontiguousarray(L0_ffn_ln_b, dtype=np.float32),
        qsc=np.ascontiguousarray(L0_ffn_q_scale, dtype=np.float32),
        ksc=np.ascontiguousarray(L0_ffn_k_scale, dtype=np.float32),
        WK=np.ascontiguousarray(L0_attn_W_K, dtype=np.float32),
        WQ=np.ascontiguousarray(L0_attn_W_Q, dtype=np.float32),
        WV=np.ascontiguousarray(L0_attn_W_V, dtype=np.float32),
        Wpred=np.ascontiguousarray(L0_attn_W_pred, dtype=np.float32),
    )
    tabs = {h: make_tables(h) for h in (0, 1)}
    in_maps = []
    for c in range(8):
        b, h = c // 2, c % 2
        if h == 1:
            x_ctx = x[b]
        else:
            x_ctx = np.concatenate(
                [np.zeros((2048, D), dtype=np.float32), x[b, :2048]], axis=0)
        A, tb, Ai, tbi = tabs[h]
        m = dict(com)
        m.update(
            x_ctx=np.ascontiguousarray(x_ctx),
            tabA=A, tabB=tb, tabAi=Ai, tabBi=tbi,
            pad_bias=np.full((128, 1), NEG if h == 0 else 0.0, dtype=np.float32),
            win0_bias=np.full((128, 1), NEG if h == 0 else 0.0, dtype=np.float32),
            halo_gate=np.full((128, 1), 0.0 if h == 0 else 1.0, dtype=np.float32),
        )
        in_maps.append(m)
    nc = _get_nc()
    res = run_bass_kernel_spmd(nc, in_maps, core_ids=list(range(8)))
    out = np.empty((Bsz, SCTX, D), dtype=np.float32)
    for c in range(8):
        b, h = c // 2, c % 2
        out[b, h * 2048:(h + 1) * 2048] = res.results[c]["out_own"]
    return out
